# version 04
# baseline (speedup 1.0000x reference)
import contextlib
import numpy as np
import concourse.bass as bass
import concourse.mybir as mybir
from concourse.bass_utils import run_bass_kernel_spmd

F32 = mybir.dt.float32
BF16 = mybir.dt.bfloat16
AF = mybir.ActivationFunctionType
ALU = mybir.AluOpType
AXX = mybir.AxisListType.X

TOK = 1024
NCB = 16
L = 128
NCH = 8
DEBUG = False
PHASES = 6
_SKIP_ODD_PT = False
A4SUB = 7


def _hadamard(n):
    h = np.array([[1.0]], dtype=np.float64)
    while h.shape[0] < n:
        h = np.block([[h, h], [h, -h]])
    return h


class Counter:
    def __init__(self):
        self.v = {}

    def inc(self, name, by):
        self.v[name] = self.v.get(name, 0) + by
        return self.v[name]

    def get(self, name):
        return self.v.get(name, 0)


def build_nc():
    nc = bass.Bass("TRN2", target_bir_lowering=False)
    C = Counter()

    d_xT = nc.dram_tensor("xT", [2048, TOK], F32, kind="ExternalInput").ap()
    d_xp0 = nc.dram_tensor("xp0", [1, 2048], F32, kind="ExternalInput").ap()
    d_xp0c = nc.dram_tensor("xp0c", [128, 16], F32, kind="ExternalInput").ap()
    d_umask = nc.dram_tensor("i_umask", [128, 16], F32, kind="ExternalInput").ap()
    vec_names = ["maa_x", "maa_r", "maa_k", "maa_v", "maa_w", "tdv",
                 "qb", "kb", "vb", "lng", "lnb"]
    d_vec = {n: nc.dram_tensor("c_" + n, [128, 16], F32, kind="ExternalInput").ap()
             for n in vec_names}
    d_su = nc.dram_tensor("c_su", [128, 16], F32, kind="ExternalInput").ap()
    d_sudp = nc.dram_tensor("c_sudp", [128, 64], F32, kind="ExternalInput").ap()
    d_wscu = nc.dram_tensor("c_wscu", [128, 64], F32, kind="ExternalInput").ap()
    d_wscg = nc.dram_tensor("c_wscg", [128, 64], F32, kind="ExternalInput").ap()
    d_wsdp = nc.dram_tensor("c_wsdp", [128, 16], F32, kind="ExternalInput").ap()
    d_H128 = nc.dram_tensor("c_H128", [128, 128], F32, kind="ExternalInput").ap()
    d_ident = nc.dram_tensor("c_ident", [128, 128], F32, kind="ExternalInput").ap()
    d_mask4 = nc.dram_tensor("c_mask4", [128, 512], F32, kind="ExternalInput").ap()
    d_ones_col = nc.dram_tensor("c_onescol", [128, 1], F32, kind="ExternalInput").ap()
    d_ones_row = nc.dram_tensor("c_onesrow", [1, 128], F32, kind="ExternalInput").ap()
    d_maaw1 = nc.dram_tensor("w_maaw1", [16, 128, 128], F32, kind="ExternalInput").ap()
    d_w2blk = nc.dram_tensor("w_w2blk", [4, 16, 128, 128], F32, kind="ExternalInput").ap()
    d_tdw1 = nc.dram_tensor("w_tdw1", [16, 128, 64], F32, kind="ExternalInput").ap()
    d_tdw2 = nc.dram_tensor("w_tdw2", [16, 64, 128], F32, kind="ExternalInput").ap()
    d_qw = nc.dram_tensor("w_qw", [16, 16, 128, 128], F32, kind="ExternalInput").ap()
    d_kw = nc.dram_tensor("w_kw", [16, 16, 128, 128], F32, kind="ExternalInput").ap()
    d_vw = nc.dram_tensor("w_vw", [16, 16, 128, 128], F32, kind="ExternalInput").ap()
    d_ow = nc.dram_tensor("w_ow", [16, 16, 128, 128], F32, kind="ExternalInput").ap()
    d_wup = nc.dram_tensor("w_wup", [64, 16, 128, 128], F32, kind="ExternalInput").ap()
    d_wg = nc.dram_tensor("w_wg", [64, 16, 128, 128], F32, kind="ExternalInput").ap()
    d_wd = nc.dram_tensor("w_wd", [16, 64, 128, 128], F32, kind="ExternalInput").ap()

    d_out = nc.dram_tensor("o_out", [2048, TOK], F32, kind="ExternalOutput").ap()
    scr = {}
    for n in ["x1", "wexp", "q", "km", "v", "s"]:
        scr[n] = nc.dram_tensor("scr_" + n, [128, NCB * TOK], BF16).ap()
    scr["qt"] = nc.dram_tensor("scr_qt", [128, NCH * 2048], BF16).ap()
    scr["oi"] = nc.dram_tensor("scr_oi", [128, NCH * 2048], BF16).ap()
    d_agin = nc.dram_tensor("ag_in", [2048, 65], F32)
    d_agout = nc.dram_tensor("ag_out", [8 * 2048, 65], F32)

    dbg = {}
    if DEBUG:
        for n, w in [("xn", NCB * TOK), ("dx", NCB * TOK), ("x1", NCB * TOK),
                     ("wexp", NCB * TOK), ("q", NCB * TOK), ("km", NCB * TOK),
                     ("v", NCB * TOK), ("sspan", NCB * 64), ("aspan", 16),
                     ("sin", NCB * 64), ("yn", NCB * TOK), ("s", NCB * TOK),
                     ("z", 64 * TOK), ("mm", TOK)]:
            dbg[n] = nc.dram_tensor("dbg_" + n, [128, w], BF16, kind="ExternalOutput").ap()

    est = contextlib.ExitStack()
    sem = {n: est.enter_context(nc.semaphore(n))
           for n in ["LD", "WL", "WA", "WB", "MM", "EV", "SC", "SP", "CC"]}

    sb = {}

    def alloc(stk, name, shape, dtype):
        t = stk.enter_context(nc.sbuf_tensor(name, shape, dtype))
        sb[name] = t
        return t

    for n in vec_names:
        alloc(est, n, [128, 16], F32)
    for n, sh, dt in [("umask", [128, 16], F32), ("su", [128, 16], F32),
                      ("sudp", [128, 64], F32), ("s_up", [128, 64], F32),
                      ("s_g", [128, 64], F32), ("s_dp", [128, 16], F32),
                      ("H128b", [128, 128], BF16), ("identb", [128, 128], BF16),
                      ("mask4", [128, 512], F32), ("onesb", [128, 1], BF16),
                      ("onesrow", [1, 128], F32),
                      ("xp0col", [128, 16], F32), ("xp0sq", [128, 16], F32),
                      ("pst", [1, 8], F32), ("p128", [128, 2], F32),
                      ("onesf", [128, 1], F32),
                      ("cbias", [128, 4], F32)]:
        alloc(est, n, sh, dt)
    if DEBUG:
        alloc(est, "dbuf", [128, TOK], BF16)
    LS = contextlib.ExitStack()
    for n, sh, dt in [("sfold", [128, 1024], F32),
                      ("S_run", [128, NCB * 64], F32), ("A_run", [128, 16], F32),
                      ("Apref", [128, 128], F32), ("Lsnap", [128, NCH * NCB * 64], BF16),
                      ("ALtmp", [128, 16], F32), ("aggst", [128, 16 * 65], F32)]:
        alloc(LS, n, sh, dt)

    ps = [est.enter_context(nc.psum_tensor("ps%d" % i, [128, 512], F32)) for i in range(7)]
    psb = est.enter_context(nc.psum_tensor("psb", [128, 1024], BF16))

    def blk_c(t, ci):
        return t[:, ci * TOK:(ci + 1) * TOK]

    def strip_load(g, dst, src_oc, ntile=16, s="WL"):
        for fc in range(ntile):
            g.dma_start(out=dst[:, fc * 128:(fc + 1) * 128], in_=src_oc[fc]).then_inc(sem[s], 16)
        return 16 * ntile

    def dump_cb(name, src, nblk):
        if not DEBUG:
            return
        with nc.Block() as blk:
            @blk.scalar
            def _(s):
                base_sp = C.get("SP")
                for ci in range(nblk):
                    if ci:
                        s.wait_ge(sem["SP"], base_sp + 16 * ci)
                    s.activation(sb["dbuf"][:, :], src[:, ci * TOK:(ci + 1) * TOK], AF.Copy)
                    s.drain()
                    s.sem_inc(sem["SC"], 1)
                C.inc("SC", nblk)

            @blk.sync
            def _(sync):
                base_sc = C.get("SC") - nblk
                for ci in range(nblk):
                    sync.wait_ge(sem["SC"], base_sc + ci + 1)
                    sync.dma_start(out=dbg[name][:, ci * TOK:(ci + 1) * TOK],
                                   in_=sb["dbuf"][:, :]).then_inc(sem["SP"], 16)
                sync.wait_ge(sem["SP"], C.inc("SP", 16 * nblk))

    def dump_small(name, src, width):
        if not DEBUG:
            return
        with nc.Block() as blk:
            @blk.vector
            def _(v):
                v.tensor_copy(sb["dbuf"][:, 0:width], src)
                v.drain()
                v.sem_inc(sem["EV"], 1)
                C.inc("EV", 1)

            @blk.sync
            def _(sync):
                sync.wait_ge(sem["EV"], C.get("EV"))
                sync.dma_start(out=dbg[name][:, 0:width], in_=sb["dbuf"][:, 0:width]).then_inc(sem["SP"], 16)
                sync.wait_ge(sem["SP"], C.inc("SP", 16))

    with nc.Block() as blk:
        @blk.sync
        def _(sync):
            n_loads = 0
            for n in vec_names:
                sync.dma_start(out=sb[n][:, :], in_=d_vec[n]).then_inc(sem["LD"], 16)
                n_loads += 1
            for n, d in [("umask", d_umask), ("su", d_su), ("sudp", d_sudp),
                         ("s_up", d_wscu), ("s_g", d_wscg), ("s_dp", d_wsdp),
                         ("mask4", d_mask4), ("onesrow", d_ones_row),
                         ("onesf", d_ones_col), ("xp0col", d_xp0c)]:
                sync.dma_start(out=sb[n][:, :], in_=d).then_inc(sem["LD"], 16)
                n_loads += 1
            C.inc("LD", 16 * n_loads)

        @blk.gpsimd
        def _(g):
            g.dma_start(out=sb["H128b"][:, :], in_=d_H128).then_inc(sem["WL"], 16)
            g.dma_start(out=sb["identb"][:, :], in_=d_ident).then_inc(sem["WL"], 16)
            g.dma_start(out=sb["onesb"][:, :], in_=d_ones_col).then_inc(sem["WL"], 16)
            C.inc("WL", 48)

        @blk.vector
        def _(v):
            v.wait_ge(sem["LD"], C.get("LD"))
            v.wait_ge(sem["WL"], C.get("WL"))
            v.tensor_scalar_mul(sb["s_up"][:, :], sb["s_up"][:, :], float(1.0 / np.sqrt(2048.0)))
            v.tensor_scalar_mul(sb["s_g"][:, :], sb["s_g"][:, :], float(1.0 / np.sqrt(2048.0)))
            v.tensor_scalar_mul(sb["s_dp"][:, :], sb["s_dp"][:, :], float(1.0 / np.sqrt(8192.0)))
            v.memset(sb["S_run"][:, :], 0.0)
            v.memset(sb["A_run"][:, :], 1.0)
            v.memset(sb["cbias"][:, 0:1], 1e-5)
            v.memset(sb["cbias"][:, 1:2], 64e-5)
            v.memset(sb["cbias"][:, 2:3], 1.0)
            v.drain()
            v.sem_inc(sem["EV"], 1)
            C.inc("EV", 1)


    def _early_out(close_a1b=False):
        with nc.Block() as blk:
            @blk.sync
            def _(sync):
                sync.dma_start(out=d_out[0:128, :], in_=sb["sfold"][:, :]).then_inc(sem["SP"], 16)
                sync.wait_ge(sem["SP"], C.inc("SP", 16))
        if close_a1b:
            A1b.close()
        LS.close()
        est.close()
        return nc

    A1b = contextlib.ExitStack()
    A1a = contextlib.ExitStack()
    xn = alloc(A1b, "xn", [128, NCB * TOK], BF16)
    dx = alloc(A1b, "dx", [128, NCB * TOK], BF16)
    alloc(A1b, "muB", [128, TOK], F32)
    alloc(A1b, "rB", [128, TOK], F32)
    xbuf = alloc(A1a, "xbuf", [128, NCB * TOK], F32)
    alloc(A1a, "statrow", [1, TOK], F32)
    alloc(A1a, "statrow2", [1, TOK], F32)
    alloc(A1a, "T1", [128, TOK], F32)
    alloc(A1a, "T2", [128, TOK], F32)
    xbb = xn

    with nc.Block() as blk:
        @blk.sync
        def _(sync):
            for ci in range(16):
                sync.dma_start(out=blk_c(xbuf, ci), in_=d_xT[ci * 128:(ci + 1) * 128, :]).then_inc(sem["LD"], 16)
            C.inc("LD", 16 * 16)

        @blk.scalar
        def _(s):
            s.wait_ge(sem["LD"], C.get("LD"))
            for ci in range(16):
                s.activation(blk_c(xbb, ci), blk_c(xbuf, ci), AF.Copy)
            s.activation(sb["xp0sq"][:, :], sb["xp0col"][:, :], AF.Square)
            s.drain()
            s.sem_inc(sem["SC"], 1)
            C.inc("SC", 1)

        @blk.tensor
        def _(t):
            t.wait_ge(sem["SC"], C.get("SC"))
            for hf in range(2):
                for ci in range(16):
                    t.matmul(ps[0 + hf][0:1, :], sb["onesb"][:, :],
                             blk_c(xbb, ci)[:, hf * 512:(hf + 1) * 512],
                             start=(ci == 0), stop=(ci == 15))
            t.matmul(ps[4][0:1, 0:16], sb["onesf"][:, :], sb["xp0col"][:, :], start=True, stop=True)
            t.matmul(ps[4][0:1, 16:32], sb["onesf"][:, :], sb["xp0sq"][:, :], start=True, stop=True)
            t.drain()
            t.sem_inc(sem["MM"], 1)
            C.inc("MM", 1)

        @blk.scalar
        def _(s):
            s.wait_ge(sem["MM"], C.get("MM"))
            for ci in range(16):
                s.activation(blk_c(xbb, ci), blk_c(xbuf, ci), AF.Square)
            s.drain()
            s.sem_inc(sem["SC"], 1)
            C.inc("SC", 1)

        @blk.tensor
        def _(t):
            t.wait_ge(sem["SC"], C.get("SC"))
            for hf in range(2):
                for ci in range(16):
                    t.matmul(ps[2 + hf][0:1, :], sb["onesb"][:, :],
                             blk_c(xbb, ci)[:, hf * 512:(hf + 1) * 512],
                             start=(ci == 0), stop=(ci == 15))
            t.drain()
            t.sem_inc(sem["MM"], 1)
            C.inc("MM", 1)

        @blk.vector
        def _(v):
            v.wait_ge(sem["MM"], C.get("MM"))
            for hf in range(2):
                sl = slice(hf * 512, (hf + 1) * 512)
                v.tensor_scalar_mul(sb["statrow"][0:1, sl], ps[0 + hf][0:1, :], float(1.0 / 2048.0))
                v.tensor_scalar_mul(sb["statrow2"][0:1, sl], ps[2 + hf][0:1, :], float(1.0 / 2048.0))
            v.reduce_sum(sb["pst"][0:1, 0:1], ps[4][0:1, 0:16], axis=AXX)
            v.reduce_sum(sb["pst"][0:1, 1:2], ps[4][0:1, 16:32], axis=AXX)
            v.drain()
            v.tensor_tensor(sb["T1"][0:1, 0:TOK], sb["statrow"][0:1, :], sb["statrow"][0:1, :], ALU.mult)
            v.tensor_scalar_mul(sb["pst"][0:1, 0:2], sb["pst"][0:1, 0:2], float(1.0 / 2048.0))
            v.drain()
            v.tensor_tensor(sb["statrow2"][0:1, :], sb["statrow2"][0:1, :], sb["T1"][0:1, 0:TOK], ALU.subtract)
            v.tensor_tensor(sb["pst"][0:1, 2:3], sb["pst"][0:1, 0:1], sb["pst"][0:1, 0:1], ALU.mult)
            v.drain()
            v.tensor_tensor(sb["pst"][0:1, 1:2], sb["pst"][0:1, 1:2], sb["pst"][0:1, 2:3], ALU.subtract)
            v.drain()
            v.sem_inc(sem["EV"], 1)
            C.inc("EV", 1)

        @blk.scalar
        def _(s):
            s.wait_ge(sem["EV"], C.get("EV"))
            s.activation(sb["statrow2"][0:1, :], sb["statrow2"][0:1, :], AF.Sqrt, bias=sb["cbias"][0:1, 0:1])
            s.activation(sb["pst"][0:1, 1:2], sb["pst"][0:1, 1:2], AF.Sqrt, bias=sb["cbias"][0:1, 0:1])
            s.drain()
            s.sem_inc(sem["SC"], 1)
            C.inc("SC", 1)

        @blk.vector
        def _(v):
            v.wait_ge(sem["SC"], C.get("SC"))
            v.reciprocal(sb["statrow2"][0:1, :], sb["statrow2"][0:1, :])
            v.reciprocal(sb["pst"][0:1, 1:2], sb["pst"][0:1, 1:2])
            v.drain()
            v.sem_inc(sem["EV"], 1)
            C.inc("EV", 1)

        @blk.tensor
        def _(t):
            t.wait_ge(sem["EV"], C.get("EV"))
            for hf in range(2):
                sl = slice(hf * 512, (hf + 1) * 512)
                t.matmul(ps[0 + hf][:, :], sb["onesrow"][0:1, :], sb["statrow"][0:1, sl], start=True, stop=True)
                t.matmul(ps[2 + hf][:, :], sb["onesrow"][0:1, :], sb["statrow2"][0:1, sl], start=True, stop=True)
            t.matmul(ps[4][:, 0:1], sb["onesrow"][0:1, :], sb["pst"][0:1, 0:1], start=True, stop=True)
            t.matmul(ps[4][:, 1:2], sb["onesrow"][0:1, :], sb["pst"][0:1, 1:2], start=True, stop=True)
            t.drain()
            t.sem_inc(sem["MM"], 1)
            C.inc("MM", 1)

        @blk.vector
        def _(v):
            v.wait_ge(sem["MM"], C.get("MM"))
            for hf in range(2):
                sl = slice(hf * 512, (hf + 1) * 512)
                v.tensor_copy(sb["muB"][:, sl], ps[0 + hf][:, :])
                v.tensor_copy(sb["rB"][:, sl], ps[2 + hf][:, :])
            v.tensor_copy(sb["p128"][:, 0:2], ps[4][:, 0:2])
            v.drain()
            v.tensor_scalar(sb["xp0col"][:, :], sb["xp0col"][:, :],
                            sb["p128"][:, 0:1], sb["p128"][:, 1:2], ALU.subtract, ALU.mult)
            for cp in range(8):
                a, b = 2 * cp, 2 * cp + 1
                v.tensor_tensor(sb["T1"][:, :], blk_c(xbuf, a), sb["muB"][:, :], ALU.subtract)
                v.tensor_tensor(sb["T2"][:, :], blk_c(xbuf, b), sb["muB"][:, :], ALU.subtract)
                v.drain()
                v.tensor_tensor(blk_c(xn, a), sb["T1"][:, :], sb["rB"][:, :], ALU.mult)
                v.tensor_tensor(blk_c(xn, b), sb["T2"][:, :], sb["rB"][:, :], ALU.mult)
                v.drain()
            for ci in range(16):
                v.tensor_tensor(blk_c(dx, ci)[:, 1:TOK], blk_c(xn, ci)[:, 0:TOK - 1],
                                blk_c(xn, ci)[:, 1:TOK], ALU.subtract)
                v.tensor_tensor(blk_c(dx, ci)[:, 0:1], sb["xp0col"][:, ci:ci + 1],
                                blk_c(xn, ci)[:, 0:1], ALU.subtract)
            v.drain()
            v.sem_inc(sem["EV"], 1)
            C.inc("EV", 1)
    A1a.close()
    dump_cb("xn", xn, 16)
    dump_cb("dx", dx, 16)

    A2 = contextlib.ExitStack()
    F0 = alloc(A2, "F0", [128, 16 * 512], F32)
    F1 = alloc(A2, "F1", [128, 16 * 512], F32)
    x1t = alloc(A2, "x1t", [128, NCB * TOK], BF16)
    alloc(A2, "H128su", [128, 16 * 128], BF16)

    with nc.Block() as blk:
        @blk.vector
        def _(v):
            for ci in range(16):
                v.tensor_scalar_mul(sb["H128su"][:, ci * 128:(ci + 1) * 128],
                                    sb["H128b"][:, :], sb["su"][:, ci:ci + 1])
            v.drain()
            v.sem_inc(sem["EV"], 1)
            C.inc("EV", 1)

    for hf in range(2):
        with nc.Block() as blk:
            @blk.tensor
            def _(t):
                evb = C.get("EV")
                for ci in range(16):
                    if ci >= 4:
                        t.wait_ge(sem["EV"], evb + ci - 3)
                    t.matmul(ps[ci % 4][:, :], sb["H128su"][:, ci * 128:(ci + 1) * 128],
                             blk_c(xn, ci)[:, hf * 512:(hf + 1) * 512], start=True, stop=True)
                    t.drain()
                    t.sem_inc(sem["MM"], 1)
                C.inc("MM", 16)

            @blk.vector
            def _(v):
                mmb = C.get("MM") - 16
                for ci in range(16):
                    v.wait_ge(sem["MM"], mmb + ci + 1)
                    v.tensor_copy(F0[:, ci * 512:(ci + 1) * 512], ps[ci % 4][:, :])
                    v.drain()
                    v.sem_inc(sem["EV"], 1)
                C.inc("EV", 16)
                v.drain()
                bufs = [F0, F1]
                for si, d in enumerate([1, 2, 4]):
                    src, dst = bufs[si % 2], bufs[(si + 1) % 2]
                    for i in range(16):
                        if i & d:
                            continue
                        j = i + d
                        a = src[:, i * 512:(i + 1) * 512]
                        b = src[:, j * 512:(j + 1) * 512]
                        v.tensor_tensor(dst[:, i * 512:(i + 1) * 512], a, b, ALU.add)
                        v.tensor_tensor(dst[:, j * 512:(j + 1) * 512], a, b, ALU.subtract)
                    v.drain()
                src = bufs[1]
                for i in range(16):
                    if i & 8:
                        continue
                    j = i + 8
                    a = src[:, i * 512:(i + 1) * 512]
                    b = src[:, j * 512:(j + 1) * 512]
                    v.tensor_tensor(x1t[:, i * TOK + hf * 512: i * TOK + hf * 512 + 512], a, b, ALU.add)
                    v.tensor_tensor(x1t[:, j * TOK + hf * 512: j * TOK + hf * 512 + 512], a, b, ALU.subtract)
                v.drain()
                v.sem_inc(sem["EV"], 1)
                C.inc("EV", 1)

    with nc.Block() as blk:
        @blk.sync
        def _(sync):
            sync.dma_start(out=scr["x1"], in_=x1t[:, :]).then_inc(sem["SP"], 16)
            sync.wait_ge(sem["SP"], C.inc("SP", 16))
    dump_cb("x1", x1t, 16)
    A2.close()

    if PHASES <= 1:
        return _early_out(close_a1b=True)

    A3 = contextlib.ExitStack()
    alloc(A3, "mmt", [128, TOK], BF16)
    alloc(A3, "tht", [128, TOK], BF16)
    xxx = alloc(A3, "xxx", [128, TOK], BF16)
    mx = alloc(A3, "mx", [128, NCB * TOK], BF16)
    pout = alloc(A3, "pout", [128, NCB * TOK], BF16)
    wstr = [alloc(A3, "wstr%d" % i, [128, 16 * 128], BF16) for i in range(2)]
    Ta = alloc(A3, "Ta", [128, 512], F32)
    Tb = alloc(A3, "Tb", [128, 512], F32)

    with nc.Block() as blk:
        @blk.gpsimd
        def _(g):
            C.inc("WL", strip_load(g, wstr[0], d_maaw1))

        @blk.vector
        def _(v):
            mmb = C.get("MM")
            for ci in range(16):
                if ci >= 1:
                    v.wait_ge(sem["MM"], mmb + ci)
                v.scalar_tensor_tensor(xxx[:, :], blk_c(dx, ci), sb["maa_x"][:, ci:ci + 1],
                                       blk_c(xn, ci), ALU.mult, ALU.add)
                v.drain()
                v.sem_inc(sem["EV"], 1)
            C.inc("EV", 16)

        @blk.tensor
        def _(t):
            t.wait_ge(sem["WL"], C.get("WL"))
            evb = C.get("EV") - 16
            for ci in range(16):
                t.wait_ge(sem["EV"], evb + ci + 1)
                for hf in range(2):
                    t.matmul(ps[4 + hf][:, :], wstr[0][:, ci * 128:(ci + 1) * 128],
                             xxx[:, hf * 512:(hf + 1) * 512], start=(ci == 0), stop=(ci == 15))
                t.drain()
                t.sem_inc(sem["MM"], 1)
            C.inc("MM", 16)

        @blk.scalar
        def _(s):
            s.wait_ge(sem["MM"], C.get("MM"))
            for hf in range(2):
                s.activation(sb["mmt"][:, hf * 512:(hf + 1) * 512], ps[4 + hf][:, :], AF.Tanh)
            s.drain()
            s.sem_inc(sem["SC"], 1)
            C.inc("SC", 1)
    dump_cb("mm", sb["mmt"], 1)

    def mix_block(tgt_idx, vecname):
        with nc.Block() as blk:
            @blk.gpsimd
            def _(g):
                C.inc("WL", strip_load(g, wstr[0], d_w2blk[tgt_idx]))

            @blk.tensor
            def _(t):
                t.wait_ge(sem["WL"], C.get("WL"))
                t.wait_ge(sem["SC"], C.get("SC"))
                evb = C.get("EV")
                for oc in range(16):
                    pp = ps[(oc % 2) * 2: (oc % 2) * 2 + 2]
                    if oc >= 2:
                        t.wait_ge(sem["EV"], evb + oc - 1)
                    for hf in range(2):
                        t.matmul(pp[hf][:, :], wstr[0][:, oc * 128:(oc + 1) * 128],
                                 sb["mmt"][:, hf * 512:(hf + 1) * 512], start=True, stop=True)
                    t.drain()
                    t.sem_inc(sem["MM"], 1)
                C.inc("MM", 16)

            @blk.vector
            def _(v):
                mmb = C.get("MM") - 16
                for oc in range(16):
                    pp = ps[(oc % 2) * 2: (oc % 2) * 2 + 2]
                    v.wait_ge(sem["MM"], mmb + oc + 1)
                    for hf in range(2):
                        v.scalar_tensor_tensor((Ta if hf == 0 else Tb)[:, :],
                                               pp[hf][:, :], sb[vecname][:, oc:oc + 1],
                                               blk_c(dx, oc)[:, hf * 512:(hf + 1) * 512],
                                               ALU.add, ALU.mult)
                    v.drain()
                    for hf in range(2):
                        v.tensor_tensor(blk_c(mx, oc)[:, hf * 512:(hf + 1) * 512],
                                        (Ta if hf == 0 else Tb)[:, :],
                                        blk_c(xn, oc)[:, hf * 512:(hf + 1) * 512], ALU.add)
                    v.drain()
                    v.sem_inc(sem["EV"], 1)
                C.inc("EV", 16)

    def proj_block(wdram, bias_col):
        with nc.Block() as blk:
            @blk.gpsimd
            def _(g):
                mmb = C.get("MM")
                for oc in range(16):
                    if oc >= 2:
                        g.wait_ge(sem["MM"], mmb + oc - 1)
                    strip_load(g, wstr[oc % 2], wdram[oc], s="WA" if oc % 2 == 0 else "WB")
                C.inc("WA", 256 * 8)
                C.inc("WB", 256 * 8)

            @blk.tensor
            def _(t):
                wlba = C.get("WA") - 256 * 8
                wlbb = C.get("WB") - 256 * 8
                scb = C.get("SC")
                for oc in range(16):
                    t.wait_ge(sem["WA" if oc % 2 == 0 else "WB"],
                              (wlba if oc % 2 == 0 else wlbb) + 256 * (oc // 2 + 1))
                    pp = ps[(oc % 2) * 2: (oc % 2) * 2 + 2]
                    if oc >= 2:
                        t.wait_ge(sem["SC"], scb + oc - 1)
                    for ci in range(16):
                        for hf in range(2):
                            t.matmul(pp[hf][:, :], wstr[oc % 2][:, ci * 128:(ci + 1) * 128],
                                     blk_c(mx, ci)[:, hf * 512:(hf + 1) * 512],
                                     start=(ci == 0), stop=(ci == 15))
                    t.drain()
                    t.sem_inc(sem["MM"], 1)
                C.inc("MM", 16)

            @blk.scalar
            def _(s):
                mmb = C.get("MM") - 16
                for oc in range(16):
                    pp = ps[(oc % 2) * 2: (oc % 2) * 2 + 2]
                    s.wait_ge(sem["MM"], mmb + oc + 1)
                    for hf in range(2):
                        s.activation(blk_c(pout, oc)[:, hf * 512:(hf + 1) * 512], pp[hf][:, :],
                                     AF.Identity, bias=sb[bias_col][:, oc:oc + 1])
                    s.drain()
                    s.sem_inc(sem["SC"], 1)
                C.inc("SC", 16)

    def spill_block(dst):
        with nc.Block() as blk:
            @blk.sync
            def _(sync):
                sync.dma_start(out=dst, in_=pout[:, :]).then_inc(sem["SP"], 16)
                sync.wait_ge(sem["SP"], C.inc("SP", 16))

    mix_block(3, "maa_w")
    with nc.Block() as blk:
        @blk.gpsimd
        def _(g):
            for fc in range(16):
                g.dma_start(out=wstr[0][:, fc * 128:fc * 128 + 64], in_=d_tdw1[fc]).then_inc(sem["WL"], 16)
            for fc in range(16):
                g.dma_start(out=wstr[1][0:64, fc * 128:(fc + 1) * 128], in_=d_tdw2[fc]).then_inc(sem["WL"], 16)
            C.inc("WL", 16 * 32)

        @blk.tensor
        def _(t):
            t.wait_ge(sem["WL"], C.get("WL"))
            for ci in range(16):
                for hf in range(2):
                    t.matmul(ps[4 + hf][0:64, :], wstr[0][:, ci * 128:ci * 128 + 64],
                             blk_c(mx, ci)[:, hf * 512:(hf + 1) * 512],
                             start=(ci == 0), stop=(ci == 15))
            t.drain()
            t.sem_inc(sem["MM"], 1)
            C.inc("MM", 1)

        @blk.scalar
        def _(s):
            s.wait_ge(sem["MM"], C.get("MM"))
            for hf in range(2):
                s.activation(sb["tht"][0:64, hf * 512:(hf + 1) * 512], ps[4 + hf][0:64, :], AF.Tanh)
            s.drain()
            s.sem_inc(sem["SC"], 1)
            C.inc("SC", 1)

        @blk.tensor
        def _(t):
            t.wait_ge(sem["SC"], C.get("SC"))
            scb = C.get("SC")
            for oc in range(16):
                pp = ps[(oc % 2) * 2: (oc % 2) * 2 + 2]
                if oc >= 2:
                    t.wait_ge(sem["SC"], scb + oc - 1)
                for hf in range(2):
                    t.matmul(pp[hf][:, :], wstr[1][0:64, oc * 128:(oc + 1) * 128],
                             sb["tht"][0:64, hf * 512:(hf + 1) * 512], start=True, stop=True)
                t.drain()
                t.sem_inc(sem["MM"], 1)
            C.inc("MM", 16)

        @blk.scalar
        def _(s):
            mmb = C.get("MM") - 16
            evb = C.get("EV")
            for oc in range(16):
                pp = ps[(oc % 2) * 2: (oc % 2) * 2 + 2]
                s.wait_ge(sem["MM"], mmb + oc + 1)
                if oc >= 1:
                    s.wait_ge(sem["EV"], evb + oc)
                for hf in range(2):
                    s.activation((Ta if hf == 0 else Tb)[:, :], pp[hf][:, :],
                                 AF.Exp, bias=sb["tdv"][:, oc:oc + 1])
                s.drain()
                s.sem_inc(sem["SC"], 1)
            C.inc("SC", 16)

        @blk.vector
        def _(v):
            scb = C.get("SC") - 16
            for oc in range(16):
                v.wait_ge(sem["SC"], scb + oc + 1)
                for hf in range(2):
                    v.tensor_scalar_min(blk_c(pout, oc)[:, hf * 512:(hf + 1) * 512],
                                        (Ta if hf == 0 else Tb)[:, :], 5.0)
                v.drain()
                v.sem_inc(sem["EV"], 1)
            C.inc("EV", 16)
    spill_block(scr["wexp"])
    dump_cb("wexp", pout, 16)

    mix_block(2, "maa_v")
    proj_block(d_vw, "vb")
    spill_block(scr["v"])
    dump_cb("v", pout, 16)

    mix_block(1, "maa_k")
    with nc.Block() as blk:
        @blk.sync
        def _(sync):
            scb = C.get("SC")
            for oc in range(16):
                if oc >= 1:
                    sync.wait_ge(sem["SC"], scb + oc)
                sync.dma_start(out=xxx[:, :], in_=scr["wexp"][:, oc * TOK:(oc + 1) * TOK]).then_inc(sem["LD"], 16)
            C.inc("LD", 16 * 16)

        @blk.gpsimd
        def _(g):
            mmb = C.get("MM")
            for oc in range(16):
                if oc >= 2:
                    g.wait_ge(sem["MM"], mmb + oc - 1)
                strip_load(g, wstr[oc % 2], d_kw[oc], s="WA" if oc % 2 == 0 else "WB")
            C.inc("WA", 256 * 8)
            C.inc("WB", 256 * 8)

        @blk.tensor
        def _(t):
            wlba = C.get("WA") - 256 * 8
            wlbb = C.get("WB") - 256 * 8
            scb = C.get("SC")
            for oc in range(16):
                t.wait_ge(sem["WA" if oc % 2 == 0 else "WB"],
                          (wlba if oc % 2 == 0 else wlbb) + 256 * (oc // 2 + 1))
                pp = ps[(oc % 2) * 2: (oc % 2) * 2 + 2]
                if oc >= 2:
                    t.wait_ge(sem["SC"], scb + oc - 1)
                for ci in range(16):
                    for hf in range(2):
                        t.matmul(pp[hf][:, :], wstr[oc % 2][:, ci * 128:(ci + 1) * 128],
                                 blk_c(mx, ci)[:, hf * 512:(hf + 1) * 512],
                                 start=(ci == 0), stop=(ci == 15))
                t.drain()
                t.sem_inc(sem["MM"], 1)
            C.inc("MM", 16)

        @blk.scalar
        def _(s):
            mmb = C.get("MM") - 16
            ldb = C.get("LD") - 256
            evb = C.get("EV")
            for oc in range(16):
                pp = ps[(oc % 2) * 2: (oc % 2) * 2 + 2]
                s.wait_ge(sem["MM"], mmb + oc + 1)
                s.wait_ge(sem["LD"], ldb + 16 * (oc + 1))
                if oc >= 1:
                    s.wait_ge(sem["EV"], evb + oc)
                s.activation(sb["muB"][:, 0:TOK], xxx[:, :], AF.Exp, scale=-1.0)
                for hf in range(2):
                    s.activation((Ta if hf == 0 else Tb)[:, :], pp[hf][:, :],
                                 AF.Identity, bias=sb["kb"][:, oc:oc + 1])
                s.drain()
                s.activation(sb["rB"][:, 0:TOK], sb["muB"][:, 0:TOK], AF.Identity, scale=-1.0, bias=sb["cbias"][:, 2:3])
                s.drain()
                s.sem_inc(sem["SC"], 1)
            C.inc("SC", 16)

        @blk.vector
        def _(v):
            scb = C.get("SC") - 16
            for oc in range(16):
                v.wait_ge(sem["SC"], scb + oc + 1)
                for hf in range(2):
                    v.tensor_tensor(blk_c(pout, oc)[:, hf * 512:(hf + 1) * 512],
                                    (Ta if hf == 0 else Tb)[:, :],
                                    sb["rB"][:, hf * 512:(hf + 1) * 512], ALU.mult)
                v.drain()
                v.sem_inc(sem["EV"], 1)
            C.inc("EV", 16)
    spill_block(scr["km"])
    dump_cb("km", pout, 16)

    mix_block(0, "maa_r")
    proj_block(d_qw, "qb")
    spill_block(scr["q"])
    dump_cb("q", pout, 16)
    A3.close()
    A1b.close()

    if PHASES <= 2:
        return _early_out()

    A4 = contextlib.ExitStack()
    qc = alloc(A4, "qc", [128, 2048], BF16)
    kmc = alloc(A4, "kmc", [128, 2048], BF16)
    vc = alloc(A4, "vc", [128, 2048], BF16)
    wxc = alloc(A4, "wxc", [128, 2048], BF16)
    csc = alloc(A4, "csc", [128, 2048], F32)
    gqc = alloc(A4, "gqc", [128, 2048], F32)
    gqic = alloc(A4, "gqic", [128, 2048], F32)
    qtc = alloc(A4, "qtc", [128, 2048], BF16)
    ktc = alloc(A4, "ktc", [128, 2048], BF16)
    k2c = alloc(A4, "k2c", [128, 2048], BF16)
    k2tm = alloc(A4, "k2tm", [128, 2048], BF16)
    vtm = alloc(A4, "vtm", [128, 2048], BF16)
    pmt = alloc(A4, "pmt", [128, 32 * 128], BF16)
    oic = alloc(A4, "oic", [128, 2048], BF16)
    qts = alloc(A4, "qts", [64, 2048], BF16)
    kts = alloc(A4, "kts", [64, 2048], BF16)

    for j in range(NCH):
        with nc.Block() as blk:
            @blk.sync
            def _(sync):
                for ci in range(16):
                    co = ci * TOK + j * 128
                    for nm, dst in [("q", qc), ("km", kmc), ("v", vc), ("wexp", wxc)]:
                        sync.dma_start(out=dst[:, ci * 128:(ci + 1) * 128],
                                       in_=scr[nm][:, co:co + 128]).then_inc(sem["LD"], 16)
                C.inc("LD", 16 * 64)

            @blk.vector
            def _(v):
                v.wait_ge(sem["LD"], C.get("LD"))
                for ci in range(16):
                    v.tensor_tensor_scan(csc[:, ci * 128:(ci + 1) * 128],
                                         wxc[:, ci * 128:(ci + 1) * 128],
                                         wxc[:, ci * 128:(ci + 1) * 128],
                                         0.0, ALU.add, ALU.bypass)
                v.drain()
                v.sem_inc(sem["EV"], 1)
                C.inc("EV", 1)

            if A4SUB <= 1:
                continue

            @blk.scalar
            def _(s):
                s.wait_ge(sem["EV"], C.get("EV"))
                s.activation(gqc[:, :], csc[:, :], AF.Exp, scale=-1.0)
                s.activation(gqic[:, :], csc[:, :], AF.Exp)
                s.drain()
                s.sem_inc(sem["SC"], 1)
                C.inc("SC", 1)

            if A4SUB <= 2:
                continue

            @blk.vector
            def _(v):
                v.wait_ge(sem["SC"], C.get("SC"))
                v.tensor_tensor(qtc[:, :], qc[:, :], gqc[:, :], ALU.mult)
                v.tensor_tensor(ktc[:, :], kmc[:, :], gqic[:, :], ALU.mult)
                v.drain()
                for ci in range(16):
                    v.tensor_scalar_mul(k2c[:, ci * 128:(ci + 1) * 128],
                                        ktc[:, ci * 128:(ci + 1) * 128],
                                        gqc[:, ci * 128 + 127:ci * 128 + 128])
                v.drain()
                v.sem_inc(sem["EV"], 1)
                C.inc("EV", 1)

            if A4SUB <= 3:
                continue

            @blk.sync
            def _(sync):
                sync.wait_ge(sem["EV"], C.get("EV"))
                sync.dma_start(out=qts[:, :], in_=qtc[64:128, :]).then_inc(sem["LD"], 16)
                sync.dma_start(out=kts[:, :], in_=ktc[64:128, :]).then_inc(sem["LD"], 16)
                C.inc("LD", 32)

            @blk.tensor
            def _(t):
                t.wait_ge(sem["EV"], C.get("EV"))
                evb = C.get("EV")
                for ci in range(16):
                    g, sl = ci // 4, ci % 4
                    if g >= 1 and sl == 0:
                        t.wait_ge(sem["EV"], evb + g)
                    t.transpose(psb[:, sl * 128:(sl + 1) * 128], vc[:, ci * 128:(ci + 1) * 128], sb["identb"][:, :])
                    t.transpose(psb[:, 512 + sl * 128: 512 + (sl + 1) * 128], k2c[:, ci * 128:(ci + 1) * 128], sb["identb"][:, :])
                    if sl == 3:
                        t.drain()
                        t.sem_inc(sem["MM"], 1)
                C.inc("MM", 4)

            @blk.vector
            def _(v):
                mmb = C.get("MM") - 4
                for g in range(4):
                    v.wait_ge(sem["MM"], mmb + g + 1)
                    v.tensor_copy(vtm[:, g * 512:(g + 1) * 512], psb[:, 0:512])
                    v.tensor_copy(k2tm[:, g * 512:(g + 1) * 512], psb[:, 512:1024])
                    v.drain()
                    v.sem_inc(sem["EV"], 1)
                C.inc("EV", 4)

            if A4SUB <= 4:
                continue

            for wave in range(2):
                @blk.tensor
                def _(t):
                    t.wait_ge(sem["EV"], C.get("EV"))
                    t.wait_ge(sem["LD"], C.get("LD"))
                    for hh in range(16):
                        h = wave * 16 + hh
                        ci, off = h // 2, (h % 2) * 64
                        kt_src = ktc[0:64, ci * 128:(ci + 1) * 128] if off == 0 else kts[:, ci * 128:(ci + 1) * 128]
                        qt_src = qtc[0:64, ci * 128:(ci + 1) * 128] if off == 0 else qts[:, ci * 128:(ci + 1) * 128]
                        t.matmul(ps[hh // 4][:, (hh % 4) * 128:(hh % 4 + 1) * 128],
                                 kt_src, qt_src, start=True, stop=True)
                    t.drain()
                    t.sem_inc(sem["MM"], 1)
                    C.inc("MM", 1)

                @blk.vector
                def _(v):
                    v.wait_ge(sem["MM"], C.get("MM"))
                    for bank in range(4):
                        v.tensor_tensor(pmt[:, wave * 2048 + bank * 512: wave * 2048 + (bank + 1) * 512],
                                        ps[bank][:, :], sb["mask4"][:, :], ALU.mult)
                    v.drain()
                    v.sem_inc(sem["EV"], 1)
                    C.inc("EV", 1)

            if A4SUB <= 5:
                continue

            @blk.tensor
            def _(t):
                t.wait_ge(sem["EV"], C.get("EV"))
                for h in range(32):
                    ci, off = h // 2, (h % 2) * 64
                    t.matmul(ps[ci % 4][off:off + 64, (ci // 4) * 128:(ci // 4) * 128 + 128],
                             vtm[:, ci * 128 + off:ci * 128 + off + 64],
                             pmt[:, h * 128:(h + 1) * 128],
                             start=True, stop=True, tile_position=(0, off))
                for h in range(32):
                    ci, off = h // 2, (h % 2) * 64
                    t.matmul(ps[4 + ci % 2][off:off + 64, (ci // 2) * 64:(ci // 2) * 64 + 64],
                             k2tm[:, ci * 128 + off:ci * 128 + off + 64],
                             vtm[:, ci * 128 + off:ci * 128 + off + 64],
                             start=True, stop=True, tile_position=(0, off))
                t.drain()
                t.sem_inc(sem["MM"], 1)
                C.inc("MM", 1)

            if A4SUB <= 6:
                continue

            @blk.vector
            def _(v):
                v.wait_ge(sem["MM"], C.get("MM"))
                v.tensor_copy(sb["Lsnap"][:, j * 1024:(j + 1) * 1024], sb["S_run"][:, :])
                for ci in range(16):
                    v.tensor_copy(sb["Apref"][:, j * 16 + ci:j * 16 + ci + 1],
                                  sb["A_run"][:, ci:ci + 1])
                for ci in range(16):
                    v.tensor_copy(oic[:, ci * 128:(ci + 1) * 128],
                                  ps[ci % 4][:, (ci // 4) * 128:(ci // 4) * 128 + 128])
                v.drain()
                for ci in range(16):
                    v.scalar_tensor_tensor(sb["S_run"][:, ci * 64:(ci + 1) * 64],
                                           sb["S_run"][:, ci * 64:(ci + 1) * 64],
                                           gqc[:, ci * 128 + 127:ci * 128 + 128],
                                           ps[4 + ci % 2][:, (ci // 2) * 64:(ci // 2) * 64 + 64],
                                           ALU.mult, ALU.add)
                    v.tensor_scalar_mul(sb["A_run"][:, ci:ci + 1], sb["A_run"][:, ci:ci + 1],
                                        gqc[:, ci * 128 + 127:ci * 128 + 128])
                v.drain()
                v.sem_inc(sem["EV"], 1)
                C.inc("EV", 1)

            @blk.sync
            def _(sync):
                sync.wait_ge(sem["EV"], C.get("EV"))
                sync.dma_start(out=scr["oi"][:, j * 2048:(j + 1) * 2048], in_=oic[:, :]).then_inc(sem["SP"], 16)
                sync.dma_start(out=scr["qt"][:, j * 2048:(j + 1) * 2048], in_=qtc[:, :]).then_inc(sem["SP"], 16)
                sync.wait_ge(sem["SP"], C.inc("SP", 32))
    A4.close()
    dump_small("sspan", sb["S_run"][:, :], NCB * 64)
    dump_small("aspan", sb["A_run"][:, :], 16)

    if PHASES <= 3:
        return _early_out()

    with nc.Block() as blk:
        @blk.vector
        def _(v):
            for ci in range(16):
                v.tensor_copy(sb["aggst"][:, ci * 65:ci * 65 + 64], sb["S_run"][:, ci * 64:(ci + 1) * 64])
                v.tensor_copy(sb["aggst"][:, ci * 65 + 64:ci * 65 + 65], sb["A_run"][:, ci:ci + 1])
            v.drain()
            v.sem_inc(sem["EV"], 1)
            C.inc("EV", 1)

        @blk.sync
        def _(sync):
            sync.wait_ge(sem["EV"], C.get("EV"))
            for ci in range(16):
                sync.dma_start(out=d_agin.ap()[ci * 128:(ci + 1) * 128, :],
                               in_=sb["aggst"][:, ci * 65:(ci + 1) * 65]).then_inc(sem["SP"], 16)
            C.inc("SP", 16 * 16)

        @blk.gpsimd
        def _(g):
            g.wait_ge(sem["SP"], C.get("SP"))
            g.collective_compute("AllGather", ALU.bypass,
                                 replica_groups=[list(range(8))],
                                 ins=[d_agin.ap()], outs=[d_agout.ap()]).then_inc(sem["CC"], 1)
            C.inc("CC", 1)

        @blk.vector
        def _(v):
            v.memset(sb["S_run"][:, :], 0.0)
            v.drain()
            v.sem_inc(sem["EV"], 1)
            C.inc("EV", 1)

    for jj in range(7):
        with nc.Block() as blk:
            @blk.sync
            def _(sync):
                sync.wait_ge(sem["CC"], C.get("CC"))
                for ci in range(16):
                    sync.dma_start(out=sb["aggst"][:, ci * 65:(ci + 1) * 65],
                                   in_=d_agout.ap()[jj * 2048 + ci * 128: jj * 2048 + (ci + 1) * 128, :]).then_inc(sem["LD"], 16)
                C.inc("LD", 16 * 16)

            @blk.vector
            def _(v):
                v.wait_ge(sem["LD"], C.get("LD"))
                for ci in range(16):
                    v.tensor_scalar(sb["ALtmp"][:, ci:ci + 1],
                                    sb["aggst"][:, ci * 65 + 64:ci * 65 + 65],
                                    sb["umask"][:, jj:jj + 1],
                                    sb["umask"][:, 8 + jj:9 + jj], ALU.mult, ALU.add)
                    v.tensor_scalar_mul(sb["sfold"][:, ci * 64:(ci + 1) * 64],
                                        sb["aggst"][:, ci * 65:ci * 65 + 64],
                                        sb["umask"][:, jj:jj + 1])
                v.drain()
                for ci in range(16):
                    v.scalar_tensor_tensor(sb["S_run"][:, ci * 64:(ci + 1) * 64],
                                           sb["S_run"][:, ci * 64:(ci + 1) * 64],
                                           sb["ALtmp"][:, ci:ci + 1],
                                           sb["sfold"][:, ci * 64:(ci + 1) * 64],
                                           ALU.mult, ALU.add)
                v.drain()
                v.sem_inc(sem["EV"], 1)
                C.inc("EV", 1)
    dump_small("sin", sb["S_run"][:, :], NCB * 64)

    with nc.Block() as blk:
        @blk.vector
        def _(v):
            for j in range(NCH):
                for ci in range(16):
                    v.scalar_tensor_tensor(sb["Lsnap"][:, j * 1024 + ci * 64:j * 1024 + (ci + 1) * 64],
                                           sb["S_run"][:, ci * 64:(ci + 1) * 64],
                                           sb["Apref"][:, j * 16 + ci:j * 16 + ci + 1],
                                           sb["Lsnap"][:, j * 1024 + ci * 64:j * 1024 + (ci + 1) * 64],
                                           ALU.mult, ALU.add)
            v.drain()
            v.sem_inc(sem["EV"], 1)
            C.inc("EV", 1)

    if PHASES <= 4:
        return _early_out()

    PB = contextlib.ExitStack()
    ynb = alloc(PB, "ynb", [128, NCB * TOK], BF16)
    ocb = alloc(PB, "ocb", [128, 2048], BF16)
    osq = alloc(PB, "osq", [128, 2048], BF16)
    qtr = alloc(PB, "qtr", [128, 2048], BF16)
    oir = alloc(PB, "oir", [128, 2048], BF16)
    mu1 = alloc(PB, "mu1", [1, 128], F32)
    rs1 = alloc(PB, "rs1", [1, 128], F32)
    qtro = alloc(PB, "qtro", [64, 2048], BF16)
    scso = alloc(PB, "scso", [64, 1024], BF16)
    muB2 = alloc(PB, "muB2", [128, 128], F32)
    rB2 = alloc(PB, "rB2", [128, 128], F32)
    Tc = alloc(PB, "Tc", [128, 128], F32)

    for j in range(NCH):
        with nc.Block() as blk:
            @blk.sync
            def _(sync):
                sync.dma_start(out=qtr[:, :], in_=scr["qt"][:, j * 2048:(j + 1) * 2048]).then_inc(sem["LD"], 16)
                sync.dma_start(out=oir[:, :], in_=scr["oi"][:, j * 2048:(j + 1) * 2048]).then_inc(sem["LD"], 16)
                sync.wait_ge(sem["LD"], C.inc("LD", 32))
                sync.dma_start(out=qtro[:, :], in_=qtr[64:128, :]).then_inc(sem["LD"], 16)
                sync.dma_start(out=scso[:, :], in_=sb["Lsnap"][64:128, j * 1024:(j + 1) * 1024]).then_inc(sem["LD"], 16)
                C.inc("LD", 32)

            @blk.tensor
            def _(t):
                t.wait_ge(sem["LD"], C.get("LD"))
                for h in range(32):
                    ci, off = h // 2, (h % 2) * 64
                    ls_src = (sb["Lsnap"][0:64, j * 1024 + ci * 64:j * 1024 + (ci + 1) * 64]
                              if off == 0 else scso[:, ci * 64:(ci + 1) * 64])
                    qt_src = (qtr[0:64, ci * 128:(ci + 1) * 128]
                              if off == 0 else qtro[:, ci * 128:(ci + 1) * 128])
                    t.matmul(ps[ci % 4][off:off + 64, (ci // 4) * 128:(ci // 4) * 128 + 128],
                             ls_src, qt_src,
                             start=True, stop=True, tile_position=(0, off))
                t.drain()
                t.sem_inc(sem["MM"], 1)
                C.inc("MM", 1)

            @blk.vector
            def _(v):
                v.wait_ge(sem["MM"], C.get("MM"))
                for ci in range(16):
                    v.tensor_tensor(ocb[:, ci * 128:(ci + 1) * 128],
                                    oir[:, ci * 128:(ci + 1) * 128],
                                    ps[ci % 4][:, (ci // 4) * 128:(ci // 4) * 128 + 128], ALU.add)
                v.drain()
                v.sem_inc(sem["EV"], 1)
                C.inc("EV", 1)

            @blk.scalar
            def _(s):
                s.wait_ge(sem["EV"], C.get("EV"))
                s.activation(osq[:, :], ocb[:, :], AF.Square)
                s.drain()
                s.sem_inc(sem["SC"], 1)
                C.inc("SC", 1)

            @blk.tensor
            def _(t):
                t.wait_ge(sem["SC"], C.get("SC"))
                for ci in range(16):
                    t.matmul(ps[4][0:1, 0:128], sb["onesb"][:, :], ocb[:, ci * 128:(ci + 1) * 128],
                             start=(ci == 0), stop=(ci == 15))
                for ci in range(16):
                    t.matmul(ps[5][0:1, 0:128], sb["onesb"][:, :], osq[:, ci * 128:(ci + 1) * 128],
                             start=(ci == 0), stop=(ci == 15))
                t.drain()
                t.sem_inc(sem["MM"], 1)
                C.inc("MM", 1)

            @blk.vector
            def _(v):
                v.wait_ge(sem["MM"], C.get("MM"))
                v.tensor_scalar_mul(mu1[0:1, :], ps[4][0:1, 0:128], float(1.0 / 2048.0))
                v.tensor_scalar_mul(rs1[0:1, :], ps[5][0:1, 0:128], float(1.0 / 2048.0))
                v.drain()
                v.tensor_tensor(Tc[0:1, :], mu1[0:1, :], mu1[0:1, :], ALU.mult)
                v.drain()
                v.tensor_tensor(rs1[0:1, :], rs1[0:1, :], Tc[0:1, :], ALU.subtract)
                v.drain()
                v.sem_inc(sem["EV"], 1)
                C.inc("EV", 1)

            @blk.scalar
            def _(s):
                s.wait_ge(sem["EV"], C.get("EV"))
                s.activation(rs1[0:1, :], rs1[0:1, :], AF.Sqrt, bias=sb["cbias"][0:1, 1:2])
                s.drain()
                s.sem_inc(sem["SC"], 1)
                C.inc("SC", 1)

            @blk.vector
            def _(v):
                v.wait_ge(sem["SC"], C.get("SC"))
                v.reciprocal(rs1[0:1, :], rs1[0:1, :])
                v.drain()
                v.sem_inc(sem["EV"], 1)
                C.inc("EV", 1)

            @blk.tensor
            def _(t):
                t.wait_ge(sem["EV"], C.get("EV"))
                t.matmul(ps[6][:, 0:128], sb["onesrow"][0:1, :], mu1[0:1, :], start=True, stop=True)
                t.matmul(ps[6][:, 128:256], sb["onesrow"][0:1, :], rs1[0:1, :], start=True, stop=True)
                t.drain()
                t.sem_inc(sem["MM"], 1)
                C.inc("MM", 1)

            @blk.vector
            def _(v):
                v.wait_ge(sem["MM"], C.get("MM"))
                v.tensor_copy(muB2[:, :], ps[6][:, 0:128])
                v.tensor_copy(rB2[:, :], ps[6][:, 128:256])
                v.drain()
                scb = C.get("SC")
                for ci in range(16):
                    if ci >= 1:
                        v.wait_ge(sem["SC"], scb + ci)
                    v.tensor_tensor(Tc[:, :], ocb[:, ci * 128:(ci + 1) * 128], muB2[:, :], ALU.subtract)
                    v.drain()
                    v.tensor_tensor(Tc[:, :], Tc[:, :], rB2[:, :], ALU.mult)
                    v.drain()
                    v.sem_inc(sem["EV"], 1)
                C.inc("EV", 16)

            @blk.scalar
            def _(s):
                evb = C.get("EV") - 16
                for ci in range(16):
                    s.wait_ge(sem["EV"], evb + ci + 1)
                    s.activation(ynb[:, ci * TOK + j * 128:ci * TOK + j * 128 + 128], Tc[:, :],
                                 AF.Identity, scale=sb["lng"][:, ci:ci + 1], bias=sb["lnb"][:, ci:ci + 1])
                    s.drain()
                    s.sem_inc(sem["SC"], 1)
                C.inc("SC", 16)
    dump_cb("yn", ynb, 16)

    OW = contextlib.ExitStack()
    wstr2 = [alloc(OW, "wstr2_%d" % i, [128, 2048], BF16) for i in range(2)]
    sbuf_s = alloc(OW, "sbuf_s", [128, NCB * TOK], BF16)
    with nc.Block() as blk:
        @blk.gpsimd
        def _(g):
            mmb = C.get("MM")
            for oc in range(16):
                if oc >= 2:
                    g.wait_ge(sem["MM"], mmb + oc - 1)
                strip_load(g, wstr2[oc % 2], d_ow[oc], s="WA" if oc % 2 == 0 else "WB")
            C.inc("WA", 256 * 8)
            C.inc("WB", 256 * 8)

        @blk.tensor
        def _(t):
            wlba = C.get("WA") - 256 * 8
            wlbb = C.get("WB") - 256 * 8
            evb = C.get("EV")
            for oc in range(16):
                t.wait_ge(sem["WA" if oc % 2 == 0 else "WB"],
                          (wlba if oc % 2 == 0 else wlbb) + 256 * (oc // 2 + 1))
                pp = ps[(oc % 2) * 2: (oc % 2) * 2 + 2]
                if oc >= 2:
                    t.wait_ge(sem["EV"], evb + oc - 1)
                for ci in range(16):
                    for hf in range(2):
                        t.matmul(pp[hf][:, :], wstr2[oc % 2][:, ci * 128:(ci + 1) * 128],
                                 blk_c(ynb, ci)[:, hf * 512:(hf + 1) * 512],
                                 start=(ci == 0), stop=(ci == 15))
                t.drain()
                t.sem_inc(sem["MM"], 1)
            C.inc("MM", 16)

        @blk.vector
        def _(v):
            mmb = C.get("MM") - 16
            for oc in range(16):
                v.wait_ge(sem["MM"], mmb + oc + 1)
                pp = ps[(oc % 2) * 2: (oc % 2) * 2 + 2]
                for hf in range(2):
                    v.tensor_copy(blk_c(sbuf_s, oc)[:, hf * 512:(hf + 1) * 512], pp[hf][:, :])
                v.drain()
                v.sem_inc(sem["EV"], 1)
            C.inc("EV", 16)

        @blk.sync
        def _(sync):
            sync.wait_ge(sem["EV"], C.get("EV"))
            sync.dma_start(out=scr["s"], in_=sbuf_s[:, :]).then_inc(sem["SP"], 16)
            sync.wait_ge(sem["SP"], C.inc("SP", 16))
    dump_cb("s", sbuf_s, 16)
    OW.close()
    PB.close()

    if PHASES <= 5:
        return _early_out()

    LS.close()

    PC = contextlib.ExitStack()
    zb = alloc(PC, "zb", [128, 64 * TOK], BF16)
    UG = contextlib.ExitStack()
    x1b = alloc(UG, "x1b", [128, NCB * TOK], BF16)
    alloc(UG, "Hdp", [128, 128], BF16)
    wstr3 = [alloc(UG, "wstr3_%d" % i, [128, 2048], BF16) for i in range(4)]
    hb = alloc(UG, "hb", [128, TOK], BF16)
    sgb = alloc(UG, "sgb", [128, TOK], F32)

    with nc.Block() as blk:
        @blk.sync
        def _(sync):
            sync.dma_start(out=x1b[:, :], in_=scr["x1"]).then_inc(sem["LD"], 16)
            C.inc("LD", 16)

        @blk.gpsimd
        def _(g):
            mmb = C.get("MM")
            for ic in range(64):
                if ic >= 2:
                    g.wait_ge(sem["MM"], mmb + 2 * (ic - 1))
                sname = "WA" if ic % 2 == 0 else "WB"
                strip_load(g, wstr3[(ic % 2) * 2], d_wup[ic], s=sname)
                strip_load(g, wstr3[(ic % 2) * 2 + 1], d_wg[ic], s=sname)
            C.inc("WA", 512 * 32)
            C.inc("WB", 512 * 32)

        @blk.tensor
        def _(t):
            t.wait_ge(sem["LD"], C.get("LD"))
            wlba = C.get("WA") - 512 * 32
            wlbb = C.get("WB") - 512 * 32
            evb = C.get("EV")
            scb = C.get("SC")
            for ic in range(64):
                t.wait_ge(sem["WA" if ic % 2 == 0 else "WB"],
                          (wlba if ic % 2 == 0 else wlbb) + 512 * (ic // 2 + 1))
                wu, wg = wstr3[(ic % 2) * 2], wstr3[(ic % 2) * 2 + 1]
                if ic >= 1:
                    t.wait_ge(sem["EV"], evb + 2 * ic - 1)
                    t.wait_ge(sem["SC"], scb + ic)
                for ci in range(16):
                    for hf in range(2):
                        t.matmul(ps[hf][:, :], wu[:, ci * 128:(ci + 1) * 128],
                                 blk_c(x1b, ci)[:, hf * 512:(hf + 1) * 512],
                                 start=(ci == 0), stop=(ci == 15))
                        t.matmul(ps[2 + hf][:, :], wg[:, ci * 128:(ci + 1) * 128],
                                 blk_c(x1b, ci)[:, hf * 512:(hf + 1) * 512],
                                 start=(ci == 0), stop=(ci == 15))
                t.drain()
                t.sem_inc(sem["MM"], 1)
                t.wait_ge(sem["EV"], evb + 2 * ic + 1)
                for hf in range(2):
                    t.matmul(ps[4 + hf][:, :], sb["Hdp"][:, :], hb[:, hf * 512:(hf + 1) * 512],
                             start=True, stop=True)
                t.drain()
                t.sem_inc(sem["MM"], 1)
            C.inc("MM", 128)

        @blk.scalar
        def _(s):
            mmb = C.get("MM") - 128
            evb = C.get("EV")
            for ic in range(64):
                s.wait_ge(sem["MM"], mmb + 2 * ic + 1)
                if ic >= 1:
                    s.wait_ge(sem["EV"], evb + 2 * ic - 1)
                for hf in range(2):
                    s.activation(sgb[:, hf * 512:(hf + 1) * 512], ps[2 + hf][:, :],
                                 AF.Silu, scale=sb["s_g"][:, ic:ic + 1])
                s.drain()
                s.sem_inc(sem["SC"], 1)
            C.inc("SC", 64)

        @blk.vector
        def _(v):
            mmb = C.get("MM") - 128
            scb = C.get("SC") - 64
            for ic in range(64):
                v.wait_ge(sem["SC"], scb + ic + 1)
                v.tensor_scalar_mul(sb["Hdp"][:, :], sb["H128b"][:, :], sb["sudp"][:, ic:ic + 1])
                for hf in range(2):
                    v.scalar_tensor_tensor(hb[:, hf * 512:(hf + 1) * 512], ps[hf][:, :],
                                           sb["s_up"][:, ic:ic + 1],
                                           sgb[:, hf * 512:(hf + 1) * 512], ALU.mult, ALU.mult)
                v.drain()
                v.sem_inc(sem["EV"], 1)
                v.wait_ge(sem["MM"], mmb + 2 * ic + 2)
                for hf in range(2):
                    v.tensor_copy(zb[:, ic * TOK + hf * 512: ic * TOK + (hf + 1) * 512], ps[4 + hf][:, :])
                v.drain()
                v.sem_inc(sem["EV"], 1)
            C.inc("EV", 128)

    BF = contextlib.ExitStack()
    ztp = [alloc(BF, "ztp%d" % i, [128, TOK], BF16) for i in range(8)]
    with nc.Block() as blk:
        @blk.vector
        def _(v):
            for d in [1, 2, 4, 8, 16, 32]:
                pairs = [i for i in range(64) if not (i & d)]
                for gs in range(0, 32, 8):
                    grp = pairs[gs:gs + 8]
                    for kk, i in enumerate(grp):
                        v.tensor_tensor(ztp[kk][:, :], zb[:, i * TOK:(i + 1) * TOK],
                                        zb[:, (i + d) * TOK:(i + d + 1) * TOK], ALU.add)
                    v.drain()
                    for i in grp:
                        jb = i + d
                        v.tensor_tensor(zb[:, jb * TOK:(jb + 1) * TOK],
                                        zb[:, i * TOK:(i + 1) * TOK],
                                        zb[:, jb * TOK:(jb + 1) * TOK], ALU.subtract)
                    v.drain()
                    for kk, i in enumerate(grp):
                        v.tensor_copy(zb[:, i * TOK:(i + 1) * TOK], ztp[kk][:, :])
                    v.drain()
            v.sem_inc(sem["EV"], 1)
            C.inc("EV", 1)
    BF.close()
    if DEBUG:
        with nc.Block() as blk:
            @blk.scalar
            def _(s):
                spb = C.get("SP")
                for ib in range(64):
                    if ib:
                        s.wait_ge(sem["SP"], spb + 16 * ib)
                    s.activation(sb["dbuf"][:, :], zb[:, ib * TOK:(ib + 1) * TOK], AF.Copy)
                    s.drain()
                    s.sem_inc(sem["SC"], 1)
                C.inc("SC", 64)

            @blk.sync
            def _(sync):
                scb = C.get("SC") - 64
                for ib in range(64):
                    sync.wait_ge(sem["SC"], scb + ib + 1)
                    sync.dma_start(out=dbg["z"][:, ib * TOK:(ib + 1) * TOK],
                                   in_=sb["dbuf"][:, :]).then_inc(sem["SP"], 16)
                sync.wait_ge(sem["SP"], C.inc("SP", 16 * 64))
    UG.close()

    DN = contextlib.ExitStack()
    gcomb = alloc(DN, "gcomb", [128, TOK], F32)
    srel = alloc(DN, "srel", [128, TOK], BF16)
    xrel = alloc(DN, "xrel", [128, TOK], F32)
    outb = alloc(DN, "outb", [128, TOK], F32)
    wstr4 = [alloc(DN, "wstr4_%d" % i, [128, 64 * 128], BF16) for i in range(2)]

    with nc.Block() as blk:
        @blk.gpsimd
        def _(g):
            mmb = C.get("MM")
            for fc in range(16):
                if fc >= 2:
                    g.wait_ge(sem["MM"], mmb + fc - 1)
                strip_load(g, wstr4[fc % 2], d_wd[fc], ntile=64, s="WA" if fc % 2 == 0 else "WB")
            C.inc("WA", 1024 * 8)
            C.inc("WB", 1024 * 8)

        @blk.sync
        def _(sync):
            evb = C.get("EV")
            for fc in range(16):
                if fc >= 1:
                    sync.wait_ge(sem["EV"], evb + 2 * fc)
                sync.dma_start(out=srel[:, :], in_=scr["s"][:, fc * TOK:(fc + 1) * TOK]).then_inc(sem["LD"], 16)
                sync.dma_start(out=xrel[:, :], in_=d_xT[fc * 128:(fc + 1) * 128, :]).then_inc(sem["LD"], 16)
            C.inc("LD", 32 * 16)

        @blk.tensor
        def _(t):
            wlba = C.get("WA") - 1024 * 8
            wlbb = C.get("WB") - 1024 * 8
            scb = C.get("SC")
            for fc in range(16):
                t.wait_ge(sem["WA" if fc % 2 == 0 else "WB"],
                          (wlba if fc % 2 == 0 else wlbb) + 1024 * (fc // 2 + 1))
                pp = ps[(fc % 2) * 2: (fc % 2) * 2 + 2]
                if fc >= 2:
                    t.wait_ge(sem["SC"], scb + fc - 1)
                for cj in range(64):
                    for hf in range(2):
                        t.matmul(pp[hf][:, :], wstr4[fc % 2][:, cj * 128:(cj + 1) * 128],
                                 zb[:, cj * TOK + hf * 512: cj * TOK + (hf + 1) * 512],
                                 start=(cj == 0), stop=(cj == 63))
                t.drain()
                t.sem_inc(sem["MM"], 1)
            C.inc("MM", 16)

        @blk.scalar
        def _(s):
            mmb = C.get("MM") - 16
            evb = C.get("EV")
            for fc in range(16):
                s.wait_ge(sem["MM"], mmb + fc + 1)
                if fc >= 1:
                    s.wait_ge(sem["EV"], evb + 2 * fc - 1)
                pp = ps[(fc % 2) * 2: (fc % 2) * 2 + 2]
                for hf in range(2):
                    s.activation(gcomb[:, hf * 512:(hf + 1) * 512], pp[hf][:, :],
                                 AF.Silu, scale=sb["s_dp"][:, fc:fc + 1])
                s.drain()
                s.sem_inc(sem["SC"], 1)
                s.wait_ge(sem["EV"], evb + 2 * fc + 2)
                s.dma_start(out=d_out[fc * 128:(fc + 1) * 128, :], in_=outb[:, :]).then_inc(sem["SP"], 16)
            C.inc("SC", 16)
            s.wait_ge(sem["SP"], C.inc("SP", 16 * 16))

        @blk.vector
        def _(v):
            scb = C.get("SC") - 16
            ldb = C.get("LD") - 32 * 16
            spb = C.get("SP") - 16 * 16
            for fc in range(16):
                v.wait_ge(sem["SC"], scb + fc + 1)
                v.wait_ge(sem["LD"], ldb + 32 * (fc + 1))
                if fc >= 1:
                    v.wait_ge(sem["SP"], spb + 16 * fc)
                v.tensor_tensor(outb[:, :], gcomb[:, :], srel[:, :], ALU.mult)
                v.drain()
                v.sem_inc(sem["EV"], 1)
                v.tensor_tensor(outb[:, :], outb[:, :], xrel[:, :], ALU.add)
                v.drain()
                v.sem_inc(sem["EV"], 1)
            C.inc("EV", 32)

    DN.close()
    PC.close()
    est.close()
    return nc


_NC_CACHE = {}


def _stage(inputs):
    x = np.asarray(inputs["L0_attn_x"], dtype=np.float32)

    def colvec(v):
        return np.ascontiguousarray(np.asarray(v, np.float32).reshape(16, 128).T)

    H128 = _hadamard(128).astype(np.float32)
    maskT = np.triu(np.ones((128, 128), np.float32))
    mask4 = np.ascontiguousarray(np.concatenate([maskT] * 4, axis=1))

    com = {
        "c_maa_x": colvec(inputs["L0_attn_maa_x"]),
        "c_maa_r": colvec(inputs["L0_attn_maa_r"]),
        "c_maa_k": colvec(inputs["L0_attn_maa_k"]),
        "c_maa_v": colvec(inputs["L0_attn_maa_v"]),
        "c_maa_w": colvec(inputs["L0_attn_maa_w"]),
        "c_tdv": colvec(inputs["L0_attn_time_decay"]),
        "c_qb": colvec(inputs["L0_attn_q_b"]),
        "c_kb": colvec(inputs["L0_attn_k_b"]),
        "c_vb": colvec(inputs["L0_attn_v_b"]),
        "c_lng": colvec(inputs["L0_attn_ln_g"]),
        "c_lnb": colvec(inputs["L0_attn_ln_b"]),
        "c_su": colvec(inputs["L0_ffn_SU_ug"]),
        "c_sudp": np.ascontiguousarray(np.asarray(inputs["L0_ffn_SU_dp"], np.float32).reshape(64, 128).T),
        "c_wscu": np.ascontiguousarray(np.asarray(inputs["L0_ffn_Wscale_ug"], np.float32)[:8192].reshape(64, 128).T),
        "c_wscg": np.ascontiguousarray(np.asarray(inputs["L0_ffn_Wscale_ug"], np.float32)[8192:].reshape(64, 128).T),
        "c_wsdp": colvec(inputs["L0_ffn_Wscale_dp"]),
        "c_H128": H128,
        "c_ident": np.eye(128, dtype=np.float32),
        "c_mask4": mask4,
        "c_onescol": np.ones((128, 1), np.float32),
        "c_onesrow": np.ones((1, 128), np.float32),
        "w_maaw1": np.ascontiguousarray(np.asarray(inputs["L0_attn_maa_w1"], np.float32).reshape(16, 128, 128)),
        "w_tdw1": np.ascontiguousarray(np.asarray(inputs["L0_attn_td_w1"], np.float32).reshape(16, 128, 64)),
        "w_tdw2": np.ascontiguousarray(np.asarray(inputs["L0_attn_td_w2"], np.float32).reshape(64, 16, 128).transpose(1, 0, 2)),
    }
    w2 = np.asarray(inputs["L0_attn_maa_w2"], np.float32)
    W2blk = np.zeros((128, 4 * 2048), np.float32)
    for i in range(4):
        W2blk[32 * i:32 * i + 32, 2048 * i:2048 * (i + 1)] = w2[i]
    com["w_w2blk"] = np.ascontiguousarray(W2blk.reshape(128, 4, 16, 128).transpose(1, 2, 0, 3))

    def tile4(w):
        return np.ascontiguousarray(
            np.asarray(w, np.float32).reshape(16, 128, 16, 128).transpose(2, 0, 1, 3))

    com["w_qw"] = tile4(inputs["L0_attn_q_w"])
    com["w_kw"] = tile4(inputs["L0_attn_k_w"])
    com["w_vw"] = tile4(inputs["L0_attn_v_w"])
    com["w_ow"] = tile4(inputs["L0_attn_o_w"])
    wupT = np.asarray(inputs["L0_ffn_W_up"], np.float32).T
    wgT = np.asarray(inputs["L0_ffn_W_gate"], np.float32).T
    com["w_wup"] = np.ascontiguousarray(wupT.reshape(16, 128, 64, 128).transpose(2, 0, 1, 3))
    com["w_wg"] = np.ascontiguousarray(wgT.reshape(16, 128, 64, 128).transpose(2, 0, 1, 3))
    wdT = np.asarray(inputs["L0_ffn_W_down"], np.float32).T
    com["w_wd"] = np.ascontiguousarray(wdT.reshape(64, 128, 16, 128).transpose(2, 0, 1, 3))

    in_maps = []
    for core in range(8):
        b, qt = core // 4, core % 4
        t0 = qt * TOK
        m = dict(com)
        m["xT"] = np.ascontiguousarray(x[b, t0:t0 + TOK, :].T)
        xp = np.zeros((2048,), np.float32) if t0 == 0 else np.asarray(x[b, t0 - 1, :], np.float32)
        m["xp0"] = np.ascontiguousarray(xp.reshape(1, 2048))
        m["xp0c"] = colvec(xp)
        um = np.zeros((128, 16), np.float32)
        for j in range(7):
            pred = (j // 4 == b) and (j % 4 < qt)
            um[:, j] = 1.0 if pred else 0.0
            um[:, 8 + j] = 0.0 if pred else 1.0
        m["i_umask"] = um
        in_maps.append(m)
    return in_maps


def kernel(**inputs):
    if "nc" not in _NC_CACHE:
        _NC_CACHE["nc"] = build_nc()
    nc = _NC_CACHE["nc"]
    in_maps = _stage(inputs)
    res = run_bass_kernel_spmd(nc, in_maps, core_ids=list(range(8)))
    out = np.zeros((2, 4096, 2048), np.float32)
    for core in range(8):
        b, qt = core // 4, core % 4
        out[b, qt * TOK:(qt + 1) * TOK, :] = res.results[core]["o_out"].T
    if DEBUG:
        kernel._dbg = res.results
    return out




# version 05
# speedup vs baseline: 1.0737x; 1.0737x over previous
import contextlib
import numpy as np
import concourse.bass as bass
import concourse.mybir as mybir
from concourse.bass_utils import run_bass_kernel_spmd

F32 = mybir.dt.float32
BF16 = mybir.dt.bfloat16
AF = mybir.ActivationFunctionType
ALU = mybir.AluOpType
AXX = mybir.AxisListType.X

TOK = 1024
NCB = 16
L = 128
NCH = 8
DEBUG = False
PHASES = 6
_SKIP_ODD_PT = False
A4SUB = 7


def _hadamard(n):
    h = np.array([[1.0]], dtype=np.float64)
    while h.shape[0] < n:
        h = np.block([[h, h], [h, -h]])
    return h


class Counter:
    def __init__(self):
        self.v = {}

    def inc(self, name, by):
        self.v[name] = self.v.get(name, 0) + by
        return self.v[name]

    def get(self, name):
        return self.v.get(name, 0)


def build_nc():
    nc = bass.Bass("TRN2", target_bir_lowering=False)
    C = Counter()

    d_xT = nc.dram_tensor("xT", [2048, TOK], F32, kind="ExternalInput").ap()
    d_xp0 = nc.dram_tensor("xp0", [1, 2048], F32, kind="ExternalInput").ap()
    d_xp0c = nc.dram_tensor("xp0c", [128, 16], F32, kind="ExternalInput").ap()
    d_umask = nc.dram_tensor("i_umask", [128, 16], F32, kind="ExternalInput").ap()
    vec_names = ["maa_x", "maa_r", "maa_k", "maa_v", "maa_w", "tdv",
                 "qb", "kb", "vb", "lng", "lnb"]
    d_vec = {n: nc.dram_tensor("c_" + n, [128, 16], F32, kind="ExternalInput").ap()
             for n in vec_names}
    d_su = nc.dram_tensor("c_su", [128, 16], F32, kind="ExternalInput").ap()
    d_sudp = nc.dram_tensor("c_sudp", [128, 64], F32, kind="ExternalInput").ap()
    d_wscu = nc.dram_tensor("c_wscu", [128, 64], F32, kind="ExternalInput").ap()
    d_wscg = nc.dram_tensor("c_wscg", [128, 64], F32, kind="ExternalInput").ap()
    d_wsdp = nc.dram_tensor("c_wsdp", [128, 16], F32, kind="ExternalInput").ap()
    d_H128 = nc.dram_tensor("c_H128", [128, 128], F32, kind="ExternalInput").ap()
    d_ident = nc.dram_tensor("c_ident", [128, 128], F32, kind="ExternalInput").ap()
    d_mask4 = nc.dram_tensor("c_mask4", [128, 512], F32, kind="ExternalInput").ap()
    d_ones_col = nc.dram_tensor("c_onescol", [128, 1], F32, kind="ExternalInput").ap()
    d_ones_row = nc.dram_tensor("c_onesrow", [1, 128], F32, kind="ExternalInput").ap()
    d_maaw1 = nc.dram_tensor("w_maaw1", [128, 2048], F32, kind="ExternalInput").ap()
    d_w2blk = nc.dram_tensor("w_w2blk", [128, 8192], F32, kind="ExternalInput").ap()
    d_tdw1 = nc.dram_tensor("w_tdw1", [128, 1024], F32, kind="ExternalInput").ap()
    d_tdw2 = nc.dram_tensor("w_tdw2", [64, 2048], F32, kind="ExternalInput").ap()
    d_qw = nc.dram_tensor("w_qw", [16, 128, 2048], F32, kind="ExternalInput").ap()
    d_kw = nc.dram_tensor("w_kw", [16, 128, 2048], F32, kind="ExternalInput").ap()
    d_vw = nc.dram_tensor("w_vw", [16, 128, 2048], F32, kind="ExternalInput").ap()
    d_ow = nc.dram_tensor("w_ow", [16, 128, 2048], F32, kind="ExternalInput").ap()
    d_wup = nc.dram_tensor("w_wup", [64, 128, 2048], F32, kind="ExternalInput").ap()
    d_wg = nc.dram_tensor("w_wg", [64, 128, 2048], F32, kind="ExternalInput").ap()
    d_wd = nc.dram_tensor("w_wd", [16, 128, 8192], F32, kind="ExternalInput").ap()

    d_out = nc.dram_tensor("o_out", [2048, TOK], F32, kind="ExternalOutput").ap()
    scr = {}
    for n in ["x1", "wexp", "q", "km", "v", "s"]:
        scr[n] = nc.dram_tensor("scr_" + n, [128, NCB * TOK], BF16).ap()
    scr["qt"] = nc.dram_tensor("scr_qt", [128, NCH * 2048], BF16).ap()
    scr["oi"] = nc.dram_tensor("scr_oi", [128, NCH * 2048], BF16).ap()
    d_agin = nc.dram_tensor("ag_in", [2048, 65], F32)
    d_agout = nc.dram_tensor("ag_out", [8 * 2048, 65], F32)

    dbg = {}
    if DEBUG:
        for n, w in [("xn", NCB * TOK), ("dx", NCB * TOK), ("x1", NCB * TOK),
                     ("wexp", NCB * TOK), ("q", NCB * TOK), ("km", NCB * TOK),
                     ("v", NCB * TOK), ("sspan", NCB * 64), ("aspan", 16),
                     ("sin", NCB * 64), ("yn", NCB * TOK), ("s", NCB * TOK),
                     ("z", 64 * TOK), ("mm", TOK)]:
            dbg[n] = nc.dram_tensor("dbg_" + n, [128, w], BF16, kind="ExternalOutput").ap()

    est = contextlib.ExitStack()
    sem = {n: est.enter_context(nc.semaphore(n))
           for n in ["LD", "WL", "WA", "WB", "MM", "EV", "SC", "SP", "CC"]}

    sb = {}

    def alloc(stk, name, shape, dtype):
        t = stk.enter_context(nc.sbuf_tensor(name, shape, dtype))
        sb[name] = t
        return t

    for n in vec_names:
        alloc(est, n, [128, 16], F32)
    for n, sh, dt in [("umask", [128, 16], F32), ("su", [128, 16], F32),
                      ("sudp", [128, 64], F32), ("s_up", [128, 64], F32),
                      ("s_g", [128, 64], F32), ("s_dp", [128, 16], F32),
                      ("H128b", [128, 128], BF16), ("identb", [128, 128], BF16),
                      ("mask4", [128, 512], F32), ("onesb", [128, 1], BF16),
                      ("onesrow", [1, 128], F32),
                      ("xp0col", [128, 16], F32), ("xp0sq", [128, 16], F32),
                      ("pst", [1, 8], F32), ("p128", [128, 2], F32),
                      ("onesf", [128, 1], F32),
                      ("cbias", [128, 4], F32)]:
        alloc(est, n, sh, dt)
    if DEBUG:
        alloc(est, "dbuf", [128, TOK], BF16)
    LS = contextlib.ExitStack()
    for n, sh, dt in [("sfold", [128, 1024], F32),
                      ("S_run", [128, NCB * 64], F32), ("A_run", [128, 16], F32),
                      ("Apref", [128, 128], F32), ("Lsnap", [128, NCH * NCB * 64], BF16),
                      ("ALtmp", [128, 16], F32), ("aggst", [128, 16 * 65], F32)]:
        alloc(LS, n, sh, dt)

    ps = [est.enter_context(nc.psum_tensor("ps%d" % i, [128, 512], F32)) for i in range(7)]
    psb = est.enter_context(nc.psum_tensor("psb", [128, 1024], BF16))

    def blk_c(t, ci):
        return t[:, ci * TOK:(ci + 1) * TOK]

    def strip_load(g, dst, src_oc, ntile=16, s="WL"):
        g.dma_start(out=dst[:, 0:ntile * 128], in_=src_oc).then_inc(sem[s], 16)
        return 16

    def dump_cb(name, src, nblk):
        if not DEBUG:
            return
        with nc.Block() as blk:
            @blk.scalar
            def _(s):
                base_sp = C.get("SP")
                for ci in range(nblk):
                    if ci:
                        s.wait_ge(sem["SP"], base_sp + 16 * ci)
                    s.activation(sb["dbuf"][:, :], src[:, ci * TOK:(ci + 1) * TOK], AF.Copy)
                    s.drain()
                    s.sem_inc(sem["SC"], 1)
                C.inc("SC", nblk)

            @blk.sync
            def _(sync):
                base_sc = C.get("SC") - nblk
                for ci in range(nblk):
                    sync.wait_ge(sem["SC"], base_sc + ci + 1)
                    sync.dma_start(out=dbg[name][:, ci * TOK:(ci + 1) * TOK],
                                   in_=sb["dbuf"][:, :]).then_inc(sem["SP"], 16)
                sync.wait_ge(sem["SP"], C.inc("SP", 16 * nblk))

    def dump_small(name, src, width):
        if not DEBUG:
            return
        with nc.Block() as blk:
            @blk.vector
            def _(v):
                v.tensor_copy(sb["dbuf"][:, 0:width], src)
                v.drain()
                v.sem_inc(sem["EV"], 1)
                C.inc("EV", 1)

            @blk.sync
            def _(sync):
                sync.wait_ge(sem["EV"], C.get("EV"))
                sync.dma_start(out=dbg[name][:, 0:width], in_=sb["dbuf"][:, 0:width]).then_inc(sem["SP"], 16)
                sync.wait_ge(sem["SP"], C.inc("SP", 16))

    with nc.Block() as blk:
        @blk.sync
        def _(sync):
            n_loads = 0
            for n in vec_names:
                sync.dma_start(out=sb[n][:, :], in_=d_vec[n]).then_inc(sem["LD"], 16)
                n_loads += 1
            for n, d in [("umask", d_umask), ("su", d_su), ("sudp", d_sudp),
                         ("s_up", d_wscu), ("s_g", d_wscg), ("s_dp", d_wsdp),
                         ("mask4", d_mask4), ("onesrow", d_ones_row),
                         ("onesf", d_ones_col), ("xp0col", d_xp0c)]:
                sync.dma_start(out=sb[n][:, :], in_=d).then_inc(sem["LD"], 16)
                n_loads += 1
            C.inc("LD", 16 * n_loads)

        @blk.gpsimd
        def _(g):
            g.dma_start(out=sb["H128b"][:, :], in_=d_H128).then_inc(sem["WL"], 16)
            g.dma_start(out=sb["identb"][:, :], in_=d_ident).then_inc(sem["WL"], 16)
            g.dma_start(out=sb["onesb"][:, :], in_=d_ones_col).then_inc(sem["WL"], 16)
            C.inc("WL", 48)

        @blk.vector
        def _(v):
            v.wait_ge(sem["LD"], C.get("LD"))
            v.wait_ge(sem["WL"], C.get("WL"))
            v.tensor_scalar_mul(sb["s_up"][:, :], sb["s_up"][:, :], float(1.0 / np.sqrt(2048.0)))
            v.tensor_scalar_mul(sb["s_g"][:, :], sb["s_g"][:, :], float(1.0 / np.sqrt(2048.0)))
            v.tensor_scalar_mul(sb["s_dp"][:, :], sb["s_dp"][:, :], float(1.0 / np.sqrt(8192.0)))
            v.memset(sb["S_run"][:, :], 0.0)
            v.memset(sb["A_run"][:, :], 1.0)
            v.memset(sb["cbias"][:, 0:1], 1e-5)
            v.memset(sb["cbias"][:, 1:2], 64e-5)
            v.memset(sb["cbias"][:, 2:3], 1.0)
            v.drain()
            v.sem_inc(sem["EV"], 1)
            C.inc("EV", 1)


    def _early_out(close_a1b=False):
        with nc.Block() as blk:
            @blk.sync
            def _(sync):
                sync.dma_start(out=d_out[0:128, :], in_=sb["sfold"][:, :]).then_inc(sem["SP"], 16)
                sync.wait_ge(sem["SP"], C.inc("SP", 16))
        if close_a1b:
            A1b.close()
        LS.close()
        est.close()
        return nc

    A1b = contextlib.ExitStack()
    A1a = contextlib.ExitStack()
    xn = alloc(A1b, "xn", [128, NCB * TOK], BF16)
    dx = alloc(A1b, "dx", [128, NCB * TOK], BF16)
    alloc(A1b, "muB", [128, TOK], F32)
    alloc(A1b, "rB", [128, TOK], F32)
    xbuf = alloc(A1a, "xbuf", [128, NCB * TOK], F32)
    alloc(A1a, "statrow", [1, TOK], F32)
    alloc(A1a, "statrow2", [1, TOK], F32)
    alloc(A1a, "T1", [128, TOK], F32)
    alloc(A1a, "T2", [128, TOK], F32)
    xbb = xn

    with nc.Block() as blk:
        @blk.sync
        def _(sync):
            for ci in range(16):
                sync.dma_start(out=blk_c(xbuf, ci), in_=d_xT[ci * 128:(ci + 1) * 128, :]).then_inc(sem["LD"], 16)
            C.inc("LD", 16 * 16)

        @blk.scalar
        def _(s):
            s.wait_ge(sem["LD"], C.get("LD"))
            for ci in range(16):
                s.activation(blk_c(xbb, ci), blk_c(xbuf, ci), AF.Copy)
            s.activation(sb["xp0sq"][:, :], sb["xp0col"][:, :], AF.Square)
            s.drain()
            s.sem_inc(sem["SC"], 1)
            C.inc("SC", 1)

        @blk.tensor
        def _(t):
            t.wait_ge(sem["SC"], C.get("SC"))
            for hf in range(2):
                for ci in range(16):
                    t.matmul(ps[0 + hf][0:1, :], sb["onesb"][:, :],
                             blk_c(xbb, ci)[:, hf * 512:(hf + 1) * 512],
                             start=(ci == 0), stop=(ci == 15))
            t.matmul(ps[4][0:1, 0:16], sb["onesf"][:, :], sb["xp0col"][:, :], start=True, stop=True)
            t.matmul(ps[4][0:1, 16:32], sb["onesf"][:, :], sb["xp0sq"][:, :], start=True, stop=True)
            t.drain()
            t.sem_inc(sem["MM"], 1)
            C.inc("MM", 1)

        @blk.scalar
        def _(s):
            s.wait_ge(sem["MM"], C.get("MM"))
            for ci in range(16):
                s.activation(blk_c(xbb, ci), blk_c(xbuf, ci), AF.Square)
            s.drain()
            s.sem_inc(sem["SC"], 1)
            C.inc("SC", 1)

        @blk.tensor
        def _(t):
            t.wait_ge(sem["SC"], C.get("SC"))
            for hf in range(2):
                for ci in range(16):
                    t.matmul(ps[2 + hf][0:1, :], sb["onesb"][:, :],
                             blk_c(xbb, ci)[:, hf * 512:(hf + 1) * 512],
                             start=(ci == 0), stop=(ci == 15))
            t.drain()
            t.sem_inc(sem["MM"], 1)
            C.inc("MM", 1)

        @blk.vector
        def _(v):
            v.wait_ge(sem["MM"], C.get("MM"))
            for hf in range(2):
                sl = slice(hf * 512, (hf + 1) * 512)
                v.tensor_scalar_mul(sb["statrow"][0:1, sl], ps[0 + hf][0:1, :], float(1.0 / 2048.0))
                v.tensor_scalar_mul(sb["statrow2"][0:1, sl], ps[2 + hf][0:1, :], float(1.0 / 2048.0))
            v.reduce_sum(sb["pst"][0:1, 0:1], ps[4][0:1, 0:16], axis=AXX)
            v.reduce_sum(sb["pst"][0:1, 1:2], ps[4][0:1, 16:32], axis=AXX)
            v.drain()
            v.tensor_tensor(sb["T1"][0:1, 0:TOK], sb["statrow"][0:1, :], sb["statrow"][0:1, :], ALU.mult)
            v.tensor_scalar_mul(sb["pst"][0:1, 0:2], sb["pst"][0:1, 0:2], float(1.0 / 2048.0))
            v.drain()
            v.tensor_tensor(sb["statrow2"][0:1, :], sb["statrow2"][0:1, :], sb["T1"][0:1, 0:TOK], ALU.subtract)
            v.tensor_tensor(sb["pst"][0:1, 2:3], sb["pst"][0:1, 0:1], sb["pst"][0:1, 0:1], ALU.mult)
            v.drain()
            v.tensor_tensor(sb["pst"][0:1, 1:2], sb["pst"][0:1, 1:2], sb["pst"][0:1, 2:3], ALU.subtract)
            v.drain()
            v.sem_inc(sem["EV"], 1)
            C.inc("EV", 1)

        @blk.scalar
        def _(s):
            s.wait_ge(sem["EV"], C.get("EV"))
            s.activation(sb["statrow2"][0:1, :], sb["statrow2"][0:1, :], AF.Sqrt, bias=sb["cbias"][0:1, 0:1])
            s.activation(sb["pst"][0:1, 1:2], sb["pst"][0:1, 1:2], AF.Sqrt, bias=sb["cbias"][0:1, 0:1])
            s.drain()
            s.sem_inc(sem["SC"], 1)
            C.inc("SC", 1)

        @blk.vector
        def _(v):
            v.wait_ge(sem["SC"], C.get("SC"))
            v.reciprocal(sb["statrow2"][0:1, :], sb["statrow2"][0:1, :])
            v.reciprocal(sb["pst"][0:1, 1:2], sb["pst"][0:1, 1:2])
            v.drain()
            v.sem_inc(sem["EV"], 1)
            C.inc("EV", 1)

        @blk.tensor
        def _(t):
            t.wait_ge(sem["EV"], C.get("EV"))
            for hf in range(2):
                sl = slice(hf * 512, (hf + 1) * 512)
                t.matmul(ps[0 + hf][:, :], sb["onesrow"][0:1, :], sb["statrow"][0:1, sl], start=True, stop=True)
                t.matmul(ps[2 + hf][:, :], sb["onesrow"][0:1, :], sb["statrow2"][0:1, sl], start=True, stop=True)
            t.matmul(ps[4][:, 0:1], sb["onesrow"][0:1, :], sb["pst"][0:1, 0:1], start=True, stop=True)
            t.matmul(ps[4][:, 1:2], sb["onesrow"][0:1, :], sb["pst"][0:1, 1:2], start=True, stop=True)
            t.drain()
            t.sem_inc(sem["MM"], 1)
            C.inc("MM", 1)

        @blk.vector
        def _(v):
            v.wait_ge(sem["MM"], C.get("MM"))
            for hf in range(2):
                sl = slice(hf * 512, (hf + 1) * 512)
                v.tensor_copy(sb["muB"][:, sl], ps[0 + hf][:, :])
                v.tensor_copy(sb["rB"][:, sl], ps[2 + hf][:, :])
            v.tensor_copy(sb["p128"][:, 0:2], ps[4][:, 0:2])
            v.drain()
            v.tensor_scalar(sb["xp0col"][:, :], sb["xp0col"][:, :],
                            sb["p128"][:, 0:1], sb["p128"][:, 1:2], ALU.subtract, ALU.mult)
            for cp in range(8):
                a, b = 2 * cp, 2 * cp + 1
                v.tensor_tensor(sb["T1"][:, :], blk_c(xbuf, a), sb["muB"][:, :], ALU.subtract)
                v.tensor_tensor(sb["T2"][:, :], blk_c(xbuf, b), sb["muB"][:, :], ALU.subtract)
                v.drain()
                v.tensor_tensor(blk_c(xn, a), sb["T1"][:, :], sb["rB"][:, :], ALU.mult)
                v.tensor_tensor(blk_c(xn, b), sb["T2"][:, :], sb["rB"][:, :], ALU.mult)
                v.drain()
            for ci in range(16):
                v.tensor_tensor(blk_c(dx, ci)[:, 1:TOK], blk_c(xn, ci)[:, 0:TOK - 1],
                                blk_c(xn, ci)[:, 1:TOK], ALU.subtract)
                v.tensor_tensor(blk_c(dx, ci)[:, 0:1], sb["xp0col"][:, ci:ci + 1],
                                blk_c(xn, ci)[:, 0:1], ALU.subtract)
            v.drain()
            v.sem_inc(sem["EV"], 1)
            C.inc("EV", 1)
    A1a.close()
    dump_cb("xn", xn, 16)
    dump_cb("dx", dx, 16)

    A2 = contextlib.ExitStack()
    F0 = alloc(A2, "F0", [128, 16 * 512], F32)
    F1 = alloc(A2, "F1", [128, 16 * 512], F32)
    x1t = alloc(A2, "x1t", [128, NCB * TOK], BF16)
    alloc(A2, "H128su", [128, 16 * 128], BF16)

    with nc.Block() as blk:
        @blk.vector
        def _(v):
            for ci in range(16):
                v.tensor_scalar_mul(sb["H128su"][:, ci * 128:(ci + 1) * 128],
                                    sb["H128b"][:, :], sb["su"][:, ci:ci + 1])
            v.drain()
            v.sem_inc(sem["EV"], 1)
            C.inc("EV", 1)

    for hf in range(2):
        with nc.Block() as blk:
            @blk.tensor
            def _(t):
                evb = C.get("EV")
                for ci in range(16):
                    if ci >= 4:
                        t.wait_ge(sem["EV"], evb + ci - 3)
                    t.matmul(ps[ci % 4][:, :], sb["H128su"][:, ci * 128:(ci + 1) * 128],
                             blk_c(xn, ci)[:, hf * 512:(hf + 1) * 512], start=True, stop=True)
                    t.drain()
                    t.sem_inc(sem["MM"], 1)
                C.inc("MM", 16)

            @blk.vector
            def _(v):
                mmb = C.get("MM") - 16
                for ci in range(16):
                    v.wait_ge(sem["MM"], mmb + ci + 1)
                    v.tensor_copy(F0[:, ci * 512:(ci + 1) * 512], ps[ci % 4][:, :])
                    v.drain()
                    v.sem_inc(sem["EV"], 1)
                C.inc("EV", 16)
                v.drain()
                bufs = [F0, F1]
                for si, d in enumerate([1, 2, 4]):
                    src, dst = bufs[si % 2], bufs[(si + 1) % 2]
                    for i in range(16):
                        if i & d:
                            continue
                        j = i + d
                        a = src[:, i * 512:(i + 1) * 512]
                        b = src[:, j * 512:(j + 1) * 512]
                        v.tensor_tensor(dst[:, i * 512:(i + 1) * 512], a, b, ALU.add)
                        v.tensor_tensor(dst[:, j * 512:(j + 1) * 512], a, b, ALU.subtract)
                    v.drain()
                src = bufs[1]
                for i in range(16):
                    if i & 8:
                        continue
                    j = i + 8
                    a = src[:, i * 512:(i + 1) * 512]
                    b = src[:, j * 512:(j + 1) * 512]
                    v.tensor_tensor(x1t[:, i * TOK + hf * 512: i * TOK + hf * 512 + 512], a, b, ALU.add)
                    v.tensor_tensor(x1t[:, j * TOK + hf * 512: j * TOK + hf * 512 + 512], a, b, ALU.subtract)
                v.drain()
                v.sem_inc(sem["EV"], 1)
                C.inc("EV", 1)

    with nc.Block() as blk:
        @blk.sync
        def _(sync):
            sync.dma_start(out=scr["x1"], in_=x1t[:, :]).then_inc(sem["SP"], 16)
            sync.wait_ge(sem["SP"], C.inc("SP", 16))
    dump_cb("x1", x1t, 16)
    A2.close()

    if PHASES <= 1:
        return _early_out(close_a1b=True)

    A3 = contextlib.ExitStack()
    alloc(A3, "mmt", [128, TOK], BF16)
    alloc(A3, "tht", [128, TOK], BF16)
    xxx = alloc(A3, "xxx", [128, TOK], BF16)
    mx = alloc(A3, "mx", [128, NCB * TOK], BF16)
    pout = alloc(A3, "pout", [128, NCB * TOK], BF16)
    wstr = [alloc(A3, "wstr%d" % i, [128, 16 * 128], BF16) for i in range(2)]
    Ta = alloc(A3, "Ta", [128, 512], F32)
    Tb = alloc(A3, "Tb", [128, 512], F32)

    with nc.Block() as blk:
        @blk.gpsimd
        def _(g):
            g.dma_start(out=wstr[0][:, :], in_=d_maaw1).then_inc(sem["WL"], 16)
            C.inc("WL", 16)

        @blk.vector
        def _(v):
            mmb = C.get("MM")
            for ci in range(16):
                if ci >= 1:
                    v.wait_ge(sem["MM"], mmb + ci)
                v.scalar_tensor_tensor(xxx[:, :], blk_c(dx, ci), sb["maa_x"][:, ci:ci + 1],
                                       blk_c(xn, ci), ALU.mult, ALU.add)
                v.drain()
                v.sem_inc(sem["EV"], 1)
            C.inc("EV", 16)

        @blk.tensor
        def _(t):
            t.wait_ge(sem["WL"], C.get("WL"))
            evb = C.get("EV") - 16
            for ci in range(16):
                t.wait_ge(sem["EV"], evb + ci + 1)
                for hf in range(2):
                    t.matmul(ps[4 + hf][:, :], wstr[0][:, ci * 128:(ci + 1) * 128],
                             xxx[:, hf * 512:(hf + 1) * 512], start=(ci == 0), stop=(ci == 15))
                t.drain()
                t.sem_inc(sem["MM"], 1)
            C.inc("MM", 16)

        @blk.scalar
        def _(s):
            s.wait_ge(sem["MM"], C.get("MM"))
            for hf in range(2):
                s.activation(sb["mmt"][:, hf * 512:(hf + 1) * 512], ps[4 + hf][:, :], AF.Tanh)
            s.drain()
            s.sem_inc(sem["SC"], 1)
            C.inc("SC", 1)
    dump_cb("mm", sb["mmt"], 1)

    def mix_block(tgt_idx, vecname):
        with nc.Block() as blk:
            @blk.gpsimd
            def _(g):
                g.dma_start(out=wstr[0][:, :], in_=d_w2blk[:, tgt_idx * 2048:(tgt_idx + 1) * 2048]).then_inc(sem["WL"], 16)
                C.inc("WL", 16)

            @blk.tensor
            def _(t):
                t.wait_ge(sem["WL"], C.get("WL"))
                t.wait_ge(sem["SC"], C.get("SC"))
                evb = C.get("EV")
                for oc in range(16):
                    pp = ps[(oc % 2) * 2: (oc % 2) * 2 + 2]
                    if oc >= 2:
                        t.wait_ge(sem["EV"], evb + oc - 1)
                    for hf in range(2):
                        t.matmul(pp[hf][:, :], wstr[0][:, oc * 128:(oc + 1) * 128],
                                 sb["mmt"][:, hf * 512:(hf + 1) * 512], start=True, stop=True)
                    t.drain()
                    t.sem_inc(sem["MM"], 1)
                C.inc("MM", 16)

            @blk.vector
            def _(v):
                mmb = C.get("MM") - 16
                for oc in range(16):
                    pp = ps[(oc % 2) * 2: (oc % 2) * 2 + 2]
                    v.wait_ge(sem["MM"], mmb + oc + 1)
                    for hf in range(2):
                        v.scalar_tensor_tensor((Ta if hf == 0 else Tb)[:, :],
                                               pp[hf][:, :], sb[vecname][:, oc:oc + 1],
                                               blk_c(dx, oc)[:, hf * 512:(hf + 1) * 512],
                                               ALU.add, ALU.mult)
                    v.drain()
                    for hf in range(2):
                        v.tensor_tensor(blk_c(mx, oc)[:, hf * 512:(hf + 1) * 512],
                                        (Ta if hf == 0 else Tb)[:, :],
                                        blk_c(xn, oc)[:, hf * 512:(hf + 1) * 512], ALU.add)
                    v.drain()
                    v.sem_inc(sem["EV"], 1)
                C.inc("EV", 16)

    def proj_block(wdram, bias_col):
        with nc.Block() as blk:
            @blk.gpsimd
            def _(g):
                mmb = C.get("MM")
                for oc in range(16):
                    if oc >= 2:
                        g.wait_ge(sem["MM"], mmb + oc - 1)
                    strip_load(g, wstr[oc % 2], wdram[oc], s="WA" if oc % 2 == 0 else "WB")
                C.inc("WA", 16 * 8)
                C.inc("WB", 16 * 8)

            @blk.tensor
            def _(t):
                wlba = C.get("WA") - 16 * 8
                wlbb = C.get("WB") - 16 * 8
                scb = C.get("SC")
                for oc in range(16):
                    t.wait_ge(sem["WA" if oc % 2 == 0 else "WB"],
                              (wlba if oc % 2 == 0 else wlbb) + 16 * (oc // 2 + 1))
                    pp = ps[(oc % 2) * 2: (oc % 2) * 2 + 2]
                    if oc >= 2:
                        t.wait_ge(sem["SC"], scb + oc - 1)
                    for ci in range(16):
                        for hf in range(2):
                            t.matmul(pp[hf][:, :], wstr[oc % 2][:, ci * 128:(ci + 1) * 128],
                                     blk_c(mx, ci)[:, hf * 512:(hf + 1) * 512],
                                     start=(ci == 0), stop=(ci == 15))
                    t.drain()
                    t.sem_inc(sem["MM"], 1)
                C.inc("MM", 16)

            @blk.scalar
            def _(s):
                mmb = C.get("MM") - 16
                for oc in range(16):
                    pp = ps[(oc % 2) * 2: (oc % 2) * 2 + 2]
                    s.wait_ge(sem["MM"], mmb + oc + 1)
                    for hf in range(2):
                        s.activation(blk_c(pout, oc)[:, hf * 512:(hf + 1) * 512], pp[hf][:, :],
                                     AF.Identity, bias=sb[bias_col][:, oc:oc + 1])
                    s.drain()
                    s.sem_inc(sem["SC"], 1)
                C.inc("SC", 16)

    def spill_block(dst):
        with nc.Block() as blk:
            @blk.sync
            def _(sync):
                sync.dma_start(out=dst, in_=pout[:, :]).then_inc(sem["SP"], 16)
                sync.wait_ge(sem["SP"], C.inc("SP", 16))

    mix_block(3, "maa_w")
    with nc.Block() as blk:
        @blk.gpsimd
        def _(g):
            g.dma_start(out=wstr[0][:, 0:1024], in_=d_tdw1).then_inc(sem["WL"], 16)
            g.dma_start(out=wstr[1][0:64, 0:2048], in_=d_tdw2).then_inc(sem["WL"], 16)
            C.inc("WL", 32)

        @blk.tensor
        def _(t):
            t.wait_ge(sem["WL"], C.get("WL"))
            for ci in range(16):
                for hf in range(2):
                    t.matmul(ps[4 + hf][0:64, :], wstr[0][:, ci * 64:ci * 64 + 64],
                             blk_c(mx, ci)[:, hf * 512:(hf + 1) * 512],
                             start=(ci == 0), stop=(ci == 15))
            t.drain()
            t.sem_inc(sem["MM"], 1)
            C.inc("MM", 1)

        @blk.scalar
        def _(s):
            s.wait_ge(sem["MM"], C.get("MM"))
            for hf in range(2):
                s.activation(sb["tht"][0:64, hf * 512:(hf + 1) * 512], ps[4 + hf][0:64, :], AF.Tanh)
            s.drain()
            s.sem_inc(sem["SC"], 1)
            C.inc("SC", 1)

        @blk.tensor
        def _(t):
            t.wait_ge(sem["SC"], C.get("SC"))
            scb = C.get("SC")
            for oc in range(16):
                pp = ps[(oc % 2) * 2: (oc % 2) * 2 + 2]
                if oc >= 2:
                    t.wait_ge(sem["SC"], scb + oc - 1)
                for hf in range(2):
                    t.matmul(pp[hf][:, :], wstr[1][0:64, oc * 128:(oc + 1) * 128],
                             sb["tht"][0:64, hf * 512:(hf + 1) * 512], start=True, stop=True)
                t.drain()
                t.sem_inc(sem["MM"], 1)
            C.inc("MM", 16)

        @blk.scalar
        def _(s):
            mmb = C.get("MM") - 16
            evb = C.get("EV")
            for oc in range(16):
                pp = ps[(oc % 2) * 2: (oc % 2) * 2 + 2]
                s.wait_ge(sem["MM"], mmb + oc + 1)
                if oc >= 1:
                    s.wait_ge(sem["EV"], evb + oc)
                for hf in range(2):
                    s.activation((Ta if hf == 0 else Tb)[:, :], pp[hf][:, :],
                                 AF.Exp, bias=sb["tdv"][:, oc:oc + 1])
                s.drain()
                s.sem_inc(sem["SC"], 1)
            C.inc("SC", 16)

        @blk.vector
        def _(v):
            scb = C.get("SC") - 16
            for oc in range(16):
                v.wait_ge(sem["SC"], scb + oc + 1)
                for hf in range(2):
                    v.tensor_scalar_min(blk_c(pout, oc)[:, hf * 512:(hf + 1) * 512],
                                        (Ta if hf == 0 else Tb)[:, :], 5.0)
                v.drain()
                v.sem_inc(sem["EV"], 1)
            C.inc("EV", 16)
    spill_block(scr["wexp"])
    dump_cb("wexp", pout, 16)

    mix_block(2, "maa_v")
    proj_block(d_vw, "vb")
    spill_block(scr["v"])
    dump_cb("v", pout, 16)

    mix_block(1, "maa_k")
    with nc.Block() as blk:
        @blk.sync
        def _(sync):
            scb = C.get("SC")
            for oc in range(16):
                if oc >= 1:
                    sync.wait_ge(sem["SC"], scb + oc)
                sync.dma_start(out=xxx[:, :], in_=scr["wexp"][:, oc * TOK:(oc + 1) * TOK]).then_inc(sem["LD"], 16)
            C.inc("LD", 16 * 16)

        @blk.gpsimd
        def _(g):
            mmb = C.get("MM")
            for oc in range(16):
                if oc >= 2:
                    g.wait_ge(sem["MM"], mmb + oc - 1)
                strip_load(g, wstr[oc % 2], d_kw[oc], s="WA" if oc % 2 == 0 else "WB")
            C.inc("WA", 16 * 8)
            C.inc("WB", 16 * 8)

        @blk.tensor
        def _(t):
            wlba = C.get("WA") - 16 * 8
            wlbb = C.get("WB") - 16 * 8
            scb = C.get("SC")
            for oc in range(16):
                t.wait_ge(sem["WA" if oc % 2 == 0 else "WB"],
                          (wlba if oc % 2 == 0 else wlbb) + 16 * (oc // 2 + 1))
                pp = ps[(oc % 2) * 2: (oc % 2) * 2 + 2]
                if oc >= 2:
                    t.wait_ge(sem["SC"], scb + oc - 1)
                for ci in range(16):
                    for hf in range(2):
                        t.matmul(pp[hf][:, :], wstr[oc % 2][:, ci * 128:(ci + 1) * 128],
                                 blk_c(mx, ci)[:, hf * 512:(hf + 1) * 512],
                                 start=(ci == 0), stop=(ci == 15))
                t.drain()
                t.sem_inc(sem["MM"], 1)
            C.inc("MM", 16)

        @blk.scalar
        def _(s):
            mmb = C.get("MM") - 16
            ldb = C.get("LD") - 256
            evb = C.get("EV")
            for oc in range(16):
                pp = ps[(oc % 2) * 2: (oc % 2) * 2 + 2]
                s.wait_ge(sem["MM"], mmb + oc + 1)
                s.wait_ge(sem["LD"], ldb + 16 * (oc + 1))
                if oc >= 1:
                    s.wait_ge(sem["EV"], evb + oc)
                s.activation(sb["muB"][:, 0:TOK], xxx[:, :], AF.Exp, scale=-1.0)
                for hf in range(2):
                    s.activation((Ta if hf == 0 else Tb)[:, :], pp[hf][:, :],
                                 AF.Identity, bias=sb["kb"][:, oc:oc + 1])
                s.drain()
                s.activation(sb["rB"][:, 0:TOK], sb["muB"][:, 0:TOK], AF.Identity, scale=-1.0, bias=sb["cbias"][:, 2:3])
                s.drain()
                s.sem_inc(sem["SC"], 1)
            C.inc("SC", 16)

        @blk.vector
        def _(v):
            scb = C.get("SC") - 16
            for oc in range(16):
                v.wait_ge(sem["SC"], scb + oc + 1)
                for hf in range(2):
                    v.tensor_tensor(blk_c(pout, oc)[:, hf * 512:(hf + 1) * 512],
                                    (Ta if hf == 0 else Tb)[:, :],
                                    sb["rB"][:, hf * 512:(hf + 1) * 512], ALU.mult)
                v.drain()
                v.sem_inc(sem["EV"], 1)
            C.inc("EV", 16)
    spill_block(scr["km"])
    dump_cb("km", pout, 16)

    mix_block(0, "maa_r")
    proj_block(d_qw, "qb")
    spill_block(scr["q"])
    dump_cb("q", pout, 16)
    A3.close()
    A1b.close()

    if PHASES <= 2:
        return _early_out()

    A4 = contextlib.ExitStack()
    qc = alloc(A4, "qc", [128, 2048], BF16)
    kmc = alloc(A4, "kmc", [128, 2048], BF16)
    vc = alloc(A4, "vc", [128, 2048], BF16)
    wxc = alloc(A4, "wxc", [128, 2048], BF16)
    csc = alloc(A4, "csc", [128, 2048], F32)
    gqc = alloc(A4, "gqc", [128, 2048], F32)
    gqic = alloc(A4, "gqic", [128, 2048], F32)
    qtc = alloc(A4, "qtc", [128, 2048], BF16)
    ktc = alloc(A4, "ktc", [128, 2048], BF16)
    k2c = alloc(A4, "k2c", [128, 2048], BF16)
    k2tm = alloc(A4, "k2tm", [128, 2048], BF16)
    vtm = alloc(A4, "vtm", [128, 2048], BF16)
    pmt = alloc(A4, "pmt", [128, 32 * 128], BF16)
    oic = alloc(A4, "oic", [128, 2048], BF16)
    qts = alloc(A4, "qts", [64, 2048], BF16)
    kts = alloc(A4, "kts", [64, 2048], BF16)

    for j in range(NCH):
        with nc.Block() as blk:
            @blk.sync
            def _(sync):
                for ci in range(16):
                    co = ci * TOK + j * 128
                    for nm, dst in [("q", qc), ("km", kmc), ("v", vc), ("wexp", wxc)]:
                        sync.dma_start(out=dst[:, ci * 128:(ci + 1) * 128],
                                       in_=scr[nm][:, co:co + 128]).then_inc(sem["LD"], 16)
                C.inc("LD", 16 * 64)

            @blk.vector
            def _(v):
                v.wait_ge(sem["LD"], C.get("LD"))
                for ci in range(16):
                    v.tensor_tensor_scan(csc[:, ci * 128:(ci + 1) * 128],
                                         wxc[:, ci * 128:(ci + 1) * 128],
                                         wxc[:, ci * 128:(ci + 1) * 128],
                                         0.0, ALU.add, ALU.bypass)
                v.drain()
                v.sem_inc(sem["EV"], 1)
                C.inc("EV", 1)

            if A4SUB <= 1:
                continue

            @blk.scalar
            def _(s):
                s.wait_ge(sem["EV"], C.get("EV"))
                s.activation(gqc[:, :], csc[:, :], AF.Exp, scale=-1.0)
                s.activation(gqic[:, :], csc[:, :], AF.Exp)
                s.drain()
                s.sem_inc(sem["SC"], 1)
                C.inc("SC", 1)

            if A4SUB <= 2:
                continue

            @blk.vector
            def _(v):
                v.wait_ge(sem["SC"], C.get("SC"))
                v.tensor_tensor(qtc[:, :], qc[:, :], gqc[:, :], ALU.mult)
                v.tensor_tensor(ktc[:, :], kmc[:, :], gqic[:, :], ALU.mult)
                v.drain()
                for ci in range(16):
                    v.tensor_scalar_mul(k2c[:, ci * 128:(ci + 1) * 128],
                                        ktc[:, ci * 128:(ci + 1) * 128],
                                        gqc[:, ci * 128 + 127:ci * 128 + 128])
                v.drain()
                v.sem_inc(sem["EV"], 1)
                C.inc("EV", 1)

            if A4SUB <= 3:
                continue

            @blk.sync
            def _(sync):
                sync.wait_ge(sem["EV"], C.get("EV"))
                sync.dma_start(out=qts[:, :], in_=qtc[64:128, :]).then_inc(sem["LD"], 16)
                sync.dma_start(out=kts[:, :], in_=ktc[64:128, :]).then_inc(sem["LD"], 16)
                C.inc("LD", 32)

            @blk.tensor
            def _(t):
                t.wait_ge(sem["EV"], C.get("EV"))
                evb = C.get("EV")
                for ci in range(16):
                    g, sl = ci // 4, ci % 4
                    if g >= 1 and sl == 0:
                        t.wait_ge(sem["EV"], evb + g)
                    t.transpose(psb[:, sl * 128:(sl + 1) * 128], vc[:, ci * 128:(ci + 1) * 128], sb["identb"][:, :])
                    t.transpose(psb[:, 512 + sl * 128: 512 + (sl + 1) * 128], k2c[:, ci * 128:(ci + 1) * 128], sb["identb"][:, :])
                    if sl == 3:
                        t.drain()
                        t.sem_inc(sem["MM"], 1)
                C.inc("MM", 4)

            @blk.vector
            def _(v):
                mmb = C.get("MM") - 4
                for g in range(4):
                    v.wait_ge(sem["MM"], mmb + g + 1)
                    v.tensor_copy(vtm[:, g * 512:(g + 1) * 512], psb[:, 0:512])
                    v.tensor_copy(k2tm[:, g * 512:(g + 1) * 512], psb[:, 512:1024])
                    v.drain()
                    v.sem_inc(sem["EV"], 1)
                C.inc("EV", 4)

            if A4SUB <= 4:
                continue

            for wave in range(2):
                @blk.tensor
                def _(t):
                    t.wait_ge(sem["EV"], C.get("EV"))
                    t.wait_ge(sem["LD"], C.get("LD"))
                    for hh in range(16):
                        h = wave * 16 + hh
                        ci, off = h // 2, (h % 2) * 64
                        kt_src = ktc[0:64, ci * 128:(ci + 1) * 128] if off == 0 else kts[:, ci * 128:(ci + 1) * 128]
                        qt_src = qtc[0:64, ci * 128:(ci + 1) * 128] if off == 0 else qts[:, ci * 128:(ci + 1) * 128]
                        t.matmul(ps[hh // 4][:, (hh % 4) * 128:(hh % 4 + 1) * 128],
                                 kt_src, qt_src, start=True, stop=True)
                    t.drain()
                    t.sem_inc(sem["MM"], 1)
                    C.inc("MM", 1)

                @blk.vector
                def _(v):
                    v.wait_ge(sem["MM"], C.get("MM"))
                    for bank in range(4):
                        v.tensor_tensor(pmt[:, wave * 2048 + bank * 512: wave * 2048 + (bank + 1) * 512],
                                        ps[bank][:, :], sb["mask4"][:, :], ALU.mult)
                    v.drain()
                    v.sem_inc(sem["EV"], 1)
                    C.inc("EV", 1)

            if A4SUB <= 5:
                continue

            @blk.tensor
            def _(t):
                t.wait_ge(sem["EV"], C.get("EV"))
                for h in range(32):
                    ci, off = h // 2, (h % 2) * 64
                    t.matmul(ps[ci % 4][off:off + 64, (ci // 4) * 128:(ci // 4) * 128 + 128],
                             vtm[:, ci * 128 + off:ci * 128 + off + 64],
                             pmt[:, h * 128:(h + 1) * 128],
                             start=True, stop=True, tile_position=(0, off))
                for h in range(32):
                    ci, off = h // 2, (h % 2) * 64
                    t.matmul(ps[4 + ci % 2][off:off + 64, (ci // 2) * 64:(ci // 2) * 64 + 64],
                             k2tm[:, ci * 128 + off:ci * 128 + off + 64],
                             vtm[:, ci * 128 + off:ci * 128 + off + 64],
                             start=True, stop=True, tile_position=(0, off))
                t.drain()
                t.sem_inc(sem["MM"], 1)
                C.inc("MM", 1)

            if A4SUB <= 6:
                continue

            @blk.vector
            def _(v):
                v.wait_ge(sem["MM"], C.get("MM"))
                v.tensor_copy(sb["Lsnap"][:, j * 1024:(j + 1) * 1024], sb["S_run"][:, :])
                for ci in range(16):
                    v.tensor_copy(sb["Apref"][:, j * 16 + ci:j * 16 + ci + 1],
                                  sb["A_run"][:, ci:ci + 1])
                for ci in range(16):
                    v.tensor_copy(oic[:, ci * 128:(ci + 1) * 128],
                                  ps[ci % 4][:, (ci // 4) * 128:(ci // 4) * 128 + 128])
                v.drain()
                for ci in range(16):
                    v.scalar_tensor_tensor(sb["S_run"][:, ci * 64:(ci + 1) * 64],
                                           sb["S_run"][:, ci * 64:(ci + 1) * 64],
                                           gqc[:, ci * 128 + 127:ci * 128 + 128],
                                           ps[4 + ci % 2][:, (ci // 2) * 64:(ci // 2) * 64 + 64],
                                           ALU.mult, ALU.add)
                    v.tensor_scalar_mul(sb["A_run"][:, ci:ci + 1], sb["A_run"][:, ci:ci + 1],
                                        gqc[:, ci * 128 + 127:ci * 128 + 128])
                v.drain()
                v.sem_inc(sem["EV"], 1)
                C.inc("EV", 1)

            @blk.sync
            def _(sync):
                sync.wait_ge(sem["EV"], C.get("EV"))
                sync.dma_start(out=scr["oi"][:, j * 2048:(j + 1) * 2048], in_=oic[:, :]).then_inc(sem["SP"], 16)
                sync.dma_start(out=scr["qt"][:, j * 2048:(j + 1) * 2048], in_=qtc[:, :]).then_inc(sem["SP"], 16)
                sync.wait_ge(sem["SP"], C.inc("SP", 32))
    A4.close()
    dump_small("sspan", sb["S_run"][:, :], NCB * 64)
    dump_small("aspan", sb["A_run"][:, :], 16)

    if PHASES <= 3:
        return _early_out()

    with nc.Block() as blk:
        @blk.vector
        def _(v):
            for ci in range(16):
                v.tensor_copy(sb["aggst"][:, ci * 65:ci * 65 + 64], sb["S_run"][:, ci * 64:(ci + 1) * 64])
                v.tensor_copy(sb["aggst"][:, ci * 65 + 64:ci * 65 + 65], sb["A_run"][:, ci:ci + 1])
            v.drain()
            v.sem_inc(sem["EV"], 1)
            C.inc("EV", 1)

        @blk.sync
        def _(sync):
            sync.wait_ge(sem["EV"], C.get("EV"))
            for ci in range(16):
                sync.dma_start(out=d_agin.ap()[ci * 128:(ci + 1) * 128, :],
                               in_=sb["aggst"][:, ci * 65:(ci + 1) * 65]).then_inc(sem["SP"], 16)
            C.inc("SP", 16 * 16)

        @blk.gpsimd
        def _(g):
            g.wait_ge(sem["SP"], C.get("SP"))
            g.collective_compute("AllGather", ALU.bypass,
                                 replica_groups=[list(range(8))],
                                 ins=[d_agin.ap()], outs=[d_agout.ap()]).then_inc(sem["CC"], 1)
            C.inc("CC", 1)

        @blk.vector
        def _(v):
            v.memset(sb["S_run"][:, :], 0.0)
            v.drain()
            v.sem_inc(sem["EV"], 1)
            C.inc("EV", 1)

    for jj in range(7):
        with nc.Block() as blk:
            @blk.sync
            def _(sync):
                sync.wait_ge(sem["CC"], C.get("CC"))
                for ci in range(16):
                    sync.dma_start(out=sb["aggst"][:, ci * 65:(ci + 1) * 65],
                                   in_=d_agout.ap()[jj * 2048 + ci * 128: jj * 2048 + (ci + 1) * 128, :]).then_inc(sem["LD"], 16)
                C.inc("LD", 16 * 16)

            @blk.vector
            def _(v):
                v.wait_ge(sem["LD"], C.get("LD"))
                for ci in range(16):
                    v.tensor_scalar(sb["ALtmp"][:, ci:ci + 1],
                                    sb["aggst"][:, ci * 65 + 64:ci * 65 + 65],
                                    sb["umask"][:, jj:jj + 1],
                                    sb["umask"][:, 8 + jj:9 + jj], ALU.mult, ALU.add)
                    v.tensor_scalar_mul(sb["sfold"][:, ci * 64:(ci + 1) * 64],
                                        sb["aggst"][:, ci * 65:ci * 65 + 64],
                                        sb["umask"][:, jj:jj + 1])
                v.drain()
                for ci in range(16):
                    v.scalar_tensor_tensor(sb["S_run"][:, ci * 64:(ci + 1) * 64],
                                           sb["S_run"][:, ci * 64:(ci + 1) * 64],
                                           sb["ALtmp"][:, ci:ci + 1],
                                           sb["sfold"][:, ci * 64:(ci + 1) * 64],
                                           ALU.mult, ALU.add)
                v.drain()
                v.sem_inc(sem["EV"], 1)
                C.inc("EV", 1)
    dump_small("sin", sb["S_run"][:, :], NCB * 64)

    with nc.Block() as blk:
        @blk.vector
        def _(v):
            for j in range(NCH):
                for ci in range(16):
                    v.scalar_tensor_tensor(sb["Lsnap"][:, j * 1024 + ci * 64:j * 1024 + (ci + 1) * 64],
                                           sb["S_run"][:, ci * 64:(ci + 1) * 64],
                                           sb["Apref"][:, j * 16 + ci:j * 16 + ci + 1],
                                           sb["Lsnap"][:, j * 1024 + ci * 64:j * 1024 + (ci + 1) * 64],
                                           ALU.mult, ALU.add)
            v.drain()
            v.sem_inc(sem["EV"], 1)
            C.inc("EV", 1)

    if PHASES <= 4:
        return _early_out()

    PB = contextlib.ExitStack()
    ynb = alloc(PB, "ynb", [128, NCB * TOK], BF16)
    ocb = alloc(PB, "ocb", [128, 2048], BF16)
    osq = alloc(PB, "osq", [128, 2048], BF16)
    qtr = alloc(PB, "qtr", [128, 2048], BF16)
    oir = alloc(PB, "oir", [128, 2048], BF16)
    mu1 = alloc(PB, "mu1", [1, 128], F32)
    rs1 = alloc(PB, "rs1", [1, 128], F32)
    qtro = alloc(PB, "qtro", [64, 2048], BF16)
    scso = alloc(PB, "scso", [64, 1024], BF16)
    muB2 = alloc(PB, "muB2", [128, 128], F32)
    rB2 = alloc(PB, "rB2", [128, 128], F32)
    Tc = alloc(PB, "Tc", [128, 128], F32)

    for j in range(NCH):
        with nc.Block() as blk:
            @blk.sync
            def _(sync):
                sync.dma_start(out=qtr[:, :], in_=scr["qt"][:, j * 2048:(j + 1) * 2048]).then_inc(sem["LD"], 16)
                sync.dma_start(out=oir[:, :], in_=scr["oi"][:, j * 2048:(j + 1) * 2048]).then_inc(sem["LD"], 16)
                sync.wait_ge(sem["LD"], C.inc("LD", 32))
                sync.dma_start(out=qtro[:, :], in_=qtr[64:128, :]).then_inc(sem["LD"], 16)
                sync.dma_start(out=scso[:, :], in_=sb["Lsnap"][64:128, j * 1024:(j + 1) * 1024]).then_inc(sem["LD"], 16)
                C.inc("LD", 32)

            @blk.tensor
            def _(t):
                t.wait_ge(sem["LD"], C.get("LD"))
                for h in range(32):
                    ci, off = h // 2, (h % 2) * 64
                    ls_src = (sb["Lsnap"][0:64, j * 1024 + ci * 64:j * 1024 + (ci + 1) * 64]
                              if off == 0 else scso[:, ci * 64:(ci + 1) * 64])
                    qt_src = (qtr[0:64, ci * 128:(ci + 1) * 128]
                              if off == 0 else qtro[:, ci * 128:(ci + 1) * 128])
                    t.matmul(ps[ci % 4][off:off + 64, (ci // 4) * 128:(ci // 4) * 128 + 128],
                             ls_src, qt_src,
                             start=True, stop=True, tile_position=(0, off))
                t.drain()
                t.sem_inc(sem["MM"], 1)
                C.inc("MM", 1)

            @blk.vector
            def _(v):
                v.wait_ge(sem["MM"], C.get("MM"))
                for ci in range(16):
                    v.tensor_tensor(ocb[:, ci * 128:(ci + 1) * 128],
                                    oir[:, ci * 128:(ci + 1) * 128],
                                    ps[ci % 4][:, (ci // 4) * 128:(ci // 4) * 128 + 128], ALU.add)
                v.drain()
                v.sem_inc(sem["EV"], 1)
                C.inc("EV", 1)

            @blk.scalar
            def _(s):
                s.wait_ge(sem["EV"], C.get("EV"))
                s.activation(osq[:, :], ocb[:, :], AF.Square)
                s.drain()
                s.sem_inc(sem["SC"], 1)
                C.inc("SC", 1)

            @blk.tensor
            def _(t):
                t.wait_ge(sem["SC"], C.get("SC"))
                for ci in range(16):
                    t.matmul(ps[4][0:1, 0:128], sb["onesb"][:, :], ocb[:, ci * 128:(ci + 1) * 128],
                             start=(ci == 0), stop=(ci == 15))
                for ci in range(16):
                    t.matmul(ps[5][0:1, 0:128], sb["onesb"][:, :], osq[:, ci * 128:(ci + 1) * 128],
                             start=(ci == 0), stop=(ci == 15))
                t.drain()
                t.sem_inc(sem["MM"], 1)
                C.inc("MM", 1)

            @blk.vector
            def _(v):
                v.wait_ge(sem["MM"], C.get("MM"))
                v.tensor_scalar_mul(mu1[0:1, :], ps[4][0:1, 0:128], float(1.0 / 2048.0))
                v.tensor_scalar_mul(rs1[0:1, :], ps[5][0:1, 0:128], float(1.0 / 2048.0))
                v.drain()
                v.tensor_tensor(Tc[0:1, :], mu1[0:1, :], mu1[0:1, :], ALU.mult)
                v.drain()
                v.tensor_tensor(rs1[0:1, :], rs1[0:1, :], Tc[0:1, :], ALU.subtract)
                v.drain()
                v.sem_inc(sem["EV"], 1)
                C.inc("EV", 1)

            @blk.scalar
            def _(s):
                s.wait_ge(sem["EV"], C.get("EV"))
                s.activation(rs1[0:1, :], rs1[0:1, :], AF.Sqrt, bias=sb["cbias"][0:1, 1:2])
                s.drain()
                s.sem_inc(sem["SC"], 1)
                C.inc("SC", 1)

            @blk.vector
            def _(v):
                v.wait_ge(sem["SC"], C.get("SC"))
                v.reciprocal(rs1[0:1, :], rs1[0:1, :])
                v.drain()
                v.sem_inc(sem["EV"], 1)
                C.inc("EV", 1)

            @blk.tensor
            def _(t):
                t.wait_ge(sem["EV"], C.get("EV"))
                t.matmul(ps[6][:, 0:128], sb["onesrow"][0:1, :], mu1[0:1, :], start=True, stop=True)
                t.matmul(ps[6][:, 128:256], sb["onesrow"][0:1, :], rs1[0:1, :], start=True, stop=True)
                t.drain()
                t.sem_inc(sem["MM"], 1)
                C.inc("MM", 1)

            @blk.vector
            def _(v):
                v.wait_ge(sem["MM"], C.get("MM"))
                v.tensor_copy(muB2[:, :], ps[6][:, 0:128])
                v.tensor_copy(rB2[:, :], ps[6][:, 128:256])
                v.drain()
                scb = C.get("SC")
                for ci in range(16):
                    if ci >= 1:
                        v.wait_ge(sem["SC"], scb + ci)
                    v.tensor_tensor(Tc[:, :], ocb[:, ci * 128:(ci + 1) * 128], muB2[:, :], ALU.subtract)
                    v.drain()
                    v.tensor_tensor(Tc[:, :], Tc[:, :], rB2[:, :], ALU.mult)
                    v.drain()
                    v.sem_inc(sem["EV"], 1)
                C.inc("EV", 16)

            @blk.scalar
            def _(s):
                evb = C.get("EV") - 16
                for ci in range(16):
                    s.wait_ge(sem["EV"], evb + ci + 1)
                    s.activation(ynb[:, ci * TOK + j * 128:ci * TOK + j * 128 + 128], Tc[:, :],
                                 AF.Identity, scale=sb["lng"][:, ci:ci + 1], bias=sb["lnb"][:, ci:ci + 1])
                    s.drain()
                    s.sem_inc(sem["SC"], 1)
                C.inc("SC", 16)
    dump_cb("yn", ynb, 16)

    OW = contextlib.ExitStack()
    wstr2 = [alloc(OW, "wstr2_%d" % i, [128, 2048], BF16) for i in range(2)]
    sbuf_s = alloc(OW, "sbuf_s", [128, NCB * TOK], BF16)
    with nc.Block() as blk:
        @blk.gpsimd
        def _(g):
            mmb = C.get("MM")
            for oc in range(16):
                if oc >= 2:
                    g.wait_ge(sem["MM"], mmb + oc - 1)
                strip_load(g, wstr2[oc % 2], d_ow[oc], s="WA" if oc % 2 == 0 else "WB")
            C.inc("WA", 16 * 8)
            C.inc("WB", 16 * 8)

        @blk.tensor
        def _(t):
            wlba = C.get("WA") - 16 * 8
            wlbb = C.get("WB") - 16 * 8
            evb = C.get("EV")
            for oc in range(16):
                t.wait_ge(sem["WA" if oc % 2 == 0 else "WB"],
                          (wlba if oc % 2 == 0 else wlbb) + 16 * (oc // 2 + 1))
                pp = ps[(oc % 2) * 2: (oc % 2) * 2 + 2]
                if oc >= 2:
                    t.wait_ge(sem["EV"], evb + oc - 1)
                for ci in range(16):
                    for hf in range(2):
                        t.matmul(pp[hf][:, :], wstr2[oc % 2][:, ci * 128:(ci + 1) * 128],
                                 blk_c(ynb, ci)[:, hf * 512:(hf + 1) * 512],
                                 start=(ci == 0), stop=(ci == 15))
                t.drain()
                t.sem_inc(sem["MM"], 1)
            C.inc("MM", 16)

        @blk.vector
        def _(v):
            mmb = C.get("MM") - 16
            for oc in range(16):
                v.wait_ge(sem["MM"], mmb + oc + 1)
                pp = ps[(oc % 2) * 2: (oc % 2) * 2 + 2]
                for hf in range(2):
                    v.tensor_copy(blk_c(sbuf_s, oc)[:, hf * 512:(hf + 1) * 512], pp[hf][:, :])
                v.drain()
                v.sem_inc(sem["EV"], 1)
            C.inc("EV", 16)

        @blk.sync
        def _(sync):
            sync.wait_ge(sem["EV"], C.get("EV"))
            sync.dma_start(out=scr["s"], in_=sbuf_s[:, :]).then_inc(sem["SP"], 16)
            sync.wait_ge(sem["SP"], C.inc("SP", 16))
    dump_cb("s", sbuf_s, 16)
    OW.close()
    PB.close()

    if PHASES <= 5:
        return _early_out()

    LS.close()

    PC = contextlib.ExitStack()
    zb = alloc(PC, "zb", [128, 64 * TOK], BF16)
    UG = contextlib.ExitStack()
    x1b = alloc(UG, "x1b", [128, NCB * TOK], BF16)
    alloc(UG, "Hdp", [128, 128], BF16)
    wstr3 = [alloc(UG, "wstr3_%d" % i, [128, 2048], BF16) for i in range(4)]
    hb = alloc(UG, "hb", [128, TOK], BF16)
    sgb = alloc(UG, "sgb", [128, TOK], F32)

    with nc.Block() as blk:
        @blk.sync
        def _(sync):
            sync.dma_start(out=x1b[:, :], in_=scr["x1"]).then_inc(sem["LD"], 16)
            C.inc("LD", 16)

        @blk.gpsimd
        def _(g):
            mmb = C.get("MM")
            for ic in range(64):
                if ic >= 2:
                    g.wait_ge(sem["MM"], mmb + 2 * (ic - 1))
                sname = "WA" if ic % 2 == 0 else "WB"
                strip_load(g, wstr3[(ic % 2) * 2], d_wup[ic], s=sname)
                strip_load(g, wstr3[(ic % 2) * 2 + 1], d_wg[ic], s=sname)
            C.inc("WA", 32 * 32)
            C.inc("WB", 32 * 32)

        @blk.tensor
        def _(t):
            t.wait_ge(sem["LD"], C.get("LD"))
            wlba = C.get("WA") - 32 * 32
            wlbb = C.get("WB") - 32 * 32
            evb = C.get("EV")
            scb = C.get("SC")
            for ic in range(64):
                t.wait_ge(sem["WA" if ic % 2 == 0 else "WB"],
                          (wlba if ic % 2 == 0 else wlbb) + 32 * (ic // 2 + 1))
                wu, wg = wstr3[(ic % 2) * 2], wstr3[(ic % 2) * 2 + 1]
                if ic >= 1:
                    t.wait_ge(sem["EV"], evb + 2 * ic - 1)
                    t.wait_ge(sem["SC"], scb + ic)
                for ci in range(16):
                    for hf in range(2):
                        t.matmul(ps[hf][:, :], wu[:, ci * 128:(ci + 1) * 128],
                                 blk_c(x1b, ci)[:, hf * 512:(hf + 1) * 512],
                                 start=(ci == 0), stop=(ci == 15))
                        t.matmul(ps[2 + hf][:, :], wg[:, ci * 128:(ci + 1) * 128],
                                 blk_c(x1b, ci)[:, hf * 512:(hf + 1) * 512],
                                 start=(ci == 0), stop=(ci == 15))
                t.drain()
                t.sem_inc(sem["MM"], 1)
                t.wait_ge(sem["EV"], evb + 2 * ic + 1)
                for hf in range(2):
                    t.matmul(ps[4 + hf][:, :], sb["Hdp"][:, :], hb[:, hf * 512:(hf + 1) * 512],
                             start=True, stop=True)
                t.drain()
                t.sem_inc(sem["MM"], 1)
            C.inc("MM", 128)

        @blk.scalar
        def _(s):
            mmb = C.get("MM") - 128
            evb = C.get("EV")
            for ic in range(64):
                s.wait_ge(sem["MM"], mmb + 2 * ic + 1)
                if ic >= 1:
                    s.wait_ge(sem["EV"], evb + 2 * ic - 1)
                for hf in range(2):
                    s.activation(sgb[:, hf * 512:(hf + 1) * 512], ps[2 + hf][:, :],
                                 AF.Silu, scale=sb["s_g"][:, ic:ic + 1])
                s.drain()
                s.sem_inc(sem["SC"], 1)
            C.inc("SC", 64)

        @blk.vector
        def _(v):
            mmb = C.get("MM") - 128
            scb = C.get("SC") - 64
            for ic in range(64):
                v.wait_ge(sem["SC"], scb + ic + 1)
                v.tensor_scalar_mul(sb["Hdp"][:, :], sb["H128b"][:, :], sb["sudp"][:, ic:ic + 1])
                for hf in range(2):
                    v.scalar_tensor_tensor(hb[:, hf * 512:(hf + 1) * 512], ps[hf][:, :],
                                           sb["s_up"][:, ic:ic + 1],
                                           sgb[:, hf * 512:(hf + 1) * 512], ALU.mult, ALU.mult)
                v.drain()
                v.sem_inc(sem["EV"], 1)
                v.wait_ge(sem["MM"], mmb + 2 * ic + 2)
                for hf in range(2):
                    v.tensor_copy(zb[:, ic * TOK + hf * 512: ic * TOK + (hf + 1) * 512], ps[4 + hf][:, :])
                v.drain()
                v.sem_inc(sem["EV"], 1)
            C.inc("EV", 128)

    BF = contextlib.ExitStack()
    ztp = [alloc(BF, "ztp%d" % i, [128, TOK], BF16) for i in range(8)]
    with nc.Block() as blk:
        @blk.vector
        def _(v):
            for d in [1, 2, 4, 8, 16, 32]:
                pairs = [i for i in range(64) if not (i & d)]
                for gs in range(0, 32, 8):
                    grp = pairs[gs:gs + 8]
                    for kk, i in enumerate(grp):
                        v.tensor_tensor(ztp[kk][:, :], zb[:, i * TOK:(i + 1) * TOK],
                                        zb[:, (i + d) * TOK:(i + d + 1) * TOK], ALU.add)
                    v.drain()
                    for i in grp:
                        jb = i + d
                        v.tensor_tensor(zb[:, jb * TOK:(jb + 1) * TOK],
                                        zb[:, i * TOK:(i + 1) * TOK],
                                        zb[:, jb * TOK:(jb + 1) * TOK], ALU.subtract)
                    v.drain()
                    for kk, i in enumerate(grp):
                        v.tensor_copy(zb[:, i * TOK:(i + 1) * TOK], ztp[kk][:, :])
                    v.drain()
            v.sem_inc(sem["EV"], 1)
            C.inc("EV", 1)
    BF.close()
    if DEBUG:
        with nc.Block() as blk:
            @blk.scalar
            def _(s):
                spb = C.get("SP")
                for ib in range(64):
                    if ib:
                        s.wait_ge(sem["SP"], spb + 16 * ib)
                    s.activation(sb["dbuf"][:, :], zb[:, ib * TOK:(ib + 1) * TOK], AF.Copy)
                    s.drain()
                    s.sem_inc(sem["SC"], 1)
                C.inc("SC", 64)

            @blk.sync
            def _(sync):
                scb = C.get("SC") - 64
                for ib in range(64):
                    sync.wait_ge(sem["SC"], scb + ib + 1)
                    sync.dma_start(out=dbg["z"][:, ib * TOK:(ib + 1) * TOK],
                                   in_=sb["dbuf"][:, :]).then_inc(sem["SP"], 16)
                sync.wait_ge(sem["SP"], C.inc("SP", 16 * 64))
    UG.close()

    DN = contextlib.ExitStack()
    gcomb = alloc(DN, "gcomb", [128, TOK], F32)
    srel = alloc(DN, "srel", [128, TOK], BF16)
    xrel = alloc(DN, "xrel", [128, TOK], F32)
    outb = alloc(DN, "outb", [128, TOK], F32)
    wstr4 = [alloc(DN, "wstr4_%d" % i, [128, 64 * 128], BF16) for i in range(2)]

    with nc.Block() as blk:
        @blk.gpsimd
        def _(g):
            mmb = C.get("MM")
            for fc in range(16):
                if fc >= 2:
                    g.wait_ge(sem["MM"], mmb + fc - 1)
                strip_load(g, wstr4[fc % 2], d_wd[fc], ntile=64, s="WA" if fc % 2 == 0 else "WB")
            C.inc("WA", 16 * 8)
            C.inc("WB", 16 * 8)

        @blk.sync
        def _(sync):
            evb = C.get("EV")
            for fc in range(16):
                if fc >= 1:
                    sync.wait_ge(sem["EV"], evb + 2 * fc)
                sync.dma_start(out=srel[:, :], in_=scr["s"][:, fc * TOK:(fc + 1) * TOK]).then_inc(sem["LD"], 16)
                sync.dma_start(out=xrel[:, :], in_=d_xT[fc * 128:(fc + 1) * 128, :]).then_inc(sem["LD"], 16)
            C.inc("LD", 32 * 16)

        @blk.tensor
        def _(t):
            wlba = C.get("WA") - 16 * 8
            wlbb = C.get("WB") - 16 * 8
            scb = C.get("SC")
            for fc in range(16):
                t.wait_ge(sem["WA" if fc % 2 == 0 else "WB"],
                          (wlba if fc % 2 == 0 else wlbb) + 16 * (fc // 2 + 1))
                pp = ps[(fc % 2) * 2: (fc % 2) * 2 + 2]
                if fc >= 2:
                    t.wait_ge(sem["SC"], scb + fc - 1)
                for cj in range(64):
                    for hf in range(2):
                        t.matmul(pp[hf][:, :], wstr4[fc % 2][:, cj * 128:(cj + 1) * 128],
                                 zb[:, cj * TOK + hf * 512: cj * TOK + (hf + 1) * 512],
                                 start=(cj == 0), stop=(cj == 63))
                t.drain()
                t.sem_inc(sem["MM"], 1)
            C.inc("MM", 16)

        @blk.scalar
        def _(s):
            mmb = C.get("MM") - 16
            evb = C.get("EV")
            for fc in range(16):
                s.wait_ge(sem["MM"], mmb + fc + 1)
                if fc >= 1:
                    s.wait_ge(sem["EV"], evb + 2 * fc - 1)
                pp = ps[(fc % 2) * 2: (fc % 2) * 2 + 2]
                for hf in range(2):
                    s.activation(gcomb[:, hf * 512:(hf + 1) * 512], pp[hf][:, :],
                                 AF.Silu, scale=sb["s_dp"][:, fc:fc + 1])
                s.drain()
                s.sem_inc(sem["SC"], 1)
                s.wait_ge(sem["EV"], evb + 2 * fc + 2)
                s.dma_start(out=d_out[fc * 128:(fc + 1) * 128, :], in_=outb[:, :]).then_inc(sem["SP"], 16)
            C.inc("SC", 16)
            s.wait_ge(sem["SP"], C.inc("SP", 16 * 16))

        @blk.vector
        def _(v):
            scb = C.get("SC") - 16
            ldb = C.get("LD") - 32 * 16
            spb = C.get("SP") - 16 * 16
            for fc in range(16):
                v.wait_ge(sem["SC"], scb + fc + 1)
                v.wait_ge(sem["LD"], ldb + 32 * (fc + 1))
                if fc >= 1:
                    v.wait_ge(sem["SP"], spb + 16 * fc)
                v.tensor_tensor(outb[:, :], gcomb[:, :], srel[:, :], ALU.mult)
                v.drain()
                v.sem_inc(sem["EV"], 1)
                v.tensor_tensor(outb[:, :], outb[:, :], xrel[:, :], ALU.add)
                v.drain()
                v.sem_inc(sem["EV"], 1)
            C.inc("EV", 32)

    DN.close()
    PC.close()
    est.close()
    return nc


_NC_CACHE = {}


def _stage(inputs):
    x = np.asarray(inputs["L0_attn_x"], dtype=np.float32)

    def colvec(v):
        return np.ascontiguousarray(np.asarray(v, np.float32).reshape(16, 128).T)

    H128 = _hadamard(128).astype(np.float32)
    maskT = np.triu(np.ones((128, 128), np.float32))
    mask4 = np.ascontiguousarray(np.concatenate([maskT] * 4, axis=1))

    com = {
        "c_maa_x": colvec(inputs["L0_attn_maa_x"]),
        "c_maa_r": colvec(inputs["L0_attn_maa_r"]),
        "c_maa_k": colvec(inputs["L0_attn_maa_k"]),
        "c_maa_v": colvec(inputs["L0_attn_maa_v"]),
        "c_maa_w": colvec(inputs["L0_attn_maa_w"]),
        "c_tdv": colvec(inputs["L0_attn_time_decay"]),
        "c_qb": colvec(inputs["L0_attn_q_b"]),
        "c_kb": colvec(inputs["L0_attn_k_b"]),
        "c_vb": colvec(inputs["L0_attn_v_b"]),
        "c_lng": colvec(inputs["L0_attn_ln_g"]),
        "c_lnb": colvec(inputs["L0_attn_ln_b"]),
        "c_su": colvec(inputs["L0_ffn_SU_ug"]),
        "c_sudp": np.ascontiguousarray(np.asarray(inputs["L0_ffn_SU_dp"], np.float32).reshape(64, 128).T),
        "c_wscu": np.ascontiguousarray(np.asarray(inputs["L0_ffn_Wscale_ug"], np.float32)[:8192].reshape(64, 128).T),
        "c_wscg": np.ascontiguousarray(np.asarray(inputs["L0_ffn_Wscale_ug"], np.float32)[8192:].reshape(64, 128).T),
        "c_wsdp": colvec(inputs["L0_ffn_Wscale_dp"]),
        "c_H128": H128,
        "c_ident": np.eye(128, dtype=np.float32),
        "c_mask4": mask4,
        "c_onescol": np.ones((128, 1), np.float32),
        "c_onesrow": np.ones((1, 128), np.float32),
        "w_maaw1": np.ascontiguousarray(np.asarray(inputs["L0_attn_maa_w1"], np.float32).reshape(16, 128, 128).transpose(1, 0, 2).reshape(128, 2048)),
        "w_tdw1": np.ascontiguousarray(np.asarray(inputs["L0_attn_td_w1"], np.float32).reshape(16, 128, 64).transpose(1, 0, 2).reshape(128, 1024)),
        "w_tdw2": np.ascontiguousarray(np.asarray(inputs["L0_attn_td_w2"], np.float32)),
    }
    w2 = np.asarray(inputs["L0_attn_maa_w2"], np.float32)
    W2blk = np.zeros((128, 4 * 2048), np.float32)
    for i in range(4):
        W2blk[32 * i:32 * i + 32, 2048 * i:2048 * (i + 1)] = w2[i]
    com["w_w2blk"] = np.ascontiguousarray(W2blk)

    def tile4(w):
        return np.ascontiguousarray(
            np.asarray(w, np.float32).reshape(16, 128, 16, 128).transpose(2, 1, 0, 3).reshape(16, 128, 2048))

    com["w_qw"] = tile4(inputs["L0_attn_q_w"])
    com["w_kw"] = tile4(inputs["L0_attn_k_w"])
    com["w_vw"] = tile4(inputs["L0_attn_v_w"])
    com["w_ow"] = tile4(inputs["L0_attn_o_w"])
    wupT = np.asarray(inputs["L0_ffn_W_up"], np.float32).T
    wgT = np.asarray(inputs["L0_ffn_W_gate"], np.float32).T
    com["w_wup"] = np.ascontiguousarray(wupT.reshape(16, 128, 64, 128).transpose(2, 1, 0, 3).reshape(64, 128, 2048))
    com["w_wg"] = np.ascontiguousarray(wgT.reshape(16, 128, 64, 128).transpose(2, 1, 0, 3).reshape(64, 128, 2048))
    wdT = np.asarray(inputs["L0_ffn_W_down"], np.float32).T
    com["w_wd"] = np.ascontiguousarray(wdT.reshape(64, 128, 16, 128).transpose(2, 1, 0, 3).reshape(16, 128, 8192))

    in_maps = []
    for core in range(8):
        b, qt = core // 4, core % 4
        t0 = qt * TOK
        m = dict(com)
        m["xT"] = np.ascontiguousarray(x[b, t0:t0 + TOK, :].T)
        xp = np.zeros((2048,), np.float32) if t0 == 0 else np.asarray(x[b, t0 - 1, :], np.float32)
        m["xp0"] = np.ascontiguousarray(xp.reshape(1, 2048))
        m["xp0c"] = colvec(xp)
        um = np.zeros((128, 16), np.float32)
        for j in range(7):
            pred = (j // 4 == b) and (j % 4 < qt)
            um[:, j] = 1.0 if pred else 0.0
            um[:, 8 + j] = 0.0 if pred else 1.0
        m["i_umask"] = um
        in_maps.append(m)
    return in_maps


def kernel(**inputs):
    if "nc" not in _NC_CACHE:
        _NC_CACHE["nc"] = build_nc()
    nc = _NC_CACHE["nc"]
    in_maps = _stage(inputs)
    res = run_bass_kernel_spmd(nc, in_maps, core_ids=list(range(8)))
    out = np.zeros((2, 4096, 2048), np.float32)
    for core in range(8):
        b, qt = core // 4, core % 4
        out[b, qt * TOK:(qt + 1) * TOK, :] = res.results[core]["o_out"].T
    if DEBUG:
        kernel._dbg = res.results
    return out




# version 06
# speedup vs baseline: 1.1510x; 1.0720x over previous
import contextlib
import numpy as np
import concourse.bass as bass
import concourse.mybir as mybir
from concourse.bass_utils import run_bass_kernel_spmd

F32 = mybir.dt.float32
BF16 = mybir.dt.bfloat16
AF = mybir.ActivationFunctionType
ALU = mybir.AluOpType
AXX = mybir.AxisListType.X

TOK = 1024
NCB = 16
L = 128
NCH = 8
DEBUG = False
PHASES = 6
_SKIP_ODD_PT = False
A4SUB = 7


def _hadamard(n):
    h = np.array([[1.0]], dtype=np.float64)
    while h.shape[0] < n:
        h = np.block([[h, h], [h, -h]])
    return h


class Counter:
    def __init__(self):
        self.v = {}

    def inc(self, name, by):
        self.v[name] = self.v.get(name, 0) + by
        return self.v[name]

    def get(self, name):
        return self.v.get(name, 0)


def build_nc():
    nc = bass.Bass("TRN2", target_bir_lowering=False)
    C = Counter()

    d_xT = nc.dram_tensor("xT", [2048, TOK], F32, kind="ExternalInput").ap()
    d_xp0 = nc.dram_tensor("xp0", [1, 2048], F32, kind="ExternalInput").ap()
    d_xp0c = nc.dram_tensor("xp0c", [128, 16], F32, kind="ExternalInput").ap()
    d_umask = nc.dram_tensor("i_umask", [128, 16], F32, kind="ExternalInput").ap()
    vec_names = ["maa_x", "maa_r", "maa_k", "maa_v", "maa_w", "tdv",
                 "qb", "kb", "vb", "lng", "lnb"]
    d_vec = {n: nc.dram_tensor("c_" + n, [128, 16], F32, kind="ExternalInput").ap()
             for n in vec_names}
    d_su = nc.dram_tensor("c_su", [128, 16], F32, kind="ExternalInput").ap()
    d_sudp = nc.dram_tensor("c_sudp", [128, 64], F32, kind="ExternalInput").ap()
    d_wscu = nc.dram_tensor("c_wscu", [128, 64], F32, kind="ExternalInput").ap()
    d_wscg = nc.dram_tensor("c_wscg", [128, 64], F32, kind="ExternalInput").ap()
    d_wsdp = nc.dram_tensor("c_wsdp", [128, 16], F32, kind="ExternalInput").ap()
    d_H128 = nc.dram_tensor("c_H128", [128, 128], F32, kind="ExternalInput").ap()
    d_ident = nc.dram_tensor("c_ident", [128, 128], F32, kind="ExternalInput").ap()
    d_mask4 = nc.dram_tensor("c_mask4", [128, 512], F32, kind="ExternalInput").ap()
    d_ones_col = nc.dram_tensor("c_onescol", [128, 1], F32, kind="ExternalInput").ap()
    d_ones_row = nc.dram_tensor("c_onesrow", [1, 128], F32, kind="ExternalInput").ap()
    d_maaw1 = nc.dram_tensor("w_maaw1", [128, 2048], F32, kind="ExternalInput").ap()
    d_w2blk = nc.dram_tensor("w_w2blk", [128, 8192], F32, kind="ExternalInput").ap()
    d_tdw1 = nc.dram_tensor("w_tdw1", [128, 1024], F32, kind="ExternalInput").ap()
    d_tdw2 = nc.dram_tensor("w_tdw2", [64, 2048], F32, kind="ExternalInput").ap()
    d_qw = nc.dram_tensor("w_qw", [16, 128, 2048], F32, kind="ExternalInput").ap()
    d_kw = nc.dram_tensor("w_kw", [16, 128, 2048], F32, kind="ExternalInput").ap()
    d_vw = nc.dram_tensor("w_vw", [16, 128, 2048], F32, kind="ExternalInput").ap()
    d_ow = nc.dram_tensor("w_ow", [16, 128, 2048], F32, kind="ExternalInput").ap()
    d_wup = nc.dram_tensor("w_wup", [64, 128, 2048], F32, kind="ExternalInput").ap()
    d_wg = nc.dram_tensor("w_wg", [64, 128, 2048], F32, kind="ExternalInput").ap()
    d_wd = nc.dram_tensor("w_wd", [16, 128, 8192], F32, kind="ExternalInput").ap()

    d_out = nc.dram_tensor("o_out", [2048, TOK], F32, kind="ExternalOutput").ap()
    scr = {}
    for n in ["x1", "wexp", "q", "km", "v", "s"]:
        scr[n] = nc.dram_tensor("scr_" + n, [128, NCB * TOK], BF16).ap()
    scr["qt"] = nc.dram_tensor("scr_qt", [128, NCH * 2048], BF16).ap()
    scr["oi"] = nc.dram_tensor("scr_oi", [128, NCH * 2048], BF16).ap()
    d_agin = nc.dram_tensor("ag_in", [2048, 65], F32)
    d_agout = nc.dram_tensor("ag_out", [8 * 2048, 65], F32)

    dbg = {}
    if DEBUG:
        for n, w in [("xn", NCB * TOK), ("dx", NCB * TOK), ("x1", NCB * TOK),
                     ("wexp", NCB * TOK), ("q", NCB * TOK), ("km", NCB * TOK),
                     ("v", NCB * TOK), ("sspan", NCB * 64), ("aspan", 16),
                     ("sin", NCB * 64), ("yn", NCB * TOK), ("s", NCB * TOK),
                     ("z", 64 * TOK), ("mm", TOK)]:
            dbg[n] = nc.dram_tensor("dbg_" + n, [128, w], BF16, kind="ExternalOutput").ap()

    est = contextlib.ExitStack()
    sem = {n: est.enter_context(nc.semaphore(n))
           for n in ["LD", "WL", "WA", "WB", "LA", "LB", "SH", "MM", "EV", "SC", "SP", "CC"]}

    sb = {}

    def alloc(stk, name, shape, dtype):
        t = stk.enter_context(nc.sbuf_tensor(name, shape, dtype))
        sb[name] = t
        return t

    for n in vec_names:
        alloc(est, n, [128, 16], F32)
    for n, sh, dt in [("umask", [128, 16], F32), ("su", [128, 16], F32),
                      ("sudp", [128, 64], F32), ("s_up", [128, 64], F32),
                      ("s_g", [128, 64], F32), ("s_dp", [128, 16], F32),
                      ("H128b", [128, 128], BF16), ("identb", [128, 128], BF16),
                      ("mask4", [128, 512], F32), ("onesb", [128, 1], BF16),
                      ("onesrow", [1, 128], F32),
                      ("xp0col", [128, 16], F32), ("xp0sq", [128, 16], F32),
                      ("pst", [1, 8], F32), ("p128", [128, 2], F32),
                      ("onesf", [128, 1], F32),
                      ("cbias", [128, 4], F32)]:
        alloc(est, n, sh, dt)
    if DEBUG:
        alloc(est, "dbuf", [128, TOK], BF16)
    LS = contextlib.ExitStack()
    for n, sh, dt in [("sfold", [128, 1024], F32),
                      ("S_run", [128, NCB * 64], F32), ("A_run", [128, 16], F32),
                      ("Apref", [128, 128], F32), ("Lsnap", [128, NCH * NCB * 64], BF16),
                      ("ALtmp", [128, 16], F32), ("aggst", [128, 16 * 65], F32)]:
        alloc(LS, n, sh, dt)

    ps = [est.enter_context(nc.psum_tensor("ps%d" % i, [128, 512], F32)) for i in range(7)]
    psb = est.enter_context(nc.psum_tensor("psb", [128, 1024], BF16))

    def blk_c(t, ci):
        return t[:, ci * TOK:(ci + 1) * TOK]

    def strip_load(g, dst, src_oc, ntile=16, s="WL"):
        g.dma_start(out=dst[:, 0:ntile * 128], in_=src_oc).then_inc(sem[s], 16)
        return 16

    def dump_cb(name, src, nblk):
        if not DEBUG:
            return
        with nc.Block() as blk:
            @blk.scalar
            def _(s):
                base_sp = C.get("SP")
                for ci in range(nblk):
                    if ci:
                        s.wait_ge(sem["SP"], base_sp + 16 * ci)
                    s.activation(sb["dbuf"][:, :], src[:, ci * TOK:(ci + 1) * TOK], AF.Copy)
                    s.drain()
                    s.sem_inc(sem["SC"], 1)
                C.inc("SC", nblk)

            @blk.sync
            def _(sync):
                base_sc = C.get("SC") - nblk
                for ci in range(nblk):
                    sync.wait_ge(sem["SC"], base_sc + ci + 1)
                    sync.dma_start(out=dbg[name][:, ci * TOK:(ci + 1) * TOK],
                                   in_=sb["dbuf"][:, :]).then_inc(sem["SP"], 16)
                sync.wait_ge(sem["SP"], C.inc("SP", 16 * nblk))

    def dump_small(name, src, width):
        if not DEBUG:
            return
        with nc.Block() as blk:
            @blk.vector
            def _(v):
                v.tensor_copy(sb["dbuf"][:, 0:width], src)
                v.drain()
                v.sem_inc(sem["EV"], 1)
                C.inc("EV", 1)

            @blk.sync
            def _(sync):
                sync.wait_ge(sem["EV"], C.get("EV"))
                sync.dma_start(out=dbg[name][:, 0:width], in_=sb["dbuf"][:, 0:width]).then_inc(sem["SP"], 16)
                sync.wait_ge(sem["SP"], C.inc("SP", 16))

    with nc.Block() as blk:
        @blk.sync
        def _(sync):
            n_loads = 0
            for n in vec_names:
                sync.dma_start(out=sb[n][:, :], in_=d_vec[n]).then_inc(sem["LD"], 16)
                n_loads += 1
            for n, d in [("umask", d_umask), ("su", d_su), ("sudp", d_sudp),
                         ("s_up", d_wscu), ("s_g", d_wscg), ("s_dp", d_wsdp),
                         ("mask4", d_mask4), ("onesrow", d_ones_row),
                         ("onesf", d_ones_col), ("xp0col", d_xp0c)]:
                sync.dma_start(out=sb[n][:, :], in_=d).then_inc(sem["LD"], 16)
                n_loads += 1
            C.inc("LD", 16 * n_loads)

        @blk.gpsimd
        def _(g):
            g.dma_start(out=sb["H128b"][:, :], in_=d_H128).then_inc(sem["WL"], 16)
            g.dma_start(out=sb["identb"][:, :], in_=d_ident).then_inc(sem["WL"], 16)
            g.dma_start(out=sb["onesb"][:, :], in_=d_ones_col).then_inc(sem["WL"], 16)
            C.inc("WL", 48)

        @blk.vector
        def _(v):
            v.wait_ge(sem["LD"], C.get("LD"))
            v.wait_ge(sem["WL"], C.get("WL"))
            v.tensor_scalar_mul(sb["s_up"][:, :], sb["s_up"][:, :], float(1.0 / np.sqrt(2048.0)))
            v.tensor_scalar_mul(sb["s_g"][:, :], sb["s_g"][:, :], float(1.0 / np.sqrt(2048.0)))
            v.tensor_scalar_mul(sb["s_dp"][:, :], sb["s_dp"][:, :], float(1.0 / np.sqrt(8192.0)))
            v.memset(sb["S_run"][:, :], 0.0)
            v.memset(sb["A_run"][:, :], 1.0)
            v.memset(sb["cbias"][:, 0:1], 1e-5)
            v.memset(sb["cbias"][:, 1:2], 64e-5)
            v.memset(sb["cbias"][:, 2:3], 1.0)
            v.drain()
            v.sem_inc(sem["EV"], 1)
            C.inc("EV", 1)


    def _early_out(close_a1b=False):
        with nc.Block() as blk:
            @blk.sync
            def _(sync):
                sync.dma_start(out=d_out[0:128, :], in_=sb["S_run"][:, :]).then_inc(sem["SP"], 16)
                sync.wait_ge(sem["SP"], C.inc("SP", 16))
        if close_a1b:
            A1b.close()
        LS.close()
        est.close()
        return nc

    A1b = contextlib.ExitStack()
    A1a = contextlib.ExitStack()
    xn = alloc(A1b, "xn", [128, NCB * TOK], BF16)
    dx = alloc(A1b, "dx", [128, NCB * TOK], BF16)
    alloc(A1b, "muB", [128, TOK], F32)
    alloc(A1b, "rB", [128, TOK], F32)
    xbuf = alloc(A1a, "xbuf", [128, NCB * TOK], F32)
    alloc(A1a, "statrow", [1, TOK], F32)
    alloc(A1a, "statrow2", [1, TOK], F32)
    alloc(A1a, "T1", [128, TOK], F32)
    alloc(A1a, "T2", [128, TOK], F32)
    xbb = xn

    with nc.Block() as blk:
        @blk.sync
        def _(sync):
            for ci in range(16):
                sync.dma_start(out=blk_c(xbuf, ci), in_=d_xT[ci * 128:(ci + 1) * 128, :]).then_inc(sem["LD"], 16)
            C.inc("LD", 16 * 16)

        @blk.scalar
        def _(s):
            s.wait_ge(sem["LD"], C.get("LD"))
            for ci in range(16):
                s.activation(blk_c(xbb, ci), blk_c(xbuf, ci), AF.Copy)
            s.activation(sb["xp0sq"][:, :], sb["xp0col"][:, :], AF.Square)
            s.drain()
            s.sem_inc(sem["SC"], 1)
            C.inc("SC", 1)

        @blk.tensor
        def _(t):
            t.wait_ge(sem["SC"], C.get("SC"))
            for hf in range(2):
                for ci in range(16):
                    t.matmul(ps[0 + hf][0:1, :], sb["onesb"][:, :],
                             blk_c(xbb, ci)[:, hf * 512:(hf + 1) * 512],
                             start=(ci == 0), stop=(ci == 15))
            t.matmul(ps[4][0:1, 0:16], sb["onesf"][:, :], sb["xp0col"][:, :], start=True, stop=True)
            t.matmul(ps[4][0:1, 16:32], sb["onesf"][:, :], sb["xp0sq"][:, :], start=True, stop=True)
            t.drain()
            t.sem_inc(sem["MM"], 1)
            C.inc("MM", 1)

        @blk.scalar
        def _(s):
            s.wait_ge(sem["MM"], C.get("MM"))
            for ci in range(16):
                s.activation(blk_c(xbb, ci), blk_c(xbuf, ci), AF.Square)
            s.drain()
            s.sem_inc(sem["SC"], 1)
            C.inc("SC", 1)

        @blk.tensor
        def _(t):
            t.wait_ge(sem["SC"], C.get("SC"))
            for hf in range(2):
                for ci in range(16):
                    t.matmul(ps[2 + hf][0:1, :], sb["onesb"][:, :],
                             blk_c(xbb, ci)[:, hf * 512:(hf + 1) * 512],
                             start=(ci == 0), stop=(ci == 15))
            t.drain()
            t.sem_inc(sem["MM"], 1)
            C.inc("MM", 1)

        @blk.vector
        def _(v):
            v.wait_ge(sem["MM"], C.get("MM"))
            for hf in range(2):
                sl = slice(hf * 512, (hf + 1) * 512)
                v.tensor_scalar_mul(sb["statrow"][0:1, sl], ps[0 + hf][0:1, :], float(1.0 / 2048.0))
                v.tensor_scalar_mul(sb["statrow2"][0:1, sl], ps[2 + hf][0:1, :], float(1.0 / 2048.0))
            v.reduce_sum(sb["pst"][0:1, 0:1], ps[4][0:1, 0:16], axis=AXX)
            v.reduce_sum(sb["pst"][0:1, 1:2], ps[4][0:1, 16:32], axis=AXX)
            v.drain()
            v.tensor_tensor(sb["T1"][0:1, 0:TOK], sb["statrow"][0:1, :], sb["statrow"][0:1, :], ALU.mult)
            v.tensor_scalar_mul(sb["pst"][0:1, 0:2], sb["pst"][0:1, 0:2], float(1.0 / 2048.0))
            v.drain()
            v.tensor_tensor(sb["statrow2"][0:1, :], sb["statrow2"][0:1, :], sb["T1"][0:1, 0:TOK], ALU.subtract)
            v.tensor_tensor(sb["pst"][0:1, 2:3], sb["pst"][0:1, 0:1], sb["pst"][0:1, 0:1], ALU.mult)
            v.drain()
            v.tensor_tensor(sb["pst"][0:1, 1:2], sb["pst"][0:1, 1:2], sb["pst"][0:1, 2:3], ALU.subtract)
            v.drain()
            v.sem_inc(sem["EV"], 1)
            C.inc("EV", 1)

        @blk.scalar
        def _(s):
            s.wait_ge(sem["EV"], C.get("EV"))
            s.activation(sb["statrow2"][0:1, :], sb["statrow2"][0:1, :], AF.Sqrt, bias=sb["cbias"][0:1, 0:1])
            s.activation(sb["pst"][0:1, 1:2], sb["pst"][0:1, 1:2], AF.Sqrt, bias=sb["cbias"][0:1, 0:1])
            s.drain()
            s.sem_inc(sem["SC"], 1)
            C.inc("SC", 1)

        @blk.vector
        def _(v):
            v.wait_ge(sem["SC"], C.get("SC"))
            v.reciprocal(sb["statrow2"][0:1, :], sb["statrow2"][0:1, :])
            v.reciprocal(sb["pst"][0:1, 1:2], sb["pst"][0:1, 1:2])
            v.drain()
            v.sem_inc(sem["EV"], 1)
            C.inc("EV", 1)

        @blk.tensor
        def _(t):
            t.wait_ge(sem["EV"], C.get("EV"))
            for hf in range(2):
                sl = slice(hf * 512, (hf + 1) * 512)
                t.matmul(ps[0 + hf][:, :], sb["onesrow"][0:1, :], sb["statrow"][0:1, sl], start=True, stop=True)
                t.matmul(ps[2 + hf][:, :], sb["onesrow"][0:1, :], sb["statrow2"][0:1, sl], start=True, stop=True)
            t.matmul(ps[4][:, 0:1], sb["onesrow"][0:1, :], sb["pst"][0:1, 0:1], start=True, stop=True)
            t.matmul(ps[4][:, 1:2], sb["onesrow"][0:1, :], sb["pst"][0:1, 1:2], start=True, stop=True)
            t.drain()
            t.sem_inc(sem["MM"], 1)
            C.inc("MM", 1)

        @blk.vector
        def _(v):
            v.wait_ge(sem["MM"], C.get("MM"))
            for hf in range(2):
                sl = slice(hf * 512, (hf + 1) * 512)
                v.tensor_copy(sb["muB"][:, sl], ps[0 + hf][:, :])
                v.tensor_copy(sb["rB"][:, sl], ps[2 + hf][:, :])
            v.tensor_copy(sb["p128"][:, 0:2], ps[4][:, 0:2])
            v.drain()
            v.tensor_scalar(sb["xp0col"][:, :], sb["xp0col"][:, :],
                            sb["p128"][:, 0:1], sb["p128"][:, 1:2], ALU.subtract, ALU.mult)
            for cp in range(8):
                a, b = 2 * cp, 2 * cp + 1
                v.tensor_tensor(sb["T1"][:, :], blk_c(xbuf, a), sb["muB"][:, :], ALU.subtract)
                v.tensor_tensor(sb["T2"][:, :], blk_c(xbuf, b), sb["muB"][:, :], ALU.subtract)
                v.drain()
                v.tensor_tensor(blk_c(xn, a), sb["T1"][:, :], sb["rB"][:, :], ALU.mult)
                v.tensor_tensor(blk_c(xn, b), sb["T2"][:, :], sb["rB"][:, :], ALU.mult)
                v.drain()
            for ci in range(16):
                v.tensor_tensor(blk_c(dx, ci)[:, 1:TOK], blk_c(xn, ci)[:, 0:TOK - 1],
                                blk_c(xn, ci)[:, 1:TOK], ALU.subtract)
                v.tensor_tensor(blk_c(dx, ci)[:, 0:1], sb["xp0col"][:, ci:ci + 1],
                                blk_c(xn, ci)[:, 0:1], ALU.subtract)
            v.drain()
            v.sem_inc(sem["EV"], 1)
            C.inc("EV", 1)
    A1a.close()
    dump_cb("xn", xn, 16)
    dump_cb("dx", dx, 16)

    A2 = contextlib.ExitStack()
    F0 = alloc(A2, "F0", [128, 16 * 512], F32)
    F1 = alloc(A2, "F1", [128, 16 * 512], F32)
    x1t = alloc(A2, "x1t", [128, NCB * TOK], BF16)
    alloc(A2, "H128su", [128, 16 * 128], BF16)

    with nc.Block() as blk:
        @blk.vector
        def _(v):
            for ci in range(16):
                v.tensor_scalar_mul(sb["H128su"][:, ci * 128:(ci + 1) * 128],
                                    sb["H128b"][:, :], sb["su"][:, ci:ci + 1])
            v.drain()
            v.sem_inc(sem["EV"], 1)
            C.inc("EV", 1)

    for hf in range(2):
        with nc.Block() as blk:
            @blk.tensor
            def _(t):
                evb = C.get("EV")
                for ci in range(16):
                    if ci >= 4:
                        t.wait_ge(sem["EV"], evb + ci - 3)
                    t.matmul(ps[ci % 4][:, :], sb["H128su"][:, ci * 128:(ci + 1) * 128],
                             blk_c(xn, ci)[:, hf * 512:(hf + 1) * 512], start=True, stop=True)
                    t.drain()
                    t.sem_inc(sem["MM"], 1)
                C.inc("MM", 16)

            @blk.vector
            def _(v):
                mmb = C.get("MM") - 16
                for ci in range(16):
                    v.wait_ge(sem["MM"], mmb + ci + 1)
                    v.tensor_copy(F0[:, ci * 512:(ci + 1) * 512], ps[ci % 4][:, :])
                    v.drain()
                    v.sem_inc(sem["EV"], 1)
                C.inc("EV", 16)
                v.drain()
                bufs = [F0, F1]
                for si, d in enumerate([1, 2, 4]):
                    src, dst = bufs[si % 2], bufs[(si + 1) % 2]
                    for i in range(16):
                        if i & d:
                            continue
                        j = i + d
                        a = src[:, i * 512:(i + 1) * 512]
                        b = src[:, j * 512:(j + 1) * 512]
                        v.tensor_tensor(dst[:, i * 512:(i + 1) * 512], a, b, ALU.add)
                        v.tensor_tensor(dst[:, j * 512:(j + 1) * 512], a, b, ALU.subtract)
                    v.drain()
                src = bufs[1]
                for i in range(16):
                    if i & 8:
                        continue
                    j = i + 8
                    a = src[:, i * 512:(i + 1) * 512]
                    b = src[:, j * 512:(j + 1) * 512]
                    v.tensor_tensor(x1t[:, i * TOK + hf * 512: i * TOK + hf * 512 + 512], a, b, ALU.add)
                    v.tensor_tensor(x1t[:, j * TOK + hf * 512: j * TOK + hf * 512 + 512], a, b, ALU.subtract)
                v.drain()
                v.sem_inc(sem["EV"], 1)
                C.inc("EV", 1)

    with nc.Block() as blk:
        @blk.sync
        def _(sync):
            sync.dma_start(out=scr["x1"], in_=x1t[:, :]).then_inc(sem["SP"], 16)
            sync.wait_ge(sem["SP"], C.inc("SP", 16))
    dump_cb("x1", x1t, 16)
    A2.close()

    if PHASES <= 1:
        return _early_out(close_a1b=True)

    A3 = contextlib.ExitStack()
    alloc(A3, "mmt", [128, TOK], BF16)
    alloc(A3, "tht", [128, TOK], BF16)
    xxx = alloc(A3, "xxx", [128, TOK], BF16)
    mx = alloc(A3, "mx", [128, NCB * TOK], BF16)
    pout = alloc(A3, "pout", [128, NCB * TOK], BF16)
    wstr = [alloc(A3, "wstr%d" % i, [128, 16 * 128], BF16) for i in range(2)]
    Ta = alloc(A3, "Ta", [128, 512], F32)
    Tb = alloc(A3, "Tb", [128, 512], F32)

    with nc.Block() as blk:
        @blk.gpsimd
        def _(g):
            g.dma_start(out=wstr[0][:, :], in_=d_maaw1).then_inc(sem["WL"], 16)
            C.inc("WL", 16)

        @blk.vector
        def _(v):
            mmb = C.get("MM")
            for ci in range(16):
                if ci >= 1:
                    v.wait_ge(sem["MM"], mmb + ci)
                v.scalar_tensor_tensor(xxx[:, :], blk_c(dx, ci), sb["maa_x"][:, ci:ci + 1],
                                       blk_c(xn, ci), ALU.mult, ALU.add)
                v.drain()
                v.sem_inc(sem["EV"], 1)
            C.inc("EV", 16)

        @blk.tensor
        def _(t):
            t.wait_ge(sem["WL"], C.get("WL"))
            evb = C.get("EV") - 16
            for ci in range(16):
                t.wait_ge(sem["EV"], evb + ci + 1)
                for hf in range(2):
                    t.matmul(ps[4 + hf][:, :], wstr[0][:, ci * 128:(ci + 1) * 128],
                             xxx[:, hf * 512:(hf + 1) * 512], start=(ci == 0), stop=(ci == 15))
                t.drain()
                t.sem_inc(sem["MM"], 1)
            C.inc("MM", 16)

        @blk.scalar
        def _(s):
            s.wait_ge(sem["MM"], C.get("MM"))
            for hf in range(2):
                s.activation(sb["mmt"][:, hf * 512:(hf + 1) * 512], ps[4 + hf][:, :], AF.Tanh)
            s.drain()
            s.sem_inc(sem["SC"], 1)
            C.inc("SC", 1)
    dump_cb("mm", sb["mmt"], 1)

    def mix_block(tgt_idx, vecname):
        with nc.Block() as blk:
            @blk.gpsimd
            def _(g):
                g.dma_start(out=wstr[0][:, :], in_=d_w2blk[:, tgt_idx * 2048:(tgt_idx + 1) * 2048]).then_inc(sem["WL"], 16)
                C.inc("WL", 16)

            @blk.tensor
            def _(t):
                t.wait_ge(sem["WL"], C.get("WL"))
                t.wait_ge(sem["SC"], C.get("SC"))
                evb = C.get("EV")
                for oc in range(16):
                    pp = ps[(oc % 2) * 2: (oc % 2) * 2 + 2]
                    if oc >= 2:
                        t.wait_ge(sem["EV"], evb + oc - 1)
                    for hf in range(2):
                        t.matmul(pp[hf][:, :], wstr[0][:, oc * 128:(oc + 1) * 128],
                                 sb["mmt"][:, hf * 512:(hf + 1) * 512], start=True, stop=True)
                    t.drain()
                    t.sem_inc(sem["MM"], 1)
                C.inc("MM", 16)

            @blk.vector
            def _(v):
                mmb = C.get("MM") - 16
                for oc in range(16):
                    pp = ps[(oc % 2) * 2: (oc % 2) * 2 + 2]
                    v.wait_ge(sem["MM"], mmb + oc + 1)
                    for hf in range(2):
                        v.scalar_tensor_tensor((Ta if hf == 0 else Tb)[:, :],
                                               pp[hf][:, :], sb[vecname][:, oc:oc + 1],
                                               blk_c(dx, oc)[:, hf * 512:(hf + 1) * 512],
                                               ALU.add, ALU.mult)
                    v.drain()
                    for hf in range(2):
                        v.tensor_tensor(blk_c(mx, oc)[:, hf * 512:(hf + 1) * 512],
                                        (Ta if hf == 0 else Tb)[:, :],
                                        blk_c(xn, oc)[:, hf * 512:(hf + 1) * 512], ALU.add)
                    v.drain()
                    v.sem_inc(sem["EV"], 1)
                C.inc("EV", 16)

    def proj_block(wdram, bias_col):
        with nc.Block() as blk:
            @blk.gpsimd
            def _(g):
                mmb = C.get("MM")
                for oc in range(16):
                    if oc >= 2:
                        g.wait_ge(sem["MM"], mmb + oc - 1)
                    strip_load(g, wstr[oc % 2], wdram[oc], s="WA" if oc % 2 == 0 else "WB")
                C.inc("WA", 16 * 8)
                C.inc("WB", 16 * 8)

            @blk.tensor
            def _(t):
                wlba = C.get("WA") - 16 * 8
                wlbb = C.get("WB") - 16 * 8
                scb = C.get("SC")
                for oc in range(16):
                    t.wait_ge(sem["WA" if oc % 2 == 0 else "WB"],
                              (wlba if oc % 2 == 0 else wlbb) + 16 * (oc // 2 + 1))
                    pp = ps[(oc % 2) * 2: (oc % 2) * 2 + 2]
                    if oc >= 2:
                        t.wait_ge(sem["SC"], scb + oc - 1)
                    for ci in range(16):
                        for hf in range(2):
                            t.matmul(pp[hf][:, :], wstr[oc % 2][:, ci * 128:(ci + 1) * 128],
                                     blk_c(mx, ci)[:, hf * 512:(hf + 1) * 512],
                                     start=(ci == 0), stop=(ci == 15))
                    t.drain()
                    t.sem_inc(sem["MM"], 1)
                C.inc("MM", 16)

            @blk.scalar
            def _(s):
                mmb = C.get("MM") - 16
                for oc in range(16):
                    pp = ps[(oc % 2) * 2: (oc % 2) * 2 + 2]
                    s.wait_ge(sem["MM"], mmb + oc + 1)
                    for hf in range(2):
                        s.activation(blk_c(pout, oc)[:, hf * 512:(hf + 1) * 512], pp[hf][:, :],
                                     AF.Identity, bias=sb[bias_col][:, oc:oc + 1])
                    s.drain()
                    s.sem_inc(sem["SC"], 1)
                C.inc("SC", 16)

    def spill_block(dst):
        with nc.Block() as blk:
            @blk.sync
            def _(sync):
                sync.dma_start(out=dst, in_=pout[:, :]).then_inc(sem["SP"], 16)
                sync.wait_ge(sem["SP"], C.inc("SP", 16))

    mix_block(3, "maa_w")
    with nc.Block() as blk:
        @blk.gpsimd
        def _(g):
            g.dma_start(out=wstr[0][:, 0:1024], in_=d_tdw1).then_inc(sem["WL"], 16)
            g.dma_start(out=wstr[1][0:64, 0:2048], in_=d_tdw2).then_inc(sem["WL"], 16)
            C.inc("WL", 32)

        @blk.tensor
        def _(t):
            t.wait_ge(sem["WL"], C.get("WL"))
            for ci in range(16):
                for hf in range(2):
                    t.matmul(ps[4 + hf][0:64, :], wstr[0][:, ci * 64:ci * 64 + 64],
                             blk_c(mx, ci)[:, hf * 512:(hf + 1) * 512],
                             start=(ci == 0), stop=(ci == 15))
            t.drain()
            t.sem_inc(sem["MM"], 1)
            C.inc("MM", 1)

        @blk.scalar
        def _(s):
            s.wait_ge(sem["MM"], C.get("MM"))
            for hf in range(2):
                s.activation(sb["tht"][0:64, hf * 512:(hf + 1) * 512], ps[4 + hf][0:64, :], AF.Tanh)
            s.drain()
            s.sem_inc(sem["SC"], 1)
            C.inc("SC", 1)

        @blk.tensor
        def _(t):
            t.wait_ge(sem["SC"], C.get("SC"))
            scb = C.get("SC")
            for oc in range(16):
                pp = ps[(oc % 2) * 2: (oc % 2) * 2 + 2]
                if oc >= 2:
                    t.wait_ge(sem["SC"], scb + oc - 1)
                for hf in range(2):
                    t.matmul(pp[hf][:, :], wstr[1][0:64, oc * 128:(oc + 1) * 128],
                             sb["tht"][0:64, hf * 512:(hf + 1) * 512], start=True, stop=True)
                t.drain()
                t.sem_inc(sem["MM"], 1)
            C.inc("MM", 16)

        @blk.scalar
        def _(s):
            mmb = C.get("MM") - 16
            evb = C.get("EV")
            for oc in range(16):
                pp = ps[(oc % 2) * 2: (oc % 2) * 2 + 2]
                s.wait_ge(sem["MM"], mmb + oc + 1)
                if oc >= 1:
                    s.wait_ge(sem["EV"], evb + oc)
                for hf in range(2):
                    s.activation((Ta if hf == 0 else Tb)[:, :], pp[hf][:, :],
                                 AF.Exp, bias=sb["tdv"][:, oc:oc + 1])
                s.drain()
                s.sem_inc(sem["SC"], 1)
            C.inc("SC", 16)

        @blk.vector
        def _(v):
            scb = C.get("SC") - 16
            for oc in range(16):
                v.wait_ge(sem["SC"], scb + oc + 1)
                for hf in range(2):
                    v.tensor_scalar_min(blk_c(pout, oc)[:, hf * 512:(hf + 1) * 512],
                                        (Ta if hf == 0 else Tb)[:, :], 5.0)
                v.drain()
                v.sem_inc(sem["EV"], 1)
            C.inc("EV", 16)
    spill_block(scr["wexp"])
    dump_cb("wexp", pout, 16)

    mix_block(2, "maa_v")
    proj_block(d_vw, "vb")
    spill_block(scr["v"])
    dump_cb("v", pout, 16)

    mix_block(1, "maa_k")
    with nc.Block() as blk:
        @blk.sync
        def _(sync):
            scb = C.get("SC")
            for oc in range(16):
                if oc >= 1:
                    sync.wait_ge(sem["SC"], scb + oc)
                sync.dma_start(out=xxx[:, :], in_=scr["wexp"][:, oc * TOK:(oc + 1) * TOK]).then_inc(sem["LD"], 16)
            C.inc("LD", 16 * 16)

        @blk.gpsimd
        def _(g):
            mmb = C.get("MM")
            for oc in range(16):
                if oc >= 2:
                    g.wait_ge(sem["MM"], mmb + oc - 1)
                strip_load(g, wstr[oc % 2], d_kw[oc], s="WA" if oc % 2 == 0 else "WB")
            C.inc("WA", 16 * 8)
            C.inc("WB", 16 * 8)

        @blk.tensor
        def _(t):
            wlba = C.get("WA") - 16 * 8
            wlbb = C.get("WB") - 16 * 8
            scb = C.get("SC")
            for oc in range(16):
                t.wait_ge(sem["WA" if oc % 2 == 0 else "WB"],
                          (wlba if oc % 2 == 0 else wlbb) + 16 * (oc // 2 + 1))
                pp = ps[(oc % 2) * 2: (oc % 2) * 2 + 2]
                if oc >= 2:
                    t.wait_ge(sem["SC"], scb + oc - 1)
                for ci in range(16):
                    for hf in range(2):
                        t.matmul(pp[hf][:, :], wstr[oc % 2][:, ci * 128:(ci + 1) * 128],
                                 blk_c(mx, ci)[:, hf * 512:(hf + 1) * 512],
                                 start=(ci == 0), stop=(ci == 15))
                t.drain()
                t.sem_inc(sem["MM"], 1)
            C.inc("MM", 16)

        @blk.scalar
        def _(s):
            mmb = C.get("MM") - 16
            ldb = C.get("LD") - 256
            evb = C.get("EV")
            for oc in range(16):
                pp = ps[(oc % 2) * 2: (oc % 2) * 2 + 2]
                s.wait_ge(sem["MM"], mmb + oc + 1)
                s.wait_ge(sem["LD"], ldb + 16 * (oc + 1))
                if oc >= 1:
                    s.wait_ge(sem["EV"], evb + oc)
                s.activation(sb["muB"][:, 0:TOK], xxx[:, :], AF.Exp, scale=-1.0)
                for hf in range(2):
                    s.activation((Ta if hf == 0 else Tb)[:, :], pp[hf][:, :],
                                 AF.Identity, bias=sb["kb"][:, oc:oc + 1])
                s.drain()
                s.activation(sb["rB"][:, 0:TOK], sb["muB"][:, 0:TOK], AF.Identity, scale=-1.0, bias=sb["cbias"][:, 2:3])
                s.drain()
                s.sem_inc(sem["SC"], 1)
            C.inc("SC", 16)

        @blk.vector
        def _(v):
            scb = C.get("SC") - 16
            for oc in range(16):
                v.wait_ge(sem["SC"], scb + oc + 1)
                for hf in range(2):
                    v.tensor_tensor(blk_c(pout, oc)[:, hf * 512:(hf + 1) * 512],
                                    (Ta if hf == 0 else Tb)[:, :],
                                    sb["rB"][:, hf * 512:(hf + 1) * 512], ALU.mult)
                v.drain()
                v.sem_inc(sem["EV"], 1)
            C.inc("EV", 16)
    spill_block(scr["km"])
    dump_cb("km", pout, 16)

    mix_block(0, "maa_r")
    proj_block(d_qw, "qb")
    spill_block(scr["q"])
    dump_cb("q", pout, 16)
    A3.close()
    A1b.close()

    if PHASES <= 2:
        return _early_out()

    A4 = contextlib.ExitStack()
    qc = [alloc(A4, "qc%d" % i, [128, 2048], BF16) for i in range(2)]
    kmc = [alloc(A4, "kmc%d" % i, [128, 2048], BF16) for i in range(2)]
    vc = [alloc(A4, "vc%d" % i, [128, 2048], BF16) for i in range(2)]
    wxc = [alloc(A4, "wxc%d" % i, [128, 2048], BF16) for i in range(2)]
    csc = [alloc(A4, "csc%d" % i, [128, 2048], F32) for i in range(2)]
    gqc = [alloc(A4, "gqc%d" % i, [128, 2048], F32) for i in range(2)]
    gqic = alloc(A4, "gqic", [128, 2048], F32)
    qtc = [alloc(A4, "qtc%d" % i, [128, 2048], BF16) for i in range(2)]
    ktc = [alloc(A4, "ktc%d" % i, [128, 2048], BF16) for i in range(2)]
    k2c = [alloc(A4, "k2c%d" % i, [128, 2048], BF16) for i in range(2)]
    qts = [alloc(A4, "qts%d" % i, [64, 2048], BF16) for i in range(2)]
    kts = [alloc(A4, "kts%d" % i, [64, 2048], BF16) for i in range(2)]
    k2tm = [alloc(A4, "k2tm%d" % i, [128, 2048], BF16) for i in range(2)]
    vtm = [alloc(A4, "vtm%d" % i, [128, 2048], BF16) for i in range(2)]
    pmt = alloc(A4, "pmt", [128, 32 * 128], BF16)
    oic = alloc(A4, "oic", [128, 2048], BF16)

    evb0 = C.get("EV"); scb0 = C.get("SC"); mmb0 = C.get("MM")
    wab0 = C.get("LA"); wbb0 = C.get("LB"); spb0 = C.get("SP")
    NV, NS, NT = 9, 1, 7

    def vec_cnt(j, stage):
        return evb0 + j * NV + stage

    def mm_cnt(j, stage):
        return mmb0 + j * NT + stage

    with nc.Block() as blk:
        @blk.sync
        def _(sync):
            spb_s = C.get("SP")

            def loads(j):
                p = j % 2
                if j >= 2:
                    sync.wait_ge(sem["MM"], mm_cnt(j - 2, 7))
                for ci in range(16):
                    co = ci * TOK + j * 128
                    for nm, dst in [("q", qc[p]), ("km", kmc[p]), ("v", vc[p]), ("wexp", wxc[p])]:
                        sync.dma_start(out=dst[:, ci * 128:(ci + 1) * 128],
                                       in_=scr[nm][:, co:co + 128]).then_inc(sem["LA" if p == 0 else "LB"], 16)

            def spill(j):
                sync.wait_ge(sem["EV"], vec_cnt(j, 9))
                sync.dma_start(out=scr["oi"][:, j * 2048:(j + 1) * 2048], in_=oic[:, :]).then_inc(sem["SP"], 16)
                sync.dma_start(out=scr["qt"][:, j * 2048:(j + 1) * 2048], in_=qtc[j % 2][:, :]).then_inc(sem["SP"], 16)

            loads(0); loads(1); loads(2)
            for j in range(3, NCH):
                spill(j - 3)
                loads(j)
            spill(NCH - 3); spill(NCH - 2); spill(NCH - 1)
            sync.wait_ge(sem["SP"], spb_s + 32 * NCH)
            C.inc("LA", 16 * 64 * 4)
            C.inc("LB", 16 * 64 * 4)
            C.inc("SP", 32 * NCH)

        @blk.scalar
        def _(s):
            for j in range(NCH):
                p = j % 2
                s.wait_ge(sem["EV"], vec_cnt(j, 1))
                s.activation(gqc[p][:, :], csc[p][:, :], AF.Exp, scale=-1.0)
                s.activation(gqic[:, :], csc[p][:, :], AF.Exp)
                s.drain()
                s.sem_inc(sem["SC"], 1)
            C.inc("SC", NCH)

        @blk.gpsimd
        def _(g):
            for j in range(NCH):
                p = j % 2
                g.wait_ge(sem["EV"], vec_cnt(j, 2))
                g.dma_start(out=qts[p][:, :], in_=qtc[p][64:128, :]).then_inc(sem["SH"], 16)
                g.dma_start(out=kts[p][:, :], in_=ktc[p][64:128, :]).then_inc(sem["SH"], 16)
            C.inc("SH", 32 * NCH)

        @blk.vector
        def _(v):
            ldb0 = C.get("LD") - 32 * NCH
            for j in range(NCH):
                p = j % 2
                wa = wab0 + (j // 2 + 1) * 16 * 64 if p == 0 else wbb0 + (j // 2 + 1) * 16 * 64
                v.wait_ge(sem["LA" if p == 0 else "LB"], wa)
                for ci in range(16):
                    v.tensor_tensor_scan(csc[p][:, ci * 128:(ci + 1) * 128],
                                         wxc[p][:, ci * 128:(ci + 1) * 128],
                                         wxc[p][:, ci * 128:(ci + 1) * 128],
                                         0.0, ALU.add, ALU.bypass)
                v.drain()
                v.sem_inc(sem["EV"], 1)
                v.wait_ge(sem["SC"], scb0 + j + 1)
                if j >= 2:
                    v.wait_ge(sem["SP"], spb0 + 32 * (j - 1))
                v.tensor_tensor(qtc[p][:, :], qc[p][:, :], gqc[p][:, :], ALU.mult)
                v.tensor_tensor(ktc[p][:, :], kmc[p][:, :], gqic[:, :], ALU.mult)
                v.drain()
                for ci in range(16):
                    v.tensor_scalar_mul(k2c[p][:, ci * 128:(ci + 1) * 128],
                                        ktc[p][:, ci * 128:(ci + 1) * 128],
                                        gqc[p][:, ci * 128 + 127:ci * 128 + 128])
                v.drain()
                v.sem_inc(sem["EV"], 1)
                for g4 in range(4):
                    v.wait_ge(sem["MM"], mm_cnt(j, g4 + 1))
                    v.tensor_copy(vtm[p][:, g4 * 512:(g4 + 1) * 512], psb[:, 0:512])
                    v.tensor_copy(k2tm[p][:, g4 * 512:(g4 + 1) * 512], psb[:, 512:1024])
                    v.drain()
                    v.sem_inc(sem["EV"], 1)
                for wave in range(2):
                    v.wait_ge(sem["MM"], mm_cnt(j, 5 + wave))
                    for bank in range(4):
                        v.tensor_tensor(pmt[:, wave * 2048 + bank * 512: wave * 2048 + (bank + 1) * 512],
                                        ps[bank][:, :], sb["mask4"][:, :], ALU.mult)
                    v.drain()
                    v.sem_inc(sem["EV"], 1)
                v.wait_ge(sem["MM"], mm_cnt(j, 7))
                if j >= 1:
                    v.wait_ge(sem["SP"], spb0 + 32 * j)
                v.tensor_copy(sb["Lsnap"][:, j * 1024:(j + 1) * 1024], sb["S_run"][:, :])
                for ci in range(16):
                    v.tensor_copy(sb["Apref"][:, j * 16 + ci:j * 16 + ci + 1],
                                  sb["A_run"][:, ci:ci + 1])
                for ci in range(16):
                    v.tensor_copy(oic[:, ci * 128:(ci + 1) * 128],
                                  ps[ci % 4][:, (ci // 4) * 128:(ci // 4) * 128 + 128])
                v.drain()
                for ci in range(16):
                    v.scalar_tensor_tensor(sb["S_run"][:, ci * 64:(ci + 1) * 64],
                                           sb["S_run"][:, ci * 64:(ci + 1) * 64],
                                           gqc[p][:, ci * 128 + 127:ci * 128 + 128],
                                           ps[4 + ci % 2][:, (ci // 2) * 64:(ci // 2) * 64 + 64],
                                           ALU.mult, ALU.add)
                    v.tensor_scalar_mul(sb["A_run"][:, ci:ci + 1], sb["A_run"][:, ci:ci + 1],
                                        gqc[p][:, ci * 128 + 127:ci * 128 + 128])
                v.drain()
                v.sem_inc(sem["EV"], 1)
            C.inc("EV", NV * NCH)

        @blk.tensor
        def _(t):
            ldb0 = C.get("SH") - 32 * NCH
            spb = C.get("SP")
            for j in range(NCH):
                p = j % 2
                t.wait_ge(sem["EV"], vec_cnt(j, 2))
                for g4 in range(4):
                    if not (j == 0 and g4 == 0):
                        prev = vec_cnt(j, 2 + g4) if g4 > 0 else vec_cnt(j - 1, 6)
                        t.wait_ge(sem["EV"], prev)
                    for sl in range(4):
                        ci = g4 * 4 + sl
                        t.transpose(psb[:, sl * 128:(sl + 1) * 128], vc[p][:, ci * 128:(ci + 1) * 128], sb["identb"][:, :])
                        ti = t.transpose(psb[:, 512 + sl * 128: 512 + (sl + 1) * 128], k2c[p][:, ci * 128:(ci + 1) * 128], sb["identb"][:, :])
                    ti.then_inc(sem["MM"], 1)
                t.wait_ge(sem["SH"], ldb0 + 32 * (j + 1))
                if j >= 1:
                    t.wait_ge(sem["EV"], vec_cnt(j - 1, 9))
                for wave in range(2):
                    if wave == 1:
                        t.wait_ge(sem["EV"], vec_cnt(j, 7))
                    for hh in range(16):
                        h = wave * 16 + hh
                        ci, off = h // 2, (h % 2) * 64
                        kt_src = ktc[p][0:64, ci * 128:(ci + 1) * 128] if off == 0 else kts[p][:, ci * 128:(ci + 1) * 128]
                        qt_src = qtc[p][0:64, ci * 128:(ci + 1) * 128] if off == 0 else qts[p][:, ci * 128:(ci + 1) * 128]
                        ti = t.matmul(ps[hh // 4][:, (hh % 4) * 128:(hh % 4 + 1) * 128],
                                      kt_src, qt_src, start=True, stop=True)
                    ti.then_inc(sem["MM"], 1)
                t.wait_ge(sem["EV"], vec_cnt(j, 8))
                for h in range(32):
                    ci, off = h // 2, (h % 2) * 64
                    t.matmul(ps[ci % 4][off:off + 64, (ci // 4) * 128:(ci // 4) * 128 + 128],
                             vtm[p][:, ci * 128 + off:ci * 128 + off + 64],
                             pmt[:, h * 128:(h + 1) * 128],
                             start=True, stop=True, tile_position=(0, off))
                for h in range(32):
                    ci, off = h // 2, (h % 2) * 64
                    ti = t.matmul(ps[4 + ci % 2][off:off + 64, (ci // 2) * 64:(ci // 2) * 64 + 64],
                                  k2tm[p][:, ci * 128 + off:ci * 128 + off + 64],
                                  vtm[p][:, ci * 128 + off:ci * 128 + off + 64],
                                  start=True, stop=True, tile_position=(0, off))
                ti.then_inc(sem["MM"], 1)
            C.inc("MM", NT * NCH)

    A4.close()
    dump_small("sspan", sb["S_run"][:, :], NCB * 64)
    dump_small("aspan", sb["A_run"][:, :], 16)

    if PHASES <= 3:
        return _early_out()

    with nc.Block() as blk:
        @blk.vector
        def _(v):
            for ci in range(16):
                v.tensor_copy(sb["aggst"][:, ci * 65:ci * 65 + 64], sb["S_run"][:, ci * 64:(ci + 1) * 64])
                v.tensor_copy(sb["aggst"][:, ci * 65 + 64:ci * 65 + 65], sb["A_run"][:, ci:ci + 1])
            v.drain()
            v.sem_inc(sem["EV"], 1)
            C.inc("EV", 1)

        @blk.sync
        def _(sync):
            sync.wait_ge(sem["EV"], C.get("EV"))
            for ci in range(16):
                sync.dma_start(out=d_agin.ap()[ci * 128:(ci + 1) * 128, :],
                               in_=sb["aggst"][:, ci * 65:(ci + 1) * 65]).then_inc(sem["SP"], 16)
            C.inc("SP", 16 * 16)

        @blk.gpsimd
        def _(g):
            g.wait_ge(sem["SP"], C.get("SP"))
            g.collective_compute("AllGather", ALU.bypass,
                                 replica_groups=[list(range(8))],
                                 ins=[d_agin.ap()], outs=[d_agout.ap()]).then_inc(sem["CC"], 1)
            C.inc("CC", 1)

        @blk.vector
        def _(v):
            v.memset(sb["S_run"][:, :], 0.0)
            v.drain()
            v.sem_inc(sem["EV"], 1)
            C.inc("EV", 1)

    for jj in range(7):
        with nc.Block() as blk:
            @blk.sync
            def _(sync):
                sync.wait_ge(sem["CC"], C.get("CC"))
                for ci in range(16):
                    sync.dma_start(out=sb["aggst"][:, ci * 65:(ci + 1) * 65],
                                   in_=d_agout.ap()[jj * 2048 + ci * 128: jj * 2048 + (ci + 1) * 128, :]).then_inc(sem["LD"], 16)
                C.inc("LD", 16 * 16)

            @blk.vector
            def _(v):
                v.wait_ge(sem["LD"], C.get("LD"))
                for ci in range(16):
                    v.tensor_scalar(sb["ALtmp"][:, ci:ci + 1],
                                    sb["aggst"][:, ci * 65 + 64:ci * 65 + 65],
                                    sb["umask"][:, jj:jj + 1],
                                    sb["umask"][:, 8 + jj:9 + jj], ALU.mult, ALU.add)
                    v.tensor_scalar_mul(sb["sfold"][:, ci * 64:(ci + 1) * 64],
                                        sb["aggst"][:, ci * 65:ci * 65 + 64],
                                        sb["umask"][:, jj:jj + 1])
                v.drain()
                for ci in range(16):
                    v.scalar_tensor_tensor(sb["S_run"][:, ci * 64:(ci + 1) * 64],
                                           sb["S_run"][:, ci * 64:(ci + 1) * 64],
                                           sb["ALtmp"][:, ci:ci + 1],
                                           sb["sfold"][:, ci * 64:(ci + 1) * 64],
                                           ALU.mult, ALU.add)
                v.drain()
                v.sem_inc(sem["EV"], 1)
                C.inc("EV", 1)
    dump_small("sin", sb["S_run"][:, :], NCB * 64)

    with nc.Block() as blk:
        @blk.vector
        def _(v):
            for j in range(NCH):
                for ci in range(16):
                    v.scalar_tensor_tensor(sb["Lsnap"][:, j * 1024 + ci * 64:j * 1024 + (ci + 1) * 64],
                                           sb["S_run"][:, ci * 64:(ci + 1) * 64],
                                           sb["Apref"][:, j * 16 + ci:j * 16 + ci + 1],
                                           sb["Lsnap"][:, j * 1024 + ci * 64:j * 1024 + (ci + 1) * 64],
                                           ALU.mult, ALU.add)
            v.drain()
            v.sem_inc(sem["EV"], 1)
            C.inc("EV", 1)

    if PHASES <= 4:
        return _early_out()

    PB = contextlib.ExitStack()
    ynb = alloc(PB, "ynb", [128, NCB * TOK], BF16)
    ocb = alloc(PB, "ocb", [128, 2048], BF16)
    osq = alloc(PB, "osq", [128, 2048], BF16)
    qtr = alloc(PB, "qtr", [128, 2048], BF16)
    oir = alloc(PB, "oir", [128, 2048], BF16)
    mu1 = alloc(PB, "mu1", [1, 128], F32)
    rs1 = alloc(PB, "rs1", [1, 128], F32)
    qtro = alloc(PB, "qtro", [64, 2048], BF16)
    scso = alloc(PB, "scso", [64, 1024], BF16)
    muB2 = alloc(PB, "muB2", [128, 128], F32)
    rB2 = alloc(PB, "rB2", [128, 128], F32)
    Tc = alloc(PB, "Tc", [128, 128], F32)

    for j in range(NCH):
        with nc.Block() as blk:
            @blk.sync
            def _(sync):
                sync.dma_start(out=qtr[:, :], in_=scr["qt"][:, j * 2048:(j + 1) * 2048]).then_inc(sem["LD"], 16)
                sync.dma_start(out=oir[:, :], in_=scr["oi"][:, j * 2048:(j + 1) * 2048]).then_inc(sem["LD"], 16)
                sync.wait_ge(sem["LD"], C.inc("LD", 32))
                sync.dma_start(out=qtro[:, :], in_=qtr[64:128, :]).then_inc(sem["LD"], 16)
                sync.dma_start(out=scso[:, :], in_=sb["Lsnap"][64:128, j * 1024:(j + 1) * 1024]).then_inc(sem["LD"], 16)
                C.inc("LD", 32)

            @blk.tensor
            def _(t):
                t.wait_ge(sem["LD"], C.get("LD"))
                for h in range(32):
                    ci, off = h // 2, (h % 2) * 64
                    ls_src = (sb["Lsnap"][0:64, j * 1024 + ci * 64:j * 1024 + (ci + 1) * 64]
                              if off == 0 else scso[:, ci * 64:(ci + 1) * 64])
                    qt_src = (qtr[0:64, ci * 128:(ci + 1) * 128]
                              if off == 0 else qtro[:, ci * 128:(ci + 1) * 128])
                    t.matmul(ps[ci % 4][off:off + 64, (ci // 4) * 128:(ci // 4) * 128 + 128],
                             ls_src, qt_src,
                             start=True, stop=True, tile_position=(0, off))
                t.drain()
                t.sem_inc(sem["MM"], 1)
                C.inc("MM", 1)

            @blk.vector
            def _(v):
                v.wait_ge(sem["MM"], C.get("MM"))
                for ci in range(16):
                    v.tensor_tensor(ocb[:, ci * 128:(ci + 1) * 128],
                                    oir[:, ci * 128:(ci + 1) * 128],
                                    ps[ci % 4][:, (ci // 4) * 128:(ci // 4) * 128 + 128], ALU.add)
                v.drain()
                v.sem_inc(sem["EV"], 1)
                C.inc("EV", 1)

            @blk.scalar
            def _(s):
                s.wait_ge(sem["EV"], C.get("EV"))
                s.activation(osq[:, :], ocb[:, :], AF.Square)
                s.drain()
                s.sem_inc(sem["SC"], 1)
                C.inc("SC", 1)

            @blk.tensor
            def _(t):
                t.wait_ge(sem["SC"], C.get("SC"))
                for ci in range(16):
                    t.matmul(ps[4][0:1, 0:128], sb["onesb"][:, :], ocb[:, ci * 128:(ci + 1) * 128],
                             start=(ci == 0), stop=(ci == 15))
                for ci in range(16):
                    t.matmul(ps[5][0:1, 0:128], sb["onesb"][:, :], osq[:, ci * 128:(ci + 1) * 128],
                             start=(ci == 0), stop=(ci == 15))
                t.drain()
                t.sem_inc(sem["MM"], 1)
                C.inc("MM", 1)

            @blk.vector
            def _(v):
                v.wait_ge(sem["MM"], C.get("MM"))
                v.tensor_scalar_mul(mu1[0:1, :], ps[4][0:1, 0:128], float(1.0 / 2048.0))
                v.tensor_scalar_mul(rs1[0:1, :], ps[5][0:1, 0:128], float(1.0 / 2048.0))
                v.drain()
                v.tensor_tensor(Tc[0:1, :], mu1[0:1, :], mu1[0:1, :], ALU.mult)
                v.drain()
                v.tensor_tensor(rs1[0:1, :], rs1[0:1, :], Tc[0:1, :], ALU.subtract)
                v.drain()
                v.sem_inc(sem["EV"], 1)
                C.inc("EV", 1)

            @blk.scalar
            def _(s):
                s.wait_ge(sem["EV"], C.get("EV"))
                s.activation(rs1[0:1, :], rs1[0:1, :], AF.Sqrt, bias=sb["cbias"][0:1, 1:2])
                s.drain()
                s.sem_inc(sem["SC"], 1)
                C.inc("SC", 1)

            @blk.vector
            def _(v):
                v.wait_ge(sem["SC"], C.get("SC"))
                v.reciprocal(rs1[0:1, :], rs1[0:1, :])
                v.drain()
                v.sem_inc(sem["EV"], 1)
                C.inc("EV", 1)

            @blk.tensor
            def _(t):
                t.wait_ge(sem["EV"], C.get("EV"))
                t.matmul(ps[6][:, 0:128], sb["onesrow"][0:1, :], mu1[0:1, :], start=True, stop=True)
                t.matmul(ps[6][:, 128:256], sb["onesrow"][0:1, :], rs1[0:1, :], start=True, stop=True)
                t.drain()
                t.sem_inc(sem["MM"], 1)
                C.inc("MM", 1)

            @blk.vector
            def _(v):
                v.wait_ge(sem["MM"], C.get("MM"))
                v.tensor_copy(muB2[:, :], ps[6][:, 0:128])
                v.tensor_copy(rB2[:, :], ps[6][:, 128:256])
                v.drain()
                scb = C.get("SC")
                for ci in range(16):
                    if ci >= 1:
                        v.wait_ge(sem["SC"], scb + ci)
                    v.tensor_tensor(Tc[:, :], ocb[:, ci * 128:(ci + 1) * 128], muB2[:, :], ALU.subtract)
                    v.drain()
                    v.tensor_tensor(Tc[:, :], Tc[:, :], rB2[:, :], ALU.mult)
                    v.drain()
                    v.sem_inc(sem["EV"], 1)
                C.inc("EV", 16)

            @blk.scalar
            def _(s):
                evb = C.get("EV") - 16
                for ci in range(16):
                    s.wait_ge(sem["EV"], evb + ci + 1)
                    s.activation(ynb[:, ci * TOK + j * 128:ci * TOK + j * 128 + 128], Tc[:, :],
                                 AF.Identity, scale=sb["lng"][:, ci:ci + 1], bias=sb["lnb"][:, ci:ci + 1])
                    s.drain()
                    s.sem_inc(sem["SC"], 1)
                C.inc("SC", 16)
    dump_cb("yn", ynb, 16)

    OW = contextlib.ExitStack()
    wstr2 = [alloc(OW, "wstr2_%d" % i, [128, 2048], BF16) for i in range(2)]
    sbuf_s = alloc(OW, "sbuf_s", [128, NCB * TOK], BF16)
    with nc.Block() as blk:
        @blk.gpsimd
        def _(g):
            mmb = C.get("MM")
            for oc in range(16):
                if oc >= 2:
                    g.wait_ge(sem["MM"], mmb + oc - 1)
                strip_load(g, wstr2[oc % 2], d_ow[oc], s="WA" if oc % 2 == 0 else "WB")
            C.inc("WA", 16 * 8)
            C.inc("WB", 16 * 8)

        @blk.tensor
        def _(t):
            wlba = C.get("WA") - 16 * 8
            wlbb = C.get("WB") - 16 * 8
            evb = C.get("EV")
            for oc in range(16):
                t.wait_ge(sem["WA" if oc % 2 == 0 else "WB"],
                          (wlba if oc % 2 == 0 else wlbb) + 16 * (oc // 2 + 1))
                pp = ps[(oc % 2) * 2: (oc % 2) * 2 + 2]
                if oc >= 2:
                    t.wait_ge(sem["EV"], evb + oc - 1)
                for ci in range(16):
                    for hf in range(2):
                        t.matmul(pp[hf][:, :], wstr2[oc % 2][:, ci * 128:(ci + 1) * 128],
                                 blk_c(ynb, ci)[:, hf * 512:(hf + 1) * 512],
                                 start=(ci == 0), stop=(ci == 15))
                t.drain()
                t.sem_inc(sem["MM"], 1)
            C.inc("MM", 16)

        @blk.vector
        def _(v):
            mmb = C.get("MM") - 16
            for oc in range(16):
                v.wait_ge(sem["MM"], mmb + oc + 1)
                pp = ps[(oc % 2) * 2: (oc % 2) * 2 + 2]
                for hf in range(2):
                    v.tensor_copy(blk_c(sbuf_s, oc)[:, hf * 512:(hf + 1) * 512], pp[hf][:, :])
                v.drain()
                v.sem_inc(sem["EV"], 1)
            C.inc("EV", 16)

        @blk.sync
        def _(sync):
            sync.wait_ge(sem["EV"], C.get("EV"))
            sync.dma_start(out=scr["s"], in_=sbuf_s[:, :]).then_inc(sem["SP"], 16)
            sync.wait_ge(sem["SP"], C.inc("SP", 16))
    dump_cb("s", sbuf_s, 16)
    OW.close()
    PB.close()

    if PHASES <= 5:
        return _early_out()

    LS.close()

    PC = contextlib.ExitStack()
    zb = alloc(PC, "zb", [128, 64 * TOK], BF16)
    UG = contextlib.ExitStack()
    x1b = alloc(UG, "x1b", [128, NCB * TOK], BF16)
    alloc(UG, "Hdp", [128, 128], BF16)
    wstr3 = [alloc(UG, "wstr3_%d" % i, [128, 2048], BF16) for i in range(4)]
    hb = alloc(UG, "hb", [128, TOK], BF16)
    sgb = alloc(UG, "sgb", [128, TOK], F32)

    with nc.Block() as blk:
        @blk.sync
        def _(sync):
            sync.dma_start(out=x1b[:, :], in_=scr["x1"]).then_inc(sem["LD"], 16)
            C.inc("LD", 16)

        @blk.gpsimd
        def _(g):
            mmb = C.get("MM")
            for ic in range(64):
                if ic >= 2:
                    g.wait_ge(sem["MM"], mmb + 2 * (ic - 1))
                sname = "WA" if ic % 2 == 0 else "WB"
                strip_load(g, wstr3[(ic % 2) * 2], d_wup[ic], s=sname)
                strip_load(g, wstr3[(ic % 2) * 2 + 1], d_wg[ic], s=sname)
            C.inc("WA", 32 * 32)
            C.inc("WB", 32 * 32)

        @blk.tensor
        def _(t):
            t.wait_ge(sem["LD"], C.get("LD"))
            wlba = C.get("WA") - 32 * 32
            wlbb = C.get("WB") - 32 * 32
            evb = C.get("EV")
            scb = C.get("SC")
            for ic in range(64):
                t.wait_ge(sem["WA" if ic % 2 == 0 else "WB"],
                          (wlba if ic % 2 == 0 else wlbb) + 32 * (ic // 2 + 1))
                wu, wg = wstr3[(ic % 2) * 2], wstr3[(ic % 2) * 2 + 1]
                if ic >= 1:
                    t.wait_ge(sem["EV"], evb + 2 * ic - 1)
                    t.wait_ge(sem["SC"], scb + ic)
                for ci in range(16):
                    for hf in range(2):
                        t.matmul(ps[hf][:, :], wu[:, ci * 128:(ci + 1) * 128],
                                 blk_c(x1b, ci)[:, hf * 512:(hf + 1) * 512],
                                 start=(ci == 0), stop=(ci == 15))
                        t.matmul(ps[2 + hf][:, :], wg[:, ci * 128:(ci + 1) * 128],
                                 blk_c(x1b, ci)[:, hf * 512:(hf + 1) * 512],
                                 start=(ci == 0), stop=(ci == 15))
                t.drain()
                t.sem_inc(sem["MM"], 1)
                t.wait_ge(sem["EV"], evb + 2 * ic + 1)
                for hf in range(2):
                    t.matmul(ps[4 + hf][:, :], sb["Hdp"][:, :], hb[:, hf * 512:(hf + 1) * 512],
                             start=True, stop=True)
                t.drain()
                t.sem_inc(sem["MM"], 1)
            C.inc("MM", 128)

        @blk.scalar
        def _(s):
            mmb = C.get("MM") - 128
            evb = C.get("EV")
            for ic in range(64):
                s.wait_ge(sem["MM"], mmb + 2 * ic + 1)
                if ic >= 1:
                    s.wait_ge(sem["EV"], evb + 2 * ic - 1)
                for hf in range(2):
                    s.activation(sgb[:, hf * 512:(hf + 1) * 512], ps[2 + hf][:, :],
                                 AF.Silu, scale=sb["s_g"][:, ic:ic + 1])
                s.drain()
                s.sem_inc(sem["SC"], 1)
            C.inc("SC", 64)

        @blk.vector
        def _(v):
            mmb = C.get("MM") - 128
            scb = C.get("SC") - 64
            for ic in range(64):
                v.wait_ge(sem["SC"], scb + ic + 1)
                v.tensor_scalar_mul(sb["Hdp"][:, :], sb["H128b"][:, :], sb["sudp"][:, ic:ic + 1])
                for hf in range(2):
                    v.scalar_tensor_tensor(hb[:, hf * 512:(hf + 1) * 512], ps[hf][:, :],
                                           sb["s_up"][:, ic:ic + 1],
                                           sgb[:, hf * 512:(hf + 1) * 512], ALU.mult, ALU.mult)
                v.drain()
                v.sem_inc(sem["EV"], 1)
                v.wait_ge(sem["MM"], mmb + 2 * ic + 2)
                for hf in range(2):
                    v.tensor_copy(zb[:, ic * TOK + hf * 512: ic * TOK + (hf + 1) * 512], ps[4 + hf][:, :])
                v.drain()
                v.sem_inc(sem["EV"], 1)
            C.inc("EV", 128)

    BF = contextlib.ExitStack()
    ztp = [alloc(BF, "ztp%d" % i, [128, TOK], BF16) for i in range(8)]
    with nc.Block() as blk:
        @blk.vector
        def _(v):
            for d in [1, 2, 4, 8, 16, 32]:
                pairs = [i for i in range(64) if not (i & d)]
                for gs in range(0, 32, 8):
                    grp = pairs[gs:gs + 8]
                    for kk, i in enumerate(grp):
                        v.tensor_tensor(ztp[kk][:, :], zb[:, i * TOK:(i + 1) * TOK],
                                        zb[:, (i + d) * TOK:(i + d + 1) * TOK], ALU.add)
                    v.drain()
                    for i in grp:
                        jb = i + d
                        v.tensor_tensor(zb[:, jb * TOK:(jb + 1) * TOK],
                                        zb[:, i * TOK:(i + 1) * TOK],
                                        zb[:, jb * TOK:(jb + 1) * TOK], ALU.subtract)
                    v.drain()
                    for kk, i in enumerate(grp):
                        v.tensor_copy(zb[:, i * TOK:(i + 1) * TOK], ztp[kk][:, :])
                    v.drain()
            v.sem_inc(sem["EV"], 1)
            C.inc("EV", 1)
    BF.close()
    if DEBUG:
        with nc.Block() as blk:
            @blk.scalar
            def _(s):
                spb = C.get("SP")
                for ib in range(64):
                    if ib:
                        s.wait_ge(sem["SP"], spb + 16 * ib)
                    s.activation(sb["dbuf"][:, :], zb[:, ib * TOK:(ib + 1) * TOK], AF.Copy)
                    s.drain()
                    s.sem_inc(sem["SC"], 1)
                C.inc("SC", 64)

            @blk.sync
            def _(sync):
                scb = C.get("SC") - 64
                for ib in range(64):
                    sync.wait_ge(sem["SC"], scb + ib + 1)
                    sync.dma_start(out=dbg["z"][:, ib * TOK:(ib + 1) * TOK],
                                   in_=sb["dbuf"][:, :]).then_inc(sem["SP"], 16)
                sync.wait_ge(sem["SP"], C.inc("SP", 16 * 64))
    UG.close()

    DN = contextlib.ExitStack()
    gcomb = alloc(DN, "gcomb", [128, TOK], F32)
    srel = alloc(DN, "srel", [128, TOK], BF16)
    xrel = alloc(DN, "xrel", [128, TOK], F32)
    outb = alloc(DN, "outb", [128, TOK], F32)
    wstr4 = [alloc(DN, "wstr4_%d" % i, [128, 64 * 128], BF16) for i in range(2)]

    with nc.Block() as blk:
        @blk.gpsimd
        def _(g):
            mmb = C.get("MM")
            for fc in range(16):
                if fc >= 2:
                    g.wait_ge(sem["MM"], mmb + fc - 1)
                strip_load(g, wstr4[fc % 2], d_wd[fc], ntile=64, s="WA" if fc % 2 == 0 else "WB")
            C.inc("WA", 16 * 8)
            C.inc("WB", 16 * 8)

        @blk.sync
        def _(sync):
            evb = C.get("EV")
            for fc in range(16):
                if fc >= 1:
                    sync.wait_ge(sem["EV"], evb + 2 * fc)
                sync.dma_start(out=srel[:, :], in_=scr["s"][:, fc * TOK:(fc + 1) * TOK]).then_inc(sem["LD"], 16)
                sync.dma_start(out=xrel[:, :], in_=d_xT[fc * 128:(fc + 1) * 128, :]).then_inc(sem["LD"], 16)
            C.inc("LD", 32 * 16)

        @blk.tensor
        def _(t):
            wlba = C.get("WA") - 16 * 8
            wlbb = C.get("WB") - 16 * 8
            scb = C.get("SC")
            for fc in range(16):
                t.wait_ge(sem["WA" if fc % 2 == 0 else "WB"],
                          (wlba if fc % 2 == 0 else wlbb) + 16 * (fc // 2 + 1))
                pp = ps[(fc % 2) * 2: (fc % 2) * 2 + 2]
                if fc >= 2:
                    t.wait_ge(sem["SC"], scb + fc - 1)
                for cj in range(64):
                    for hf in range(2):
                        t.matmul(pp[hf][:, :], wstr4[fc % 2][:, cj * 128:(cj + 1) * 128],
                                 zb[:, cj * TOK + hf * 512: cj * TOK + (hf + 1) * 512],
                                 start=(cj == 0), stop=(cj == 63))
                t.drain()
                t.sem_inc(sem["MM"], 1)
            C.inc("MM", 16)

        @blk.scalar
        def _(s):
            mmb = C.get("MM") - 16
            evb = C.get("EV")
            for fc in range(16):
                s.wait_ge(sem["MM"], mmb + fc + 1)
                if fc >= 1:
                    s.wait_ge(sem["EV"], evb + 2 * fc - 1)
                pp = ps[(fc % 2) * 2: (fc % 2) * 2 + 2]
                for hf in range(2):
                    s.activation(gcomb[:, hf * 512:(hf + 1) * 512], pp[hf][:, :],
                                 AF.Silu, scale=sb["s_dp"][:, fc:fc + 1])
                s.drain()
                s.sem_inc(sem["SC"], 1)
                s.wait_ge(sem["EV"], evb + 2 * fc + 2)
                s.dma_start(out=d_out[fc * 128:(fc + 1) * 128, :], in_=outb[:, :]).then_inc(sem["SP"], 16)
            C.inc("SC", 16)
            s.wait_ge(sem["SP"], C.inc("SP", 16 * 16))

        @blk.vector
        def _(v):
            scb = C.get("SC") - 16
            ldb = C.get("LD") - 32 * 16
            spb = C.get("SP") - 16 * 16
            for fc in range(16):
                v.wait_ge(sem["SC"], scb + fc + 1)
                v.wait_ge(sem["LD"], ldb + 32 * (fc + 1))
                if fc >= 1:
                    v.wait_ge(sem["SP"], spb + 16 * fc)
                v.tensor_tensor(outb[:, :], gcomb[:, :], srel[:, :], ALU.mult)
                v.drain()
                v.sem_inc(sem["EV"], 1)
                v.tensor_tensor(outb[:, :], outb[:, :], xrel[:, :], ALU.add)
                v.drain()
                v.sem_inc(sem["EV"], 1)
            C.inc("EV", 32)

    DN.close()
    PC.close()
    est.close()
    return nc


_NC_CACHE = {}


def _stage(inputs):
    x = np.asarray(inputs["L0_attn_x"], dtype=np.float32)

    def colvec(v):
        return np.ascontiguousarray(np.asarray(v, np.float32).reshape(16, 128).T)

    H128 = _hadamard(128).astype(np.float32)
    maskT = np.triu(np.ones((128, 128), np.float32))
    mask4 = np.ascontiguousarray(np.concatenate([maskT] * 4, axis=1))

    com = {
        "c_maa_x": colvec(inputs["L0_attn_maa_x"]),
        "c_maa_r": colvec(inputs["L0_attn_maa_r"]),
        "c_maa_k": colvec(inputs["L0_attn_maa_k"]),
        "c_maa_v": colvec(inputs["L0_attn_maa_v"]),
        "c_maa_w": colvec(inputs["L0_attn_maa_w"]),
        "c_tdv": colvec(inputs["L0_attn_time_decay"]),
        "c_qb": colvec(inputs["L0_attn_q_b"]),
        "c_kb": colvec(inputs["L0_attn_k_b"]),
        "c_vb": colvec(inputs["L0_attn_v_b"]),
        "c_lng": colvec(inputs["L0_attn_ln_g"]),
        "c_lnb": colvec(inputs["L0_attn_ln_b"]),
        "c_su": colvec(inputs["L0_ffn_SU_ug"]),
        "c_sudp": np.ascontiguousarray(np.asarray(inputs["L0_ffn_SU_dp"], np.float32).reshape(64, 128).T),
        "c_wscu": np.ascontiguousarray(np.asarray(inputs["L0_ffn_Wscale_ug"], np.float32)[:8192].reshape(64, 128).T),
        "c_wscg": np.ascontiguousarray(np.asarray(inputs["L0_ffn_Wscale_ug"], np.float32)[8192:].reshape(64, 128).T),
        "c_wsdp": colvec(inputs["L0_ffn_Wscale_dp"]),
        "c_H128": H128,
        "c_ident": np.eye(128, dtype=np.float32),
        "c_mask4": mask4,
        "c_onescol": np.ones((128, 1), np.float32),
        "c_onesrow": np.ones((1, 128), np.float32),
        "w_maaw1": np.ascontiguousarray(np.asarray(inputs["L0_attn_maa_w1"], np.float32).reshape(16, 128, 128).transpose(1, 0, 2).reshape(128, 2048)),
        "w_tdw1": np.ascontiguousarray(np.asarray(inputs["L0_attn_td_w1"], np.float32).reshape(16, 128, 64).transpose(1, 0, 2).reshape(128, 1024)),
        "w_tdw2": np.ascontiguousarray(np.asarray(inputs["L0_attn_td_w2"], np.float32)),
    }
    w2 = np.asarray(inputs["L0_attn_maa_w2"], np.float32)
    W2blk = np.zeros((128, 4 * 2048), np.float32)
    for i in range(4):
        W2blk[32 * i:32 * i + 32, 2048 * i:2048 * (i + 1)] = w2[i]
    com["w_w2blk"] = np.ascontiguousarray(W2blk)

    def tile4(w):
        return np.ascontiguousarray(
            np.asarray(w, np.float32).reshape(16, 128, 16, 128).transpose(2, 1, 0, 3).reshape(16, 128, 2048))

    com["w_qw"] = tile4(inputs["L0_attn_q_w"])
    com["w_kw"] = tile4(inputs["L0_attn_k_w"])
    com["w_vw"] = tile4(inputs["L0_attn_v_w"])
    com["w_ow"] = tile4(inputs["L0_attn_o_w"])
    wupT = np.asarray(inputs["L0_ffn_W_up"], np.float32).T
    wgT = np.asarray(inputs["L0_ffn_W_gate"], np.float32).T
    com["w_wup"] = np.ascontiguousarray(wupT.reshape(16, 128, 64, 128).transpose(2, 1, 0, 3).reshape(64, 128, 2048))
    com["w_wg"] = np.ascontiguousarray(wgT.reshape(16, 128, 64, 128).transpose(2, 1, 0, 3).reshape(64, 128, 2048))
    wdT = np.asarray(inputs["L0_ffn_W_down"], np.float32).T
    com["w_wd"] = np.ascontiguousarray(wdT.reshape(64, 128, 16, 128).transpose(2, 1, 0, 3).reshape(16, 128, 8192))

    in_maps = []
    for core in range(8):
        b, qt = core // 4, core % 4
        t0 = qt * TOK
        m = dict(com)
        m["xT"] = np.ascontiguousarray(x[b, t0:t0 + TOK, :].T)
        xp = np.zeros((2048,), np.float32) if t0 == 0 else np.asarray(x[b, t0 - 1, :], np.float32)
        m["xp0"] = np.ascontiguousarray(xp.reshape(1, 2048))
        m["xp0c"] = colvec(xp)
        um = np.zeros((128, 16), np.float32)
        for j in range(7):
            pred = (j // 4 == b) and (j % 4 < qt)
            um[:, j] = 1.0 if pred else 0.0
            um[:, 8 + j] = 0.0 if pred else 1.0
        m["i_umask"] = um
        in_maps.append(m)
    return in_maps


def kernel(**inputs):
    if "nc" not in _NC_CACHE:
        _NC_CACHE["nc"] = build_nc()
    nc = _NC_CACHE["nc"]
    in_maps = _stage(inputs)
    res = run_bass_kernel_spmd(nc, in_maps, core_ids=list(range(8)))
    out = np.zeros((2, 4096, 2048), np.float32)
    for core in range(8):
        b, qt = core // 4, core % 4
        out[b, qt * TOK:(qt + 1) * TOK, :] = res.results[core]["o_out"].T
    if DEBUG:
        kernel._dbg = res.results
    return out




# version 07
# speedup vs baseline: 1.1931x; 1.0365x over previous
import contextlib
import numpy as np
import concourse.bass as bass
import concourse.mybir as mybir
from concourse.bass_utils import run_bass_kernel_spmd

F32 = mybir.dt.float32
BF16 = mybir.dt.bfloat16
AF = mybir.ActivationFunctionType
ALU = mybir.AluOpType
AXX = mybir.AxisListType.X

TOK = 1024
NCB = 16
L = 128
NCH = 8
DEBUG = False
PHASES = 6
_SKIP_ODD_PT = False
A4SUB = 7


def _hadamard(n):
    h = np.array([[1.0]], dtype=np.float64)
    while h.shape[0] < n:
        h = np.block([[h, h], [h, -h]])
    return h


class Counter:
    def __init__(self):
        self.v = {}

    def inc(self, name, by):
        self.v[name] = self.v.get(name, 0) + by
        return self.v[name]

    def get(self, name):
        return self.v.get(name, 0)


def build_nc():
    nc = bass.Bass("TRN2", target_bir_lowering=False)
    C = Counter()

    d_xT = nc.dram_tensor("xT", [2048, TOK], F32, kind="ExternalInput").ap()
    d_xp0 = nc.dram_tensor("xp0", [1, 2048], F32, kind="ExternalInput").ap()
    d_xp0c = nc.dram_tensor("xp0c", [128, 16], F32, kind="ExternalInput").ap()
    d_umask = nc.dram_tensor("i_umask", [128, 16], F32, kind="ExternalInput").ap()
    vec_names = ["maa_x", "maa_r", "maa_k", "maa_v", "maa_w", "tdv",
                 "qb", "kb", "vb", "lng", "lnb"]
    d_vec = {n: nc.dram_tensor("c_" + n, [128, 16], F32, kind="ExternalInput").ap()
             for n in vec_names}
    d_su = nc.dram_tensor("c_su", [128, 16], F32, kind="ExternalInput").ap()
    d_sudp = nc.dram_tensor("c_sudp", [128, 64], F32, kind="ExternalInput").ap()
    d_wscu = nc.dram_tensor("c_wscu", [128, 64], F32, kind="ExternalInput").ap()
    d_wscg = nc.dram_tensor("c_wscg", [128, 64], F32, kind="ExternalInput").ap()
    d_wsdp = nc.dram_tensor("c_wsdp", [128, 16], F32, kind="ExternalInput").ap()
    d_H128 = nc.dram_tensor("c_H128", [128, 128], F32, kind="ExternalInput").ap()
    d_ident = nc.dram_tensor("c_ident", [128, 128], F32, kind="ExternalInput").ap()
    d_mask4 = nc.dram_tensor("c_mask4", [128, 512], F32, kind="ExternalInput").ap()
    d_ones_col = nc.dram_tensor("c_onescol", [128, 1], F32, kind="ExternalInput").ap()
    d_ones_row = nc.dram_tensor("c_onesrow", [1, 128], F32, kind="ExternalInput").ap()
    d_maaw1 = nc.dram_tensor("w_maaw1", [128, 2048], F32, kind="ExternalInput").ap()
    d_w2blk = nc.dram_tensor("w_w2blk", [128, 8192], F32, kind="ExternalInput").ap()
    d_tdw1 = nc.dram_tensor("w_tdw1", [128, 1024], F32, kind="ExternalInput").ap()
    d_tdw2 = nc.dram_tensor("w_tdw2", [64, 2048], F32, kind="ExternalInput").ap()
    d_qw = nc.dram_tensor("w_qw", [16, 128, 2048], F32, kind="ExternalInput").ap()
    d_kw = nc.dram_tensor("w_kw", [16, 128, 2048], F32, kind="ExternalInput").ap()
    d_vw = nc.dram_tensor("w_vw", [16, 128, 2048], F32, kind="ExternalInput").ap()
    d_ow = nc.dram_tensor("w_ow", [16, 128, 2048], F32, kind="ExternalInput").ap()
    d_wup = nc.dram_tensor("w_wup", [64, 128, 2048], F32, kind="ExternalInput").ap()
    d_wg = nc.dram_tensor("w_wg", [64, 128, 2048], F32, kind="ExternalInput").ap()
    d_wd = nc.dram_tensor("w_wd", [16, 128, 8192], F32, kind="ExternalInput").ap()

    d_out = nc.dram_tensor("o_out", [2048, TOK], F32, kind="ExternalOutput").ap()
    scr = {}
    for n in ["x1", "wexp", "q", "km", "v", "s"]:
        scr[n] = nc.dram_tensor("scr_" + n, [128, NCB * TOK], BF16).ap()
    scr["qt"] = nc.dram_tensor("scr_qt", [128, NCH * 2048], BF16).ap()
    scr["oi"] = nc.dram_tensor("scr_oi", [128, NCH * 2048], BF16).ap()
    d_agin = nc.dram_tensor("ag_in", [2048, 65], F32)
    d_agout = nc.dram_tensor("ag_out", [8 * 2048, 65], F32)

    dbg = {}
    if DEBUG:
        for n, w in [("xn", NCB * TOK), ("dx", NCB * TOK), ("x1", NCB * TOK),
                     ("wexp", NCB * TOK), ("q", NCB * TOK), ("km", NCB * TOK),
                     ("v", NCB * TOK), ("sspan", NCB * 64), ("aspan", 16),
                     ("sin", NCB * 64), ("yn", NCB * TOK), ("s", NCB * TOK),
                     ("z", 64 * TOK), ("mm", TOK)]:
            dbg[n] = nc.dram_tensor("dbg_" + n, [128, w], BF16, kind="ExternalOutput").ap()

    est = contextlib.ExitStack()
    sem = {n: est.enter_context(nc.semaphore(n))
           for n in ["LD", "WL", "WA", "WB", "LA", "LB", "SH", "MM", "EV", "SC", "SP", "CC"]}

    sb = {}

    def alloc(stk, name, shape, dtype):
        t = stk.enter_context(nc.sbuf_tensor(name, shape, dtype))
        sb[name] = t
        return t

    for n in vec_names:
        alloc(est, n, [128, 16], F32)
    for n, sh, dt in [("umask", [128, 16], F32), ("su", [128, 16], F32),
                      ("sudp", [128, 64], F32), ("s_up", [128, 64], F32),
                      ("s_g", [128, 64], F32), ("s_dp", [128, 16], F32),
                      ("H128b", [128, 128], BF16), ("identb", [128, 128], BF16),
                      ("mask4", [128, 512], F32), ("onesb", [128, 1], BF16),
                      ("onesrow", [1, 128], F32),
                      ("xp0col", [128, 16], F32), ("xp0sq", [128, 16], F32),
                      ("pst", [1, 8], F32), ("p128", [128, 2], F32),
                      ("onesf", [128, 1], F32),
                      ("cbias", [128, 4], F32)]:
        alloc(est, n, sh, dt)
    if DEBUG:
        alloc(est, "dbuf", [128, TOK], BF16)
    LS = contextlib.ExitStack()
    for n, sh, dt in [("sfold", [128, 1024], F32),
                      ("S_run", [128, NCB * 64], F32), ("A_run", [128, 16], F32),
                      ("Apref", [128, 128], F32), ("Lsnap", [128, NCH * NCB * 64], BF16),
                      ("ALtmp", [128, 16], F32), ("aggst", [128, 16 * 65], F32)]:
        alloc(LS, n, sh, dt)

    ps = [est.enter_context(nc.psum_tensor("ps%d" % i, [128, 512], F32)) for i in range(7)]
    psb = est.enter_context(nc.psum_tensor("psb", [128, 1024], BF16))

    def blk_c(t, ci):
        return t[:, ci * TOK:(ci + 1) * TOK]

    def strip_load(g, dst, src_oc, ntile=16, s="WL"):
        g.dma_start(out=dst[:, 0:ntile * 128], in_=src_oc).then_inc(sem[s], 16)
        return 16

    def dump_cb(name, src, nblk):
        if not DEBUG:
            return
        with nc.Block() as blk:
            @blk.scalar
            def _(s):
                base_sp = C.get("SP")
                for ci in range(nblk):
                    if ci:
                        s.wait_ge(sem["SP"], base_sp + 16 * ci)
                    s.activation(sb["dbuf"][:, :], src[:, ci * TOK:(ci + 1) * TOK], AF.Copy)
                    s.drain()
                    s.sem_inc(sem["SC"], 1)
                C.inc("SC", nblk)

            @blk.sync
            def _(sync):
                base_sc = C.get("SC") - nblk
                for ci in range(nblk):
                    sync.wait_ge(sem["SC"], base_sc + ci + 1)
                    sync.dma_start(out=dbg[name][:, ci * TOK:(ci + 1) * TOK],
                                   in_=sb["dbuf"][:, :]).then_inc(sem["SP"], 16)
                sync.wait_ge(sem["SP"], C.inc("SP", 16 * nblk))

    def dump_small(name, src, width):
        if not DEBUG:
            return
        with nc.Block() as blk:
            @blk.vector
            def _(v):
                v.tensor_copy(sb["dbuf"][:, 0:width], src)
                v.drain()
                v.sem_inc(sem["EV"], 1)
                C.inc("EV", 1)

            @blk.sync
            def _(sync):
                sync.wait_ge(sem["EV"], C.get("EV"))
                sync.dma_start(out=dbg[name][:, 0:width], in_=sb["dbuf"][:, 0:width]).then_inc(sem["SP"], 16)
                sync.wait_ge(sem["SP"], C.inc("SP", 16))

    with nc.Block() as blk:
        @blk.sync
        def _(sync):
            n_loads = 0
            for n in vec_names:
                sync.dma_start(out=sb[n][:, :], in_=d_vec[n]).then_inc(sem["LD"], 16)
                n_loads += 1
            for n, d in [("umask", d_umask), ("su", d_su), ("sudp", d_sudp),
                         ("s_up", d_wscu), ("s_g", d_wscg), ("s_dp", d_wsdp),
                         ("mask4", d_mask4), ("onesrow", d_ones_row),
                         ("onesf", d_ones_col), ("xp0col", d_xp0c)]:
                sync.dma_start(out=sb[n][:, :], in_=d).then_inc(sem["LD"], 16)
                n_loads += 1
            C.inc("LD", 16 * n_loads)

        @blk.gpsimd
        def _(g):
            g.dma_start(out=sb["H128b"][:, :], in_=d_H128).then_inc(sem["WL"], 16)
            g.dma_start(out=sb["identb"][:, :], in_=d_ident).then_inc(sem["WL"], 16)
            g.dma_start(out=sb["onesb"][:, :], in_=d_ones_col).then_inc(sem["WL"], 16)
            C.inc("WL", 48)

        @blk.vector
        def _(v):
            v.wait_ge(sem["LD"], C.get("LD"))
            v.wait_ge(sem["WL"], C.get("WL"))
            v.tensor_scalar_mul(sb["s_up"][:, :], sb["s_up"][:, :], float(1.0 / np.sqrt(2048.0)))
            v.tensor_scalar_mul(sb["s_g"][:, :], sb["s_g"][:, :], float(1.0 / np.sqrt(2048.0)))
            v.tensor_scalar_mul(sb["s_dp"][:, :], sb["s_dp"][:, :], float(1.0 / np.sqrt(8192.0)))
            v.memset(sb["S_run"][:, :], 0.0)
            v.memset(sb["A_run"][:, :], 1.0)
            v.memset(sb["cbias"][:, 0:1], 1e-5)
            v.memset(sb["cbias"][:, 1:2], 64e-5)
            v.memset(sb["cbias"][:, 2:3], 1.0)
            v.drain()
            v.sem_inc(sem["EV"], 1)
            C.inc("EV", 1)


    def _early_out(close_a1b=False):
        with nc.Block() as blk:
            @blk.sync
            def _(sync):
                sync.dma_start(out=d_out[0:128, :], in_=sb["S_run"][:, :]).then_inc(sem["SP"], 16)
                sync.wait_ge(sem["SP"], C.inc("SP", 16))
        if close_a1b:
            A1b.close()
        LS.close()
        est.close()
        return nc

    A1b = contextlib.ExitStack()
    A1a = contextlib.ExitStack()
    xn = alloc(A1b, "xn", [128, NCB * TOK], BF16)
    dx = alloc(A1b, "dx", [128, NCB * TOK], BF16)
    alloc(A1b, "muB", [128, TOK], F32)
    alloc(A1b, "rB", [128, TOK], F32)
    xbuf = alloc(A1a, "xbuf", [128, NCB * TOK], F32)
    alloc(A1a, "statrow", [1, TOK], F32)
    alloc(A1a, "statrow2", [1, TOK], F32)
    alloc(A1a, "T1", [128, TOK], F32)
    alloc(A1a, "T2", [128, TOK], F32)
    xbb = xn

    with nc.Block() as blk:
        @blk.sync
        def _(sync):
            for ci in range(16):
                sync.dma_start(out=blk_c(xbuf, ci), in_=d_xT[ci * 128:(ci + 1) * 128, :]).then_inc(sem["LD"], 16)
            C.inc("LD", 16 * 16)

        @blk.scalar
        def _(s):
            s.wait_ge(sem["LD"], C.get("LD"))
            for ci in range(16):
                s.activation(blk_c(xbb, ci), blk_c(xbuf, ci), AF.Copy)
            s.activation(sb["xp0sq"][:, :], sb["xp0col"][:, :], AF.Square)
            s.drain()
            s.sem_inc(sem["SC"], 1)
            C.inc("SC", 1)

        @blk.tensor
        def _(t):
            t.wait_ge(sem["SC"], C.get("SC"))
            for hf in range(2):
                for ci in range(16):
                    t.matmul(ps[0 + hf][0:1, :], sb["onesb"][:, :],
                             blk_c(xbb, ci)[:, hf * 512:(hf + 1) * 512],
                             start=(ci == 0), stop=(ci == 15))
            t.matmul(ps[4][0:1, 0:16], sb["onesf"][:, :], sb["xp0col"][:, :], start=True, stop=True)
            t.matmul(ps[4][0:1, 16:32], sb["onesf"][:, :], sb["xp0sq"][:, :], start=True, stop=True)
            t.drain()
            t.sem_inc(sem["MM"], 1)
            C.inc("MM", 1)

        @blk.scalar
        def _(s):
            s.wait_ge(sem["MM"], C.get("MM"))
            for ci in range(16):
                s.activation(blk_c(xbb, ci), blk_c(xbuf, ci), AF.Square)
            s.drain()
            s.sem_inc(sem["SC"], 1)
            C.inc("SC", 1)

        @blk.tensor
        def _(t):
            t.wait_ge(sem["SC"], C.get("SC"))
            for hf in range(2):
                for ci in range(16):
                    t.matmul(ps[2 + hf][0:1, :], sb["onesb"][:, :],
                             blk_c(xbb, ci)[:, hf * 512:(hf + 1) * 512],
                             start=(ci == 0), stop=(ci == 15))
            t.drain()
            t.sem_inc(sem["MM"], 1)
            C.inc("MM", 1)

        @blk.vector
        def _(v):
            v.wait_ge(sem["MM"], C.get("MM"))
            for hf in range(2):
                sl = slice(hf * 512, (hf + 1) * 512)
                v.tensor_scalar_mul(sb["statrow"][0:1, sl], ps[0 + hf][0:1, :], float(1.0 / 2048.0))
                v.tensor_scalar_mul(sb["statrow2"][0:1, sl], ps[2 + hf][0:1, :], float(1.0 / 2048.0))
            v.reduce_sum(sb["pst"][0:1, 0:1], ps[4][0:1, 0:16], axis=AXX)
            v.reduce_sum(sb["pst"][0:1, 1:2], ps[4][0:1, 16:32], axis=AXX)
            v.drain()
            v.tensor_tensor(sb["T1"][0:1, 0:TOK], sb["statrow"][0:1, :], sb["statrow"][0:1, :], ALU.mult)
            v.tensor_scalar_mul(sb["pst"][0:1, 0:2], sb["pst"][0:1, 0:2], float(1.0 / 2048.0))
            v.drain()
            v.tensor_tensor(sb["statrow2"][0:1, :], sb["statrow2"][0:1, :], sb["T1"][0:1, 0:TOK], ALU.subtract)
            v.tensor_tensor(sb["pst"][0:1, 2:3], sb["pst"][0:1, 0:1], sb["pst"][0:1, 0:1], ALU.mult)
            v.drain()
            v.tensor_tensor(sb["pst"][0:1, 1:2], sb["pst"][0:1, 1:2], sb["pst"][0:1, 2:3], ALU.subtract)
            v.drain()
            v.sem_inc(sem["EV"], 1)
            C.inc("EV", 1)

        @blk.scalar
        def _(s):
            s.wait_ge(sem["EV"], C.get("EV"))
            s.activation(sb["statrow2"][0:1, :], sb["statrow2"][0:1, :], AF.Sqrt, bias=sb["cbias"][0:1, 0:1])
            s.activation(sb["pst"][0:1, 1:2], sb["pst"][0:1, 1:2], AF.Sqrt, bias=sb["cbias"][0:1, 0:1])
            s.drain()
            s.sem_inc(sem["SC"], 1)
            C.inc("SC", 1)

        @blk.vector
        def _(v):
            v.wait_ge(sem["SC"], C.get("SC"))
            v.reciprocal(sb["statrow2"][0:1, :], sb["statrow2"][0:1, :])
            v.reciprocal(sb["pst"][0:1, 1:2], sb["pst"][0:1, 1:2])
            v.drain()
            v.sem_inc(sem["EV"], 1)
            C.inc("EV", 1)

        @blk.tensor
        def _(t):
            t.wait_ge(sem["EV"], C.get("EV"))
            for hf in range(2):
                sl = slice(hf * 512, (hf + 1) * 512)
                t.matmul(ps[0 + hf][:, :], sb["onesrow"][0:1, :], sb["statrow"][0:1, sl], start=True, stop=True)
                t.matmul(ps[2 + hf][:, :], sb["onesrow"][0:1, :], sb["statrow2"][0:1, sl], start=True, stop=True)
            t.matmul(ps[4][:, 0:1], sb["onesrow"][0:1, :], sb["pst"][0:1, 0:1], start=True, stop=True)
            t.matmul(ps[4][:, 1:2], sb["onesrow"][0:1, :], sb["pst"][0:1, 1:2], start=True, stop=True)
            t.drain()
            t.sem_inc(sem["MM"], 1)
            C.inc("MM", 1)

        @blk.vector
        def _(v):
            v.wait_ge(sem["MM"], C.get("MM"))
            for hf in range(2):
                sl = slice(hf * 512, (hf + 1) * 512)
                v.tensor_copy(sb["muB"][:, sl], ps[0 + hf][:, :])
                v.tensor_copy(sb["rB"][:, sl], ps[2 + hf][:, :])
            v.tensor_copy(sb["p128"][:, 0:2], ps[4][:, 0:2])
            v.drain()
            v.tensor_scalar(sb["xp0col"][:, :], sb["xp0col"][:, :],
                            sb["p128"][:, 0:1], sb["p128"][:, 1:2], ALU.subtract, ALU.mult)
            for cp in range(8):
                a, b = 2 * cp, 2 * cp + 1
                v.tensor_tensor(sb["T1"][:, :], blk_c(xbuf, a), sb["muB"][:, :], ALU.subtract)
                v.tensor_tensor(sb["T2"][:, :], blk_c(xbuf, b), sb["muB"][:, :], ALU.subtract)
                v.drain()
                v.tensor_tensor(blk_c(xn, a), sb["T1"][:, :], sb["rB"][:, :], ALU.mult)
                v.tensor_tensor(blk_c(xn, b), sb["T2"][:, :], sb["rB"][:, :], ALU.mult)
                v.drain()
            for ci in range(16):
                v.tensor_tensor(blk_c(dx, ci)[:, 1:TOK], blk_c(xn, ci)[:, 0:TOK - 1],
                                blk_c(xn, ci)[:, 1:TOK], ALU.subtract)
                v.tensor_tensor(blk_c(dx, ci)[:, 0:1], sb["xp0col"][:, ci:ci + 1],
                                blk_c(xn, ci)[:, 0:1], ALU.subtract)
            v.drain()
            v.sem_inc(sem["EV"], 1)
            C.inc("EV", 1)
    A1a.close()
    dump_cb("xn", xn, 16)
    dump_cb("dx", dx, 16)

    A2 = contextlib.ExitStack()
    F0 = alloc(A2, "F0", [128, 16 * 512], F32)
    F1 = alloc(A2, "F1", [128, 16 * 512], F32)
    x1t = alloc(A2, "x1t", [128, NCB * TOK], BF16)
    alloc(A2, "H128su", [128, 16 * 128], BF16)

    with nc.Block() as blk:
        @blk.vector
        def _(v):
            for ci in range(16):
                v.tensor_scalar_mul(sb["H128su"][:, ci * 128:(ci + 1) * 128],
                                    sb["H128b"][:, :], sb["su"][:, ci:ci + 1])
            v.drain()
            v.sem_inc(sem["EV"], 1)
            C.inc("EV", 1)

    for hf in range(2):
        with nc.Block() as blk:
            @blk.tensor
            def _(t):
                evb = C.get("EV")
                for ci in range(16):
                    if ci >= 4:
                        t.wait_ge(sem["EV"], evb + ci - 3)
                    t.matmul(ps[ci % 4][:, :], sb["H128su"][:, ci * 128:(ci + 1) * 128],
                             blk_c(xn, ci)[:, hf * 512:(hf + 1) * 512], start=True, stop=True)
                    t.drain()
                    t.sem_inc(sem["MM"], 1)
                C.inc("MM", 16)

            @blk.vector
            def _(v):
                mmb = C.get("MM") - 16
                for ci in range(16):
                    v.wait_ge(sem["MM"], mmb + ci + 1)
                    v.tensor_copy(F0[:, ci * 512:(ci + 1) * 512], ps[ci % 4][:, :])
                    v.drain()
                    v.sem_inc(sem["EV"], 1)
                C.inc("EV", 16)
                v.drain()
                bufs = [F0, F1]
                for si, d in enumerate([1, 2, 4]):
                    src, dst = bufs[si % 2], bufs[(si + 1) % 2]
                    for i in range(16):
                        if i & d:
                            continue
                        j = i + d
                        a = src[:, i * 512:(i + 1) * 512]
                        b = src[:, j * 512:(j + 1) * 512]
                        v.tensor_tensor(dst[:, i * 512:(i + 1) * 512], a, b, ALU.add)
                        v.tensor_tensor(dst[:, j * 512:(j + 1) * 512], a, b, ALU.subtract)
                    v.drain()
                src = bufs[1]
                for i in range(16):
                    if i & 8:
                        continue
                    j = i + 8
                    a = src[:, i * 512:(i + 1) * 512]
                    b = src[:, j * 512:(j + 1) * 512]
                    v.tensor_tensor(x1t[:, i * TOK + hf * 512: i * TOK + hf * 512 + 512], a, b, ALU.add)
                    v.tensor_tensor(x1t[:, j * TOK + hf * 512: j * TOK + hf * 512 + 512], a, b, ALU.subtract)
                v.drain()
                v.sem_inc(sem["EV"], 1)
                C.inc("EV", 1)

    with nc.Block() as blk:
        @blk.sync
        def _(sync):
            sync.dma_start(out=scr["x1"], in_=x1t[:, :]).then_inc(sem["SP"], 16)
            sync.wait_ge(sem["SP"], C.inc("SP", 16))
    dump_cb("x1", x1t, 16)
    A2.close()

    if PHASES <= 1:
        return _early_out(close_a1b=True)

    A3 = contextlib.ExitStack()
    alloc(A3, "mmt", [128, TOK], BF16)
    alloc(A3, "tht", [128, TOK], BF16)
    xxx = alloc(A3, "xxx", [128, TOK], BF16)
    mx = alloc(A3, "mx", [128, NCB * TOK], BF16)
    pout = alloc(A3, "pout", [128, NCB * TOK], BF16)
    wstr = [alloc(A3, "wstr%d" % i, [128, 16 * 128], BF16) for i in range(2)]
    Ta = alloc(A3, "Ta", [128, 512], F32)
    Tb = alloc(A3, "Tb", [128, 512], F32)

    with nc.Block() as blk:
        @blk.gpsimd
        def _(g):
            g.dma_start(out=wstr[0][:, :], in_=d_maaw1).then_inc(sem["WL"], 16)
            C.inc("WL", 16)

        @blk.vector
        def _(v):
            mmb = C.get("MM")
            for ci in range(16):
                if ci >= 1:
                    v.wait_ge(sem["MM"], mmb + ci)
                v.scalar_tensor_tensor(xxx[:, :], blk_c(dx, ci), sb["maa_x"][:, ci:ci + 1],
                                       blk_c(xn, ci), ALU.mult, ALU.add)
                v.drain()
                v.sem_inc(sem["EV"], 1)
            C.inc("EV", 16)

        @blk.tensor
        def _(t):
            t.wait_ge(sem["WL"], C.get("WL"))
            evb = C.get("EV") - 16
            for ci in range(16):
                t.wait_ge(sem["EV"], evb + ci + 1)
                for hf in range(2):
                    t.matmul(ps[4 + hf][:, :], wstr[0][:, ci * 128:(ci + 1) * 128],
                             xxx[:, hf * 512:(hf + 1) * 512], start=(ci == 0), stop=(ci == 15))
                t.drain()
                t.sem_inc(sem["MM"], 1)
            C.inc("MM", 16)

        @blk.scalar
        def _(s):
            s.wait_ge(sem["MM"], C.get("MM"))
            for hf in range(2):
                s.activation(sb["mmt"][:, hf * 512:(hf + 1) * 512], ps[4 + hf][:, :], AF.Tanh)
            s.drain()
            s.sem_inc(sem["SC"], 1)
            C.inc("SC", 1)
    dump_cb("mm", sb["mmt"], 1)

    def mix_block(tgt_idx, vecname):
        with nc.Block() as blk:
            @blk.gpsimd
            def _(g):
                g.dma_start(out=wstr[0][:, :], in_=d_w2blk[:, tgt_idx * 2048:(tgt_idx + 1) * 2048]).then_inc(sem["WL"], 16)
                C.inc("WL", 16)

            @blk.tensor
            def _(t):
                t.wait_ge(sem["WL"], C.get("WL"))
                t.wait_ge(sem["SC"], C.get("SC"))
                evb = C.get("EV")
                for oc in range(16):
                    pp = ps[(oc % 2) * 2: (oc % 2) * 2 + 2]
                    if oc >= 2:
                        t.wait_ge(sem["EV"], evb + oc - 1)
                    for hf in range(2):
                        t.matmul(pp[hf][:, :], wstr[0][:, oc * 128:(oc + 1) * 128],
                                 sb["mmt"][:, hf * 512:(hf + 1) * 512], start=True, stop=True)
                    t.drain()
                    t.sem_inc(sem["MM"], 1)
                C.inc("MM", 16)

            @blk.vector
            def _(v):
                mmb = C.get("MM") - 16
                for oc in range(16):
                    pp = ps[(oc % 2) * 2: (oc % 2) * 2 + 2]
                    v.wait_ge(sem["MM"], mmb + oc + 1)
                    for hf in range(2):
                        v.scalar_tensor_tensor((Ta if hf == 0 else Tb)[:, :],
                                               pp[hf][:, :], sb[vecname][:, oc:oc + 1],
                                               blk_c(dx, oc)[:, hf * 512:(hf + 1) * 512],
                                               ALU.add, ALU.mult)
                    v.drain()
                    for hf in range(2):
                        v.tensor_tensor(blk_c(mx, oc)[:, hf * 512:(hf + 1) * 512],
                                        (Ta if hf == 0 else Tb)[:, :],
                                        blk_c(xn, oc)[:, hf * 512:(hf + 1) * 512], ALU.add)
                    v.drain()
                    v.sem_inc(sem["EV"], 1)
                C.inc("EV", 16)

    def proj_block(wdram, bias_col):
        with nc.Block() as blk:
            @blk.gpsimd
            def _(g):
                mmb = C.get("MM")
                for oc in range(16):
                    if oc >= 2:
                        g.wait_ge(sem["MM"], mmb + oc - 1)
                    strip_load(g, wstr[oc % 2], wdram[oc], s="WA" if oc % 2 == 0 else "WB")
                C.inc("WA", 16 * 8)
                C.inc("WB", 16 * 8)

            @blk.tensor
            def _(t):
                wlba = C.get("WA") - 16 * 8
                wlbb = C.get("WB") - 16 * 8
                scb = C.get("SC")
                for oc in range(16):
                    t.wait_ge(sem["WA" if oc % 2 == 0 else "WB"],
                              (wlba if oc % 2 == 0 else wlbb) + 16 * (oc // 2 + 1))
                    pp = ps[(oc % 2) * 2: (oc % 2) * 2 + 2]
                    if oc >= 2:
                        t.wait_ge(sem["SC"], scb + oc - 1)
                    for ci in range(16):
                        for hf in range(2):
                            t.matmul(pp[hf][:, :], wstr[oc % 2][:, ci * 128:(ci + 1) * 128],
                                     blk_c(mx, ci)[:, hf * 512:(hf + 1) * 512],
                                     start=(ci == 0), stop=(ci == 15))
                    t.drain()
                    t.sem_inc(sem["MM"], 1)
                C.inc("MM", 16)

            @blk.scalar
            def _(s):
                mmb = C.get("MM") - 16
                for oc in range(16):
                    pp = ps[(oc % 2) * 2: (oc % 2) * 2 + 2]
                    s.wait_ge(sem["MM"], mmb + oc + 1)
                    for hf in range(2):
                        s.activation(blk_c(pout, oc)[:, hf * 512:(hf + 1) * 512], pp[hf][:, :],
                                     AF.Identity, bias=sb[bias_col][:, oc:oc + 1])
                    s.drain()
                    s.sem_inc(sem["SC"], 1)
                C.inc("SC", 16)

    def spill_block(dst):
        with nc.Block() as blk:
            @blk.sync
            def _(sync):
                sync.dma_start(out=dst, in_=pout[:, :]).then_inc(sem["SP"], 16)
                sync.wait_ge(sem["SP"], C.inc("SP", 16))

    mix_block(3, "maa_w")
    with nc.Block() as blk:
        @blk.gpsimd
        def _(g):
            g.dma_start(out=wstr[0][:, 0:1024], in_=d_tdw1).then_inc(sem["WL"], 16)
            g.dma_start(out=wstr[1][0:64, 0:2048], in_=d_tdw2).then_inc(sem["WL"], 16)
            C.inc("WL", 32)

        @blk.tensor
        def _(t):
            t.wait_ge(sem["WL"], C.get("WL"))
            for ci in range(16):
                for hf in range(2):
                    t.matmul(ps[4 + hf][0:64, :], wstr[0][:, ci * 64:ci * 64 + 64],
                             blk_c(mx, ci)[:, hf * 512:(hf + 1) * 512],
                             start=(ci == 0), stop=(ci == 15))
            t.drain()
            t.sem_inc(sem["MM"], 1)
            C.inc("MM", 1)

        @blk.scalar
        def _(s):
            s.wait_ge(sem["MM"], C.get("MM"))
            for hf in range(2):
                s.activation(sb["tht"][0:64, hf * 512:(hf + 1) * 512], ps[4 + hf][0:64, :], AF.Tanh)
            s.drain()
            s.sem_inc(sem["SC"], 1)
            C.inc("SC", 1)

        @blk.tensor
        def _(t):
            t.wait_ge(sem["SC"], C.get("SC"))
            scb = C.get("SC")
            for oc in range(16):
                pp = ps[(oc % 2) * 2: (oc % 2) * 2 + 2]
                if oc >= 2:
                    t.wait_ge(sem["SC"], scb + oc - 1)
                for hf in range(2):
                    t.matmul(pp[hf][:, :], wstr[1][0:64, oc * 128:(oc + 1) * 128],
                             sb["tht"][0:64, hf * 512:(hf + 1) * 512], start=True, stop=True)
                t.drain()
                t.sem_inc(sem["MM"], 1)
            C.inc("MM", 16)

        @blk.scalar
        def _(s):
            mmb = C.get("MM") - 16
            evb = C.get("EV")
            for oc in range(16):
                pp = ps[(oc % 2) * 2: (oc % 2) * 2 + 2]
                s.wait_ge(sem["MM"], mmb + oc + 1)
                if oc >= 1:
                    s.wait_ge(sem["EV"], evb + oc)
                for hf in range(2):
                    s.activation((Ta if hf == 0 else Tb)[:, :], pp[hf][:, :],
                                 AF.Exp, bias=sb["tdv"][:, oc:oc + 1])
                s.drain()
                s.sem_inc(sem["SC"], 1)
            C.inc("SC", 16)

        @blk.vector
        def _(v):
            scb = C.get("SC") - 16
            for oc in range(16):
                v.wait_ge(sem["SC"], scb + oc + 1)
                for hf in range(2):
                    v.tensor_scalar_min(blk_c(pout, oc)[:, hf * 512:(hf + 1) * 512],
                                        (Ta if hf == 0 else Tb)[:, :], 5.0)
                v.drain()
                v.sem_inc(sem["EV"], 1)
            C.inc("EV", 16)
    spill_block(scr["wexp"])
    dump_cb("wexp", pout, 16)

    mix_block(2, "maa_v")
    proj_block(d_vw, "vb")
    spill_block(scr["v"])
    dump_cb("v", pout, 16)

    mix_block(1, "maa_k")
    with nc.Block() as blk:
        @blk.sync
        def _(sync):
            scb = C.get("SC")
            for oc in range(16):
                if oc >= 1:
                    sync.wait_ge(sem["SC"], scb + oc)
                sync.dma_start(out=xxx[:, :], in_=scr["wexp"][:, oc * TOK:(oc + 1) * TOK]).then_inc(sem["LD"], 16)
            C.inc("LD", 16 * 16)

        @blk.gpsimd
        def _(g):
            mmb = C.get("MM")
            for oc in range(16):
                if oc >= 2:
                    g.wait_ge(sem["MM"], mmb + oc - 1)
                strip_load(g, wstr[oc % 2], d_kw[oc], s="WA" if oc % 2 == 0 else "WB")
            C.inc("WA", 16 * 8)
            C.inc("WB", 16 * 8)

        @blk.tensor
        def _(t):
            wlba = C.get("WA") - 16 * 8
            wlbb = C.get("WB") - 16 * 8
            scb = C.get("SC")
            for oc in range(16):
                t.wait_ge(sem["WA" if oc % 2 == 0 else "WB"],
                          (wlba if oc % 2 == 0 else wlbb) + 16 * (oc // 2 + 1))
                pp = ps[(oc % 2) * 2: (oc % 2) * 2 + 2]
                if oc >= 2:
                    t.wait_ge(sem["SC"], scb + oc - 1)
                for ci in range(16):
                    for hf in range(2):
                        t.matmul(pp[hf][:, :], wstr[oc % 2][:, ci * 128:(ci + 1) * 128],
                                 blk_c(mx, ci)[:, hf * 512:(hf + 1) * 512],
                                 start=(ci == 0), stop=(ci == 15))
                t.drain()
                t.sem_inc(sem["MM"], 1)
            C.inc("MM", 16)

        @blk.scalar
        def _(s):
            mmb = C.get("MM") - 16
            ldb = C.get("LD") - 256
            evb = C.get("EV")
            for oc in range(16):
                pp = ps[(oc % 2) * 2: (oc % 2) * 2 + 2]
                s.wait_ge(sem["MM"], mmb + oc + 1)
                s.wait_ge(sem["LD"], ldb + 16 * (oc + 1))
                if oc >= 1:
                    s.wait_ge(sem["EV"], evb + oc)
                s.activation(sb["muB"][:, 0:TOK], xxx[:, :], AF.Exp, scale=-1.0)
                for hf in range(2):
                    s.activation((Ta if hf == 0 else Tb)[:, :], pp[hf][:, :],
                                 AF.Identity, bias=sb["kb"][:, oc:oc + 1])
                s.drain()
                s.activation(sb["rB"][:, 0:TOK], sb["muB"][:, 0:TOK], AF.Identity, scale=-1.0, bias=sb["cbias"][:, 2:3])
                s.drain()
                s.sem_inc(sem["SC"], 1)
            C.inc("SC", 16)

        @blk.vector
        def _(v):
            scb = C.get("SC") - 16
            for oc in range(16):
                v.wait_ge(sem["SC"], scb + oc + 1)
                for hf in range(2):
                    v.tensor_tensor(blk_c(pout, oc)[:, hf * 512:(hf + 1) * 512],
                                    (Ta if hf == 0 else Tb)[:, :],
                                    sb["rB"][:, hf * 512:(hf + 1) * 512], ALU.mult)
                v.drain()
                v.sem_inc(sem["EV"], 1)
            C.inc("EV", 16)
    spill_block(scr["km"])
    dump_cb("km", pout, 16)

    mix_block(0, "maa_r")
    proj_block(d_qw, "qb")
    spill_block(scr["q"])
    dump_cb("q", pout, 16)
    A3.close()
    A1b.close()

    if PHASES <= 2:
        return _early_out()

    A4 = contextlib.ExitStack()
    qc = [alloc(A4, "qc%d" % i, [128, 2048], BF16) for i in range(2)]
    kmc = [alloc(A4, "kmc%d" % i, [128, 2048], BF16) for i in range(2)]
    vc = [alloc(A4, "vc%d" % i, [128, 2048], BF16) for i in range(2)]
    wxc = [alloc(A4, "wxc%d" % i, [128, 2048], BF16) for i in range(2)]
    csc = [alloc(A4, "csc%d" % i, [128, 2048], F32) for i in range(2)]
    gqc = [alloc(A4, "gqc%d" % i, [128, 2048], F32) for i in range(2)]
    gqic = alloc(A4, "gqic", [128, 2048], F32)
    qtc = [alloc(A4, "qtc%d" % i, [128, 2048], BF16) for i in range(2)]
    ktc = [alloc(A4, "ktc%d" % i, [128, 2048], BF16) for i in range(2)]
    k2c = [alloc(A4, "k2c%d" % i, [128, 2048], BF16) for i in range(2)]
    qts = [alloc(A4, "qts%d" % i, [64, 2048], BF16) for i in range(2)]
    kts = [alloc(A4, "kts%d" % i, [64, 2048], BF16) for i in range(2)]
    k2tm = [alloc(A4, "k2tm%d" % i, [128, 2048], BF16) for i in range(2)]
    vtm = [alloc(A4, "vtm%d" % i, [128, 2048], BF16) for i in range(2)]
    pmt = alloc(A4, "pmt", [128, 32 * 128], BF16)
    oic = alloc(A4, "oic", [128, 2048], BF16)

    evb0 = C.get("EV"); scb0 = C.get("SC"); mmb0 = C.get("MM")
    wab0 = C.get("LA"); wbb0 = C.get("LB"); spb0 = C.get("SP")
    NV, NS, NT = 9, 1, 7

    def vec_cnt(j, stage):
        return evb0 + j * NV + stage

    def mm_cnt(j, stage):
        return mmb0 + j * NT + stage

    with nc.Block() as blk:
        @blk.sync
        def _(sync):
            spb_s = C.get("SP")

            def loads(j):
                p = j % 2
                if j >= 2:
                    sync.wait_ge(sem["MM"], mm_cnt(j - 2, 7))
                for ci in range(16):
                    co = ci * TOK + j * 128
                    for nm, dst in [("q", qc[p]), ("km", kmc[p]), ("v", vc[p]), ("wexp", wxc[p])]:
                        sync.dma_start(out=dst[:, ci * 128:(ci + 1) * 128],
                                       in_=scr[nm][:, co:co + 128]).then_inc(sem["LA" if p == 0 else "LB"], 16)

            def spill(j):
                sync.wait_ge(sem["EV"], vec_cnt(j, 9))
                sync.dma_start(out=scr["oi"][:, j * 2048:(j + 1) * 2048], in_=oic[:, :]).then_inc(sem["SP"], 16)
                sync.dma_start(out=scr["qt"][:, j * 2048:(j + 1) * 2048], in_=qtc[j % 2][:, :]).then_inc(sem["SP"], 16)

            loads(0); loads(1); loads(2)
            for j in range(3, NCH):
                spill(j - 3)
                loads(j)
            spill(NCH - 3); spill(NCH - 2); spill(NCH - 1)
            sync.wait_ge(sem["SP"], spb_s + 32 * NCH)
            C.inc("LA", 16 * 64 * 4)
            C.inc("LB", 16 * 64 * 4)
            C.inc("SP", 32 * NCH)

        @blk.scalar
        def _(s):
            for j in range(NCH):
                p = j % 2
                s.wait_ge(sem["EV"], vec_cnt(j, 1))
                s.activation(gqc[p][:, :], csc[p][:, :], AF.Exp, scale=-1.0)
                s.activation(gqic[:, :], csc[p][:, :], AF.Exp)
                s.drain()
                s.sem_inc(sem["SC"], 1)
            C.inc("SC", NCH)

        @blk.gpsimd
        def _(g):
            for j in range(NCH):
                p = j % 2
                g.wait_ge(sem["EV"], vec_cnt(j, 2))
                g.dma_start(out=qts[p][:, :], in_=qtc[p][64:128, :]).then_inc(sem["SH"], 16)
                g.dma_start(out=kts[p][:, :], in_=ktc[p][64:128, :]).then_inc(sem["SH"], 16)
            C.inc("SH", 32 * NCH)

        @blk.vector
        def _(v):
            ldb0 = C.get("LD") - 32 * NCH
            for j in range(NCH):
                p = j % 2
                wa = wab0 + (j // 2 + 1) * 16 * 64 if p == 0 else wbb0 + (j // 2 + 1) * 16 * 64
                v.wait_ge(sem["LA" if p == 0 else "LB"], wa)
                for ci in range(16):
                    v.tensor_tensor_scan(csc[p][:, ci * 128:(ci + 1) * 128],
                                         wxc[p][:, ci * 128:(ci + 1) * 128],
                                         wxc[p][:, ci * 128:(ci + 1) * 128],
                                         0.0, ALU.add, ALU.bypass)
                v.drain()
                v.sem_inc(sem["EV"], 1)
                v.wait_ge(sem["SC"], scb0 + j + 1)
                if j >= 2:
                    v.wait_ge(sem["SP"], spb0 + 32 * (j - 1))
                v.tensor_tensor(qtc[p][:, :], qc[p][:, :], gqc[p][:, :], ALU.mult)
                v.tensor_tensor(ktc[p][:, :], kmc[p][:, :], gqic[:, :], ALU.mult)
                v.drain()
                for ci in range(16):
                    v.tensor_scalar_mul(k2c[p][:, ci * 128:(ci + 1) * 128],
                                        ktc[p][:, ci * 128:(ci + 1) * 128],
                                        gqc[p][:, ci * 128 + 127:ci * 128 + 128])
                v.drain()
                v.sem_inc(sem["EV"], 1)
                for g4 in range(4):
                    v.wait_ge(sem["MM"], mm_cnt(j, g4 + 1))
                    v.tensor_copy(vtm[p][:, g4 * 512:(g4 + 1) * 512], psb[:, 0:512])
                    v.tensor_copy(k2tm[p][:, g4 * 512:(g4 + 1) * 512], psb[:, 512:1024])
                    v.drain()
                    v.sem_inc(sem["EV"], 1)
                for wave in range(2):
                    v.wait_ge(sem["MM"], mm_cnt(j, 5 + wave))
                    for bank in range(4):
                        v.tensor_tensor(pmt[:, wave * 2048 + bank * 512: wave * 2048 + (bank + 1) * 512],
                                        ps[bank][:, :], sb["mask4"][:, :], ALU.mult)
                    v.drain()
                    v.sem_inc(sem["EV"], 1)
                v.wait_ge(sem["MM"], mm_cnt(j, 7))
                if j >= 1:
                    v.wait_ge(sem["SP"], spb0 + 32 * j)
                v.tensor_copy(sb["Lsnap"][:, j * 1024:(j + 1) * 1024], sb["S_run"][:, :])
                for ci in range(16):
                    v.tensor_copy(sb["Apref"][:, j * 16 + ci:j * 16 + ci + 1],
                                  sb["A_run"][:, ci:ci + 1])
                for ci in range(16):
                    v.tensor_copy(oic[:, ci * 128:(ci + 1) * 128],
                                  ps[ci % 4][:, (ci // 4) * 128:(ci // 4) * 128 + 128])
                v.drain()
                for ci in range(16):
                    v.scalar_tensor_tensor(sb["S_run"][:, ci * 64:(ci + 1) * 64],
                                           sb["S_run"][:, ci * 64:(ci + 1) * 64],
                                           gqc[p][:, ci * 128 + 127:ci * 128 + 128],
                                           ps[4 + ci % 2][:, (ci // 2) * 64:(ci // 2) * 64 + 64],
                                           ALU.mult, ALU.add)
                    v.tensor_scalar_mul(sb["A_run"][:, ci:ci + 1], sb["A_run"][:, ci:ci + 1],
                                        gqc[p][:, ci * 128 + 127:ci * 128 + 128])
                v.drain()
                v.sem_inc(sem["EV"], 1)
            C.inc("EV", NV * NCH)

        @blk.tensor
        def _(t):
            ldb0 = C.get("SH") - 32 * NCH
            spb = C.get("SP")
            for j in range(NCH):
                p = j % 2
                t.wait_ge(sem["EV"], vec_cnt(j, 2))
                for g4 in range(4):
                    if not (j == 0 and g4 == 0):
                        prev = vec_cnt(j, 2 + g4) if g4 > 0 else vec_cnt(j - 1, 6)
                        t.wait_ge(sem["EV"], prev)
                    for sl in range(4):
                        ci = g4 * 4 + sl
                        t.transpose(psb[:, sl * 128:(sl + 1) * 128], vc[p][:, ci * 128:(ci + 1) * 128], sb["identb"][:, :])
                        ti = t.transpose(psb[:, 512 + sl * 128: 512 + (sl + 1) * 128], k2c[p][:, ci * 128:(ci + 1) * 128], sb["identb"][:, :])
                    ti.then_inc(sem["MM"], 1)
                t.wait_ge(sem["SH"], ldb0 + 32 * (j + 1))
                if j >= 1:
                    t.wait_ge(sem["EV"], vec_cnt(j - 1, 9))
                for wave in range(2):
                    if wave == 1:
                        t.wait_ge(sem["EV"], vec_cnt(j, 7))
                    for hh in range(16):
                        h = wave * 16 + hh
                        ci, off = h // 2, (h % 2) * 64
                        kt_src = ktc[p][0:64, ci * 128:(ci + 1) * 128] if off == 0 else kts[p][:, ci * 128:(ci + 1) * 128]
                        qt_src = qtc[p][0:64, ci * 128:(ci + 1) * 128] if off == 0 else qts[p][:, ci * 128:(ci + 1) * 128]
                        ti = t.matmul(ps[hh // 4][:, (hh % 4) * 128:(hh % 4 + 1) * 128],
                                      kt_src, qt_src, start=True, stop=True)
                    ti.then_inc(sem["MM"], 1)
                t.wait_ge(sem["EV"], vec_cnt(j, 8))
                for h in range(32):
                    ci, off = h // 2, (h % 2) * 64
                    t.matmul(ps[ci % 4][off:off + 64, (ci // 4) * 128:(ci // 4) * 128 + 128],
                             vtm[p][:, ci * 128 + off:ci * 128 + off + 64],
                             pmt[:, h * 128:(h + 1) * 128],
                             start=True, stop=True, tile_position=(0, off))
                for h in range(32):
                    ci, off = h // 2, (h % 2) * 64
                    ti = t.matmul(ps[4 + ci % 2][off:off + 64, (ci // 2) * 64:(ci // 2) * 64 + 64],
                                  k2tm[p][:, ci * 128 + off:ci * 128 + off + 64],
                                  vtm[p][:, ci * 128 + off:ci * 128 + off + 64],
                                  start=True, stop=True, tile_position=(0, off))
                ti.then_inc(sem["MM"], 1)
            C.inc("MM", NT * NCH)

    A4.close()
    dump_small("sspan", sb["S_run"][:, :], NCB * 64)
    dump_small("aspan", sb["A_run"][:, :], 16)

    if PHASES <= 3:
        return _early_out()

    with nc.Block() as blk:
        @blk.vector
        def _(v):
            for ci in range(16):
                v.tensor_copy(sb["aggst"][:, ci * 65:ci * 65 + 64], sb["S_run"][:, ci * 64:(ci + 1) * 64])
                v.tensor_copy(sb["aggst"][:, ci * 65 + 64:ci * 65 + 65], sb["A_run"][:, ci:ci + 1])
            v.drain()
            v.sem_inc(sem["EV"], 1)
            C.inc("EV", 1)

        @blk.sync
        def _(sync):
            sync.wait_ge(sem["EV"], C.get("EV"))
            for ci in range(16):
                sync.dma_start(out=d_agin.ap()[ci * 128:(ci + 1) * 128, :],
                               in_=sb["aggst"][:, ci * 65:(ci + 1) * 65]).then_inc(sem["SP"], 16)
            C.inc("SP", 16 * 16)

        @blk.gpsimd
        def _(g):
            g.wait_ge(sem["SP"], C.get("SP"))
            g.collective_compute("AllGather", ALU.bypass,
                                 replica_groups=[list(range(8))],
                                 ins=[d_agin.ap()], outs=[d_agout.ap()]).then_inc(sem["CC"], 1)
            C.inc("CC", 1)

        @blk.vector
        def _(v):
            v.memset(sb["S_run"][:, :], 0.0)
            v.drain()
            v.sem_inc(sem["EV"], 1)
            C.inc("EV", 1)

    for jj in range(7):
        with nc.Block() as blk:
            @blk.sync
            def _(sync):
                sync.wait_ge(sem["CC"], C.get("CC"))
                for ci in range(16):
                    sync.dma_start(out=sb["aggst"][:, ci * 65:(ci + 1) * 65],
                                   in_=d_agout.ap()[jj * 2048 + ci * 128: jj * 2048 + (ci + 1) * 128, :]).then_inc(sem["LD"], 16)
                C.inc("LD", 16 * 16)

            @blk.vector
            def _(v):
                v.wait_ge(sem["LD"], C.get("LD"))
                for ci in range(16):
                    v.tensor_scalar(sb["ALtmp"][:, ci:ci + 1],
                                    sb["aggst"][:, ci * 65 + 64:ci * 65 + 65],
                                    sb["umask"][:, jj:jj + 1],
                                    sb["umask"][:, 8 + jj:9 + jj], ALU.mult, ALU.add)
                    v.tensor_scalar_mul(sb["sfold"][:, ci * 64:(ci + 1) * 64],
                                        sb["aggst"][:, ci * 65:ci * 65 + 64],
                                        sb["umask"][:, jj:jj + 1])
                v.drain()
                for ci in range(16):
                    v.scalar_tensor_tensor(sb["S_run"][:, ci * 64:(ci + 1) * 64],
                                           sb["S_run"][:, ci * 64:(ci + 1) * 64],
                                           sb["ALtmp"][:, ci:ci + 1],
                                           sb["sfold"][:, ci * 64:(ci + 1) * 64],
                                           ALU.mult, ALU.add)
                v.drain()
                v.sem_inc(sem["EV"], 1)
                C.inc("EV", 1)
    dump_small("sin", sb["S_run"][:, :], NCB * 64)

    with nc.Block() as blk:
        @blk.vector
        def _(v):
            for j in range(NCH):
                for ci in range(16):
                    v.scalar_tensor_tensor(sb["Lsnap"][:, j * 1024 + ci * 64:j * 1024 + (ci + 1) * 64],
                                           sb["S_run"][:, ci * 64:(ci + 1) * 64],
                                           sb["Apref"][:, j * 16 + ci:j * 16 + ci + 1],
                                           sb["Lsnap"][:, j * 1024 + ci * 64:j * 1024 + (ci + 1) * 64],
                                           ALU.mult, ALU.add)
            v.drain()
            v.sem_inc(sem["EV"], 1)
            C.inc("EV", 1)

    if PHASES <= 4:
        return _early_out()

    PB = contextlib.ExitStack()
    ynb = alloc(PB, "ynb", [128, NCB * TOK], BF16)
    qtr = [alloc(PB, "qtr%d" % i, [128, 2048], BF16) for i in range(2)]
    oir = [alloc(PB, "oir%d" % i, [128, 2048], BF16) for i in range(2)]
    qtro = [alloc(PB, "qtro%d" % i, [64, 2048], BF16) for i in range(2)]
    scso = [alloc(PB, "scso%d" % i, [64, 1024], BF16) for i in range(2)]
    ocb = alloc(PB, "ocb", [128, 2048], BF16)
    osq = alloc(PB, "osq", [128, 2048], BF16)
    Tcw = alloc(PB, "Tcw", [128, 2048], F32)
    muw = alloc(PB, "muw", [128, 2048], F32)
    rw = alloc(PB, "rw", [128, 2048], F32)
    mu1 = alloc(PB, "mu1", [1, 128], F32)
    rs1 = alloc(PB, "rs1", [1, 128], F32)

    evbP = C.get("EV"); mmbP = C.get("MM"); scbP = C.get("SC")
    labP = C.get("LA"); lbbP = C.get("LB")

    def vP(j, s):
        return evbP + 4 * j + s

    def mP(j, s):
        return mmbP + 3 * j + s

    def sP(j, s):
        return scbP + 3 * j + s

    with nc.Block() as blk:
        @blk.sync
        def _(sync):
            def loadsP(j):
                p = j % 2
                sname = "LA" if p == 0 else "LB"
                base = (labP if p == 0 else lbbP) + (j // 2) * 64
                if j >= 2:
                    sync.wait_ge(sem["MM"], mP(j - 2, 1))
                    sync.wait_ge(sem["EV"], vP(j - 2, 1))
                sync.dma_start(out=qtr[p][:, :], in_=scr["qt"][:, j * 2048:(j + 1) * 2048]).then_inc(sem[sname], 16)
                sync.dma_start(out=oir[p][:, :], in_=scr["oi"][:, j * 2048:(j + 1) * 2048]).then_inc(sem[sname], 16)
                sync.wait_ge(sem[sname], base + 32)
                sync.dma_start(out=qtro[p][:, :], in_=qtr[p][64:128, :]).then_inc(sem[sname], 16)
                sync.dma_start(out=scso[p][:, :], in_=sb["Lsnap"][64:128, j * 1024:(j + 1) * 1024]).then_inc(sem[sname], 16)
            loadsP(0)
            loadsP(1)
            for j in range(2, NCH):
                loadsP(j)
            C.inc("LA", 64 * 4)
            C.inc("LB", 64 * 4)

        @blk.tensor
        def _(t):
            for j in range(NCH):
                p = j % 2
                sname = "LA" if p == 0 else "LB"
                t.wait_ge(sem[sname], (labP if p == 0 else lbbP) + (j // 2) * 64 + 64)
                if j >= 1:
                    t.wait_ge(sem["EV"], vP(j - 1, 1))
                for h in range(32):
                    ci, off = h // 2, (h % 2) * 64
                    ls_src = (sb["Lsnap"][0:64, j * 1024 + ci * 64:j * 1024 + (ci + 1) * 64]
                              if off == 0 else scso[p][:, ci * 64:(ci + 1) * 64])
                    qt_src = (qtr[p][0:64, ci * 128:(ci + 1) * 128]
                              if off == 0 else qtro[p][:, ci * 128:(ci + 1) * 128])
                    ti = t.matmul(ps[ci % 4][off:off + 64, (ci // 4) * 128:(ci // 4) * 128 + 128],
                                  ls_src, qt_src, start=True, stop=True, tile_position=(0, off))
                ti.then_inc(sem["MM"], 1)
                t.wait_ge(sem["SC"], sP(j, 1))
                if j >= 1:
                    t.wait_ge(sem["EV"], vP(j - 1, 2))
                for ci in range(16):
                    t.matmul(ps[4][0:1, 0:128], sb["onesb"][:, :], ocb[:, ci * 128:(ci + 1) * 128],
                             start=(ci == 0), stop=(ci == 15))
                for ci in range(16):
                    ti = t.matmul(ps[5][0:1, 0:128], sb["onesb"][:, :], osq[:, ci * 128:(ci + 1) * 128],
                                  start=(ci == 0), stop=(ci == 15))
                ti.then_inc(sem["MM"], 1)
                t.wait_ge(sem["EV"], vP(j, 3))
                t.matmul(ps[6][:, 0:128], sb["onesrow"][0:1, :], mu1[0:1, :], start=True, stop=True)
                ti = t.matmul(ps[6][:, 128:256], sb["onesrow"][0:1, :], rs1[0:1, :], start=True, stop=True)
                ti.then_inc(sem["MM"], 1)
            C.inc("MM", 3 * NCH)

        @blk.vector
        def _(v):
            for j in range(NCH):
                p = j % 2
                sname = "LA" if p == 0 else "LB"
                v.wait_ge(sem["MM"], mP(j, 1))
                v.wait_ge(sem[sname], (labP if p == 0 else lbbP) + (j // 2) * 64 + 64)
                if j >= 1:
                    v.wait_ge(sem["SC"], sP(j - 1, 3))
                    v.wait_ge(sem["MM"], mP(j - 1, 2))
                for ci in range(16):
                    v.tensor_tensor(ocb[:, ci * 128:(ci + 1) * 128],
                                    oir[p][:, ci * 128:(ci + 1) * 128],
                                    ps[ci % 4][:, (ci // 4) * 128:(ci // 4) * 128 + 128], ALU.add)
                v.drain()
                v.sem_inc(sem["EV"], 1)
                v.wait_ge(sem["MM"], mP(j, 2))
                v.tensor_scalar_mul(mu1[0:1, :], ps[4][0:1, 0:128], float(1.0 / 2048.0))
                v.tensor_scalar_mul(rs1[0:1, :], ps[5][0:1, 0:128], float(1.0 / 2048.0))
                v.drain()
                v.tensor_tensor(Tcw[0:1, 0:128], mu1[0:1, :], mu1[0:1, :], ALU.mult)
                v.drain()
                v.tensor_tensor(rs1[0:1, :], rs1[0:1, :], Tcw[0:1, 0:128], ALU.subtract)
                v.drain()
                v.sem_inc(sem["EV"], 1)
                v.wait_ge(sem["SC"], sP(j, 2))
                v.reciprocal(rs1[0:1, :], rs1[0:1, :])
                v.drain()
                v.sem_inc(sem["EV"], 1)
                v.wait_ge(sem["MM"], mP(j, 3))
                for ci in range(16):
                    v.tensor_copy(muw[:, ci * 128:(ci + 1) * 128], ps[6][:, 0:128])
                    v.tensor_copy(rw[:, ci * 128:(ci + 1) * 128], ps[6][:, 128:256])
                v.drain()
                v.tensor_tensor(Tcw[:, :], ocb[:, :], muw[:, :], ALU.subtract)
                v.drain()
                v.tensor_tensor(Tcw[:, :], Tcw[:, :], rw[:, :], ALU.mult)
                v.drain()
                v.sem_inc(sem["EV"], 1)
            C.inc("EV", 4 * NCH)

        @blk.scalar
        def _(s):
            for j in range(NCH):
                s.wait_ge(sem["EV"], vP(j, 1))
                s.activation(osq[:, :], ocb[:, :], AF.Square)
                s.drain()
                s.sem_inc(sem["SC"], 1)
                s.wait_ge(sem["EV"], vP(j, 2))
                s.activation(rs1[0:1, :], rs1[0:1, :], AF.Sqrt, bias=sb["cbias"][0:1, 1:2])
                s.drain()
                s.sem_inc(sem["SC"], 1)
                s.wait_ge(sem["EV"], vP(j, 4))
                for ci in range(16):
                    s.activation(ynb[:, ci * TOK + j * 128:ci * TOK + j * 128 + 128],
                                 Tcw[:, ci * 128:(ci + 1) * 128],
                                 AF.Identity, scale=sb["lng"][:, ci:ci + 1], bias=sb["lnb"][:, ci:ci + 1])
                s.drain()
                s.sem_inc(sem["SC"], 1)
            C.inc("SC", 3 * NCH)
    dump_cb("yn", ynb, 16)

    OW = contextlib.ExitStack()
    wstr2 = [alloc(OW, "wstr2_%d" % i, [128, 2048], BF16) for i in range(2)]
    sbuf_s = alloc(OW, "sbuf_s", [128, NCB * TOK], BF16)
    with nc.Block() as blk:
        @blk.gpsimd
        def _(g):
            mmb = C.get("MM")
            for oc in range(16):
                if oc >= 2:
                    g.wait_ge(sem["MM"], mmb + oc - 1)
                strip_load(g, wstr2[oc % 2], d_ow[oc], s="WA" if oc % 2 == 0 else "WB")
            C.inc("WA", 16 * 8)
            C.inc("WB", 16 * 8)

        @blk.tensor
        def _(t):
            wlba = C.get("WA") - 16 * 8
            wlbb = C.get("WB") - 16 * 8
            evb = C.get("EV")
            for oc in range(16):
                t.wait_ge(sem["WA" if oc % 2 == 0 else "WB"],
                          (wlba if oc % 2 == 0 else wlbb) + 16 * (oc // 2 + 1))
                pp = ps[(oc % 2) * 2: (oc % 2) * 2 + 2]
                if oc >= 2:
                    t.wait_ge(sem["EV"], evb + oc - 1)
                for ci in range(16):
                    for hf in range(2):
                        t.matmul(pp[hf][:, :], wstr2[oc % 2][:, ci * 128:(ci + 1) * 128],
                                 blk_c(ynb, ci)[:, hf * 512:(hf + 1) * 512],
                                 start=(ci == 0), stop=(ci == 15))
                t.drain()
                t.sem_inc(sem["MM"], 1)
            C.inc("MM", 16)

        @blk.vector
        def _(v):
            mmb = C.get("MM") - 16
            for oc in range(16):
                v.wait_ge(sem["MM"], mmb + oc + 1)
                pp = ps[(oc % 2) * 2: (oc % 2) * 2 + 2]
                for hf in range(2):
                    v.tensor_copy(blk_c(sbuf_s, oc)[:, hf * 512:(hf + 1) * 512], pp[hf][:, :])
                v.drain()
                v.sem_inc(sem["EV"], 1)
            C.inc("EV", 16)

        @blk.sync
        def _(sync):
            sync.wait_ge(sem["EV"], C.get("EV"))
            sync.dma_start(out=scr["s"], in_=sbuf_s[:, :]).then_inc(sem["SP"], 16)
            sync.wait_ge(sem["SP"], C.inc("SP", 16))
    dump_cb("s", sbuf_s, 16)
    OW.close()
    PB.close()

    if PHASES <= 5:
        return _early_out()

    LS.close()

    PC = contextlib.ExitStack()
    zb = alloc(PC, "zb", [128, 64 * TOK], BF16)
    UG = contextlib.ExitStack()
    x1b = alloc(UG, "x1b", [128, NCB * TOK], BF16)
    alloc(UG, "Hdp", [128, 128], BF16)
    wstr3 = [alloc(UG, "wstr3_%d" % i, [128, 2048], BF16) for i in range(4)]
    hb = alloc(UG, "hb", [128, TOK], BF16)
    sgb = alloc(UG, "sgb", [128, TOK], F32)

    with nc.Block() as blk:
        @blk.sync
        def _(sync):
            sync.dma_start(out=x1b[:, :], in_=scr["x1"]).then_inc(sem["LD"], 16)
            C.inc("LD", 16)

        @blk.gpsimd
        def _(g):
            mmb = C.get("MM")
            for ic in range(64):
                if ic >= 2:
                    g.wait_ge(sem["MM"], mmb + 2 * (ic - 1))
                sname = "WA" if ic % 2 == 0 else "WB"
                strip_load(g, wstr3[(ic % 2) * 2], d_wup[ic], s=sname)
                strip_load(g, wstr3[(ic % 2) * 2 + 1], d_wg[ic], s=sname)
            C.inc("WA", 32 * 32)
            C.inc("WB", 32 * 32)

        @blk.tensor
        def _(t):
            t.wait_ge(sem["LD"], C.get("LD"))
            wlba = C.get("WA") - 32 * 32
            wlbb = C.get("WB") - 32 * 32
            evb = C.get("EV")
            scb = C.get("SC")
            for ic in range(64):
                t.wait_ge(sem["WA" if ic % 2 == 0 else "WB"],
                          (wlba if ic % 2 == 0 else wlbb) + 32 * (ic // 2 + 1))
                wu, wg = wstr3[(ic % 2) * 2], wstr3[(ic % 2) * 2 + 1]
                if ic >= 1:
                    t.wait_ge(sem["EV"], evb + 2 * ic - 1)
                    t.wait_ge(sem["SC"], scb + ic)
                for ci in range(16):
                    for hf in range(2):
                        t.matmul(ps[hf][:, :], wu[:, ci * 128:(ci + 1) * 128],
                                 blk_c(x1b, ci)[:, hf * 512:(hf + 1) * 512],
                                 start=(ci == 0), stop=(ci == 15))
                        t.matmul(ps[2 + hf][:, :], wg[:, ci * 128:(ci + 1) * 128],
                                 blk_c(x1b, ci)[:, hf * 512:(hf + 1) * 512],
                                 start=(ci == 0), stop=(ci == 15))
                t.drain()
                t.sem_inc(sem["MM"], 1)
                t.wait_ge(sem["EV"], evb + 2 * ic + 1)
                for hf in range(2):
                    t.matmul(ps[4 + hf][:, :], sb["Hdp"][:, :], hb[:, hf * 512:(hf + 1) * 512],
                             start=True, stop=True)
                t.drain()
                t.sem_inc(sem["MM"], 1)
            C.inc("MM", 128)

        @blk.scalar
        def _(s):
            mmb = C.get("MM") - 128
            evb = C.get("EV")
            for ic in range(64):
                s.wait_ge(sem["MM"], mmb + 2 * ic + 1)
                if ic >= 1:
                    s.wait_ge(sem["EV"], evb + 2 * ic - 1)
                for hf in range(2):
                    s.activation(sgb[:, hf * 512:(hf + 1) * 512], ps[2 + hf][:, :],
                                 AF.Silu, scale=sb["s_g"][:, ic:ic + 1])
                s.drain()
                s.sem_inc(sem["SC"], 1)
            C.inc("SC", 64)

        @blk.vector
        def _(v):
            mmb = C.get("MM") - 128
            scb = C.get("SC") - 64
            for ic in range(64):
                v.wait_ge(sem["SC"], scb + ic + 1)
                v.tensor_scalar_mul(sb["Hdp"][:, :], sb["H128b"][:, :], sb["sudp"][:, ic:ic + 1])
                for hf in range(2):
                    v.scalar_tensor_tensor(hb[:, hf * 512:(hf + 1) * 512], ps[hf][:, :],
                                           sb["s_up"][:, ic:ic + 1],
                                           sgb[:, hf * 512:(hf + 1) * 512], ALU.mult, ALU.mult)
                v.drain()
                v.sem_inc(sem["EV"], 1)
                v.wait_ge(sem["MM"], mmb + 2 * ic + 2)
                for hf in range(2):
                    v.tensor_copy(zb[:, ic * TOK + hf * 512: ic * TOK + (hf + 1) * 512], ps[4 + hf][:, :])
                v.drain()
                v.sem_inc(sem["EV"], 1)
            C.inc("EV", 128)

    BF = contextlib.ExitStack()
    ztp = [alloc(BF, "ztp%d" % i, [128, TOK], BF16) for i in range(8)]
    with nc.Block() as blk:
        @blk.vector
        def _(v):
            for d in [1, 2, 4, 8, 16, 32]:
                pairs = [i for i in range(64) if not (i & d)]
                for gs in range(0, 32, 8):
                    grp = pairs[gs:gs + 8]
                    for kk, i in enumerate(grp):
                        v.tensor_tensor(ztp[kk][:, :], zb[:, i * TOK:(i + 1) * TOK],
                                        zb[:, (i + d) * TOK:(i + d + 1) * TOK], ALU.add)
                    v.drain()
                    for i in grp:
                        jb = i + d
                        v.tensor_tensor(zb[:, jb * TOK:(jb + 1) * TOK],
                                        zb[:, i * TOK:(i + 1) * TOK],
                                        zb[:, jb * TOK:(jb + 1) * TOK], ALU.subtract)
                    v.drain()
                    for kk, i in enumerate(grp):
                        v.tensor_copy(zb[:, i * TOK:(i + 1) * TOK], ztp[kk][:, :])
                    v.drain()
            v.sem_inc(sem["EV"], 1)
            C.inc("EV", 1)
    BF.close()
    if DEBUG:
        with nc.Block() as blk:
            @blk.scalar
            def _(s):
                spb = C.get("SP")
                for ib in range(64):
                    if ib:
                        s.wait_ge(sem["SP"], spb + 16 * ib)
                    s.activation(sb["dbuf"][:, :], zb[:, ib * TOK:(ib + 1) * TOK], AF.Copy)
                    s.drain()
                    s.sem_inc(sem["SC"], 1)
                C.inc("SC", 64)

            @blk.sync
            def _(sync):
                scb = C.get("SC") - 64
                for ib in range(64):
                    sync.wait_ge(sem["SC"], scb + ib + 1)
                    sync.dma_start(out=dbg["z"][:, ib * TOK:(ib + 1) * TOK],
                                   in_=sb["dbuf"][:, :]).then_inc(sem["SP"], 16)
                sync.wait_ge(sem["SP"], C.inc("SP", 16 * 64))
    UG.close()

    DN = contextlib.ExitStack()
    gcomb = alloc(DN, "gcomb", [128, TOK], F32)
    srel = alloc(DN, "srel", [128, TOK], BF16)
    xrel = alloc(DN, "xrel", [128, TOK], F32)
    outb = alloc(DN, "outb", [128, TOK], F32)
    wstr4 = [alloc(DN, "wstr4_%d" % i, [128, 64 * 128], BF16) for i in range(2)]

    with nc.Block() as blk:
        @blk.gpsimd
        def _(g):
            mmb = C.get("MM")
            for fc in range(16):
                if fc >= 2:
                    g.wait_ge(sem["MM"], mmb + fc - 1)
                strip_load(g, wstr4[fc % 2], d_wd[fc], ntile=64, s="WA" if fc % 2 == 0 else "WB")
            C.inc("WA", 16 * 8)
            C.inc("WB", 16 * 8)

        @blk.sync
        def _(sync):
            evb = C.get("EV")
            for fc in range(16):
                if fc >= 1:
                    sync.wait_ge(sem["EV"], evb + 2 * fc)
                sync.dma_start(out=srel[:, :], in_=scr["s"][:, fc * TOK:(fc + 1) * TOK]).then_inc(sem["LD"], 16)
                sync.dma_start(out=xrel[:, :], in_=d_xT[fc * 128:(fc + 1) * 128, :]).then_inc(sem["LD"], 16)
            C.inc("LD", 32 * 16)

        @blk.tensor
        def _(t):
            wlba = C.get("WA") - 16 * 8
            wlbb = C.get("WB") - 16 * 8
            scb = C.get("SC")
            for fc in range(16):
                t.wait_ge(sem["WA" if fc % 2 == 0 else "WB"],
                          (wlba if fc % 2 == 0 else wlbb) + 16 * (fc // 2 + 1))
                pp = ps[(fc % 2) * 2: (fc % 2) * 2 + 2]
                if fc >= 2:
                    t.wait_ge(sem["SC"], scb + fc - 1)
                for cj in range(64):
                    for hf in range(2):
                        t.matmul(pp[hf][:, :], wstr4[fc % 2][:, cj * 128:(cj + 1) * 128],
                                 zb[:, cj * TOK + hf * 512: cj * TOK + (hf + 1) * 512],
                                 start=(cj == 0), stop=(cj == 63))
                t.drain()
                t.sem_inc(sem["MM"], 1)
            C.inc("MM", 16)

        @blk.scalar
        def _(s):
            mmb = C.get("MM") - 16
            evb = C.get("EV")
            for fc in range(16):
                s.wait_ge(sem["MM"], mmb + fc + 1)
                if fc >= 1:
                    s.wait_ge(sem["EV"], evb + 2 * fc - 1)
                pp = ps[(fc % 2) * 2: (fc % 2) * 2 + 2]
                for hf in range(2):
                    s.activation(gcomb[:, hf * 512:(hf + 1) * 512], pp[hf][:, :],
                                 AF.Silu, scale=sb["s_dp"][:, fc:fc + 1])
                s.drain()
                s.sem_inc(sem["SC"], 1)
                s.wait_ge(sem["EV"], evb + 2 * fc + 2)
                s.dma_start(out=d_out[fc * 128:(fc + 1) * 128, :], in_=outb[:, :]).then_inc(sem["SP"], 16)
            C.inc("SC", 16)
            s.wait_ge(sem["SP"], C.inc("SP", 16 * 16))

        @blk.vector
        def _(v):
            scb = C.get("SC") - 16
            ldb = C.get("LD") - 32 * 16
            spb = C.get("SP") - 16 * 16
            for fc in range(16):
                v.wait_ge(sem["SC"], scb + fc + 1)
                v.wait_ge(sem["LD"], ldb + 32 * (fc + 1))
                if fc >= 1:
                    v.wait_ge(sem["SP"], spb + 16 * fc)
                v.tensor_tensor(outb[:, :], gcomb[:, :], srel[:, :], ALU.mult)
                v.drain()
                v.sem_inc(sem["EV"], 1)
                v.tensor_tensor(outb[:, :], outb[:, :], xrel[:, :], ALU.add)
                v.drain()
                v.sem_inc(sem["EV"], 1)
            C.inc("EV", 32)

    DN.close()
    PC.close()
    est.close()
    return nc


_NC_CACHE = {}


def _stage(inputs):
    x = np.asarray(inputs["L0_attn_x"], dtype=np.float32)

    def colvec(v):
        return np.ascontiguousarray(np.asarray(v, np.float32).reshape(16, 128).T)

    H128 = _hadamard(128).astype(np.float32)
    maskT = np.triu(np.ones((128, 128), np.float32))
    mask4 = np.ascontiguousarray(np.concatenate([maskT] * 4, axis=1))

    com = {
        "c_maa_x": colvec(inputs["L0_attn_maa_x"]),
        "c_maa_r": colvec(inputs["L0_attn_maa_r"]),
        "c_maa_k": colvec(inputs["L0_attn_maa_k"]),
        "c_maa_v": colvec(inputs["L0_attn_maa_v"]),
        "c_maa_w": colvec(inputs["L0_attn_maa_w"]),
        "c_tdv": colvec(inputs["L0_attn_time_decay"]),
        "c_qb": colvec(inputs["L0_attn_q_b"]),
        "c_kb": colvec(inputs["L0_attn_k_b"]),
        "c_vb": colvec(inputs["L0_attn_v_b"]),
        "c_lng": colvec(inputs["L0_attn_ln_g"]),
        "c_lnb": colvec(inputs["L0_attn_ln_b"]),
        "c_su": colvec(inputs["L0_ffn_SU_ug"]),
        "c_sudp": np.ascontiguousarray(np.asarray(inputs["L0_ffn_SU_dp"], np.float32).reshape(64, 128).T),
        "c_wscu": np.ascontiguousarray(np.asarray(inputs["L0_ffn_Wscale_ug"], np.float32)[:8192].reshape(64, 128).T),
        "c_wscg": np.ascontiguousarray(np.asarray(inputs["L0_ffn_Wscale_ug"], np.float32)[8192:].reshape(64, 128).T),
        "c_wsdp": colvec(inputs["L0_ffn_Wscale_dp"]),
        "c_H128": H128,
        "c_ident": np.eye(128, dtype=np.float32),
        "c_mask4": mask4,
        "c_onescol": np.ones((128, 1), np.float32),
        "c_onesrow": np.ones((1, 128), np.float32),
        "w_maaw1": np.ascontiguousarray(np.asarray(inputs["L0_attn_maa_w1"], np.float32).reshape(16, 128, 128).transpose(1, 0, 2).reshape(128, 2048)),
        "w_tdw1": np.ascontiguousarray(np.asarray(inputs["L0_attn_td_w1"], np.float32).reshape(16, 128, 64).transpose(1, 0, 2).reshape(128, 1024)),
        "w_tdw2": np.ascontiguousarray(np.asarray(inputs["L0_attn_td_w2"], np.float32)),
    }
    w2 = np.asarray(inputs["L0_attn_maa_w2"], np.float32)
    W2blk = np.zeros((128, 4 * 2048), np.float32)
    for i in range(4):
        W2blk[32 * i:32 * i + 32, 2048 * i:2048 * (i + 1)] = w2[i]
    com["w_w2blk"] = np.ascontiguousarray(W2blk)

    def tile4(w):
        return np.ascontiguousarray(
            np.asarray(w, np.float32).reshape(16, 128, 16, 128).transpose(2, 1, 0, 3).reshape(16, 128, 2048))

    com["w_qw"] = tile4(inputs["L0_attn_q_w"])
    com["w_kw"] = tile4(inputs["L0_attn_k_w"])
    com["w_vw"] = tile4(inputs["L0_attn_v_w"])
    com["w_ow"] = tile4(inputs["L0_attn_o_w"])
    wupT = np.asarray(inputs["L0_ffn_W_up"], np.float32).T
    wgT = np.asarray(inputs["L0_ffn_W_gate"], np.float32).T
    com["w_wup"] = np.ascontiguousarray(wupT.reshape(16, 128, 64, 128).transpose(2, 1, 0, 3).reshape(64, 128, 2048))
    com["w_wg"] = np.ascontiguousarray(wgT.reshape(16, 128, 64, 128).transpose(2, 1, 0, 3).reshape(64, 128, 2048))
    wdT = np.asarray(inputs["L0_ffn_W_down"], np.float32).T
    com["w_wd"] = np.ascontiguousarray(wdT.reshape(64, 128, 16, 128).transpose(2, 1, 0, 3).reshape(16, 128, 8192))

    in_maps = []
    for core in range(8):
        b, qt = core // 4, core % 4
        t0 = qt * TOK
        m = dict(com)
        m["xT"] = np.ascontiguousarray(x[b, t0:t0 + TOK, :].T)
        xp = np.zeros((2048,), np.float32) if t0 == 0 else np.asarray(x[b, t0 - 1, :], np.float32)
        m["xp0"] = np.ascontiguousarray(xp.reshape(1, 2048))
        m["xp0c"] = colvec(xp)
        um = np.zeros((128, 16), np.float32)
        for j in range(7):
            pred = (j // 4 == b) and (j % 4 < qt)
            um[:, j] = 1.0 if pred else 0.0
            um[:, 8 + j] = 0.0 if pred else 1.0
        m["i_umask"] = um
        in_maps.append(m)
    return in_maps


def kernel(**inputs):
    if "nc" not in _NC_CACHE:
        _NC_CACHE["nc"] = build_nc()
    nc = _NC_CACHE["nc"]
    in_maps = _stage(inputs)
    res = run_bass_kernel_spmd(nc, in_maps, core_ids=list(range(8)))
    out = np.zeros((2, 4096, 2048), np.float32)
    for core in range(8):
        b, qt = core // 4, core % 4
        out[b, qt * TOK:(qt + 1) * TOK, :] = res.results[core]["o_out"].T
    if DEBUG:
        kernel._dbg = res.results
    return out




# version 09 (final)
# speedup vs baseline: 1.2669x; 1.0619x over previous
import contextlib
import numpy as np
import concourse.bass as bass
import concourse.mybir as mybir
from concourse.bass_utils import run_bass_kernel_spmd

F32 = mybir.dt.float32
BF16 = mybir.dt.bfloat16
AF = mybir.ActivationFunctionType
ALU = mybir.AluOpType
AXX = mybir.AxisListType.X

TOK = 1024
NCB = 16
L = 128
NCH = 8
DEBUG = False
PHASES = 6
_SKIP_ODD_PT = False
A4SUB = 7


def _hadamard(n):
    h = np.array([[1.0]], dtype=np.float64)
    while h.shape[0] < n:
        h = np.block([[h, h], [h, -h]])
    return h


class Counter:
    def __init__(self):
        self.v = {}

    def inc(self, name, by):
        self.v[name] = self.v.get(name, 0) + by
        return self.v[name]

    def get(self, name):
        return self.v.get(name, 0)


def build_nc():
    nc = bass.Bass("TRN2", target_bir_lowering=False)
    C = Counter()

    d_xT = nc.dram_tensor("xT", [2048, TOK], F32, kind="ExternalInput").ap()
    d_xp0 = nc.dram_tensor("xp0", [1, 2048], F32, kind="ExternalInput").ap()
    d_xp0c = nc.dram_tensor("xp0c", [128, 16], F32, kind="ExternalInput").ap()
    d_umask = nc.dram_tensor("i_umask", [128, 16], F32, kind="ExternalInput").ap()
    vec_names = ["maa_x", "maa_r", "maa_k", "maa_v", "maa_w", "tdv",
                 "qb", "kb", "vb", "lng", "lnb"]
    d_vec = {n: nc.dram_tensor("c_" + n, [128, 16], F32, kind="ExternalInput").ap()
             for n in vec_names}
    d_su = nc.dram_tensor("c_su", [128, 16], F32, kind="ExternalInput").ap()
    d_sudp = nc.dram_tensor("c_sudp", [128, 64], F32, kind="ExternalInput").ap()
    d_wscu = nc.dram_tensor("c_wscu", [128, 64], F32, kind="ExternalInput").ap()
    d_wscg = nc.dram_tensor("c_wscg", [128, 64], F32, kind="ExternalInput").ap()
    d_wsdp = nc.dram_tensor("c_wsdp", [128, 16], F32, kind="ExternalInput").ap()
    d_H128 = nc.dram_tensor("c_H128", [128, 128], F32, kind="ExternalInput").ap()
    d_ident = nc.dram_tensor("c_ident", [128, 128], F32, kind="ExternalInput").ap()
    d_mask4 = nc.dram_tensor("c_mask4", [128, 512], F32, kind="ExternalInput").ap()
    d_ones_col = nc.dram_tensor("c_onescol", [128, 1], F32, kind="ExternalInput").ap()
    d_ones_row = nc.dram_tensor("c_onesrow", [1, 128], F32, kind="ExternalInput").ap()
    d_maaw1 = nc.dram_tensor("w_maaw1", [128, 2048], F32, kind="ExternalInput").ap()
    d_w2blk = nc.dram_tensor("w_w2blk", [128, 8192], F32, kind="ExternalInput").ap()
    d_tdw1 = nc.dram_tensor("w_tdw1", [128, 1024], F32, kind="ExternalInput").ap()
    d_tdw2 = nc.dram_tensor("w_tdw2", [64, 2048], F32, kind="ExternalInput").ap()
    d_qw = nc.dram_tensor("w_qw", [16, 128, 2048], F32, kind="ExternalInput").ap()
    d_kw = nc.dram_tensor("w_kw", [16, 128, 2048], F32, kind="ExternalInput").ap()
    d_vw = nc.dram_tensor("w_vw", [16, 128, 2048], F32, kind="ExternalInput").ap()
    d_ow = nc.dram_tensor("w_ow", [16, 128, 2048], F32, kind="ExternalInput").ap()
    d_wup = nc.dram_tensor("w_wup", [64, 128, 2048], F32, kind="ExternalInput").ap()
    d_wg = nc.dram_tensor("w_wg", [64, 128, 2048], F32, kind="ExternalInput").ap()
    d_wd = nc.dram_tensor("w_wd", [16, 128, 8192], F32, kind="ExternalInput").ap()

    d_out = nc.dram_tensor("o_out", [2048, TOK], F32, kind="ExternalOutput").ap()
    scr = {}
    for n in ["x1", "wexp", "q", "km", "v", "s"]:
        scr[n] = nc.dram_tensor("scr_" + n, [128, NCB * TOK], BF16).ap()
    scr["qt"] = nc.dram_tensor("scr_qt", [128, NCH * 2048], BF16).ap()
    scr["oi"] = nc.dram_tensor("scr_oi", [128, NCH * 2048], BF16).ap()
    d_agin = nc.dram_tensor("ag_in", [2048, 65], F32)
    d_agout = nc.dram_tensor("ag_out", [8 * 2048, 65], F32)

    dbg = {}
    if DEBUG:
        for n, w in [("xn", NCB * TOK), ("dx", NCB * TOK), ("x1", NCB * TOK),
                     ("wexp", NCB * TOK), ("q", NCB * TOK), ("km", NCB * TOK),
                     ("v", NCB * TOK), ("sspan", NCB * 64), ("aspan", 16),
                     ("sin", NCB * 64), ("yn", NCB * TOK), ("s", NCB * TOK),
                     ("z", 64 * TOK), ("mm", TOK)]:
            dbg[n] = nc.dram_tensor("dbg_" + n, [128, w], BF16, kind="ExternalOutput").ap()

    est = contextlib.ExitStack()
    sem = {n: est.enter_context(nc.semaphore(n))
           for n in ["LD", "WL", "WA", "WB", "LA", "LB", "SH", "MM", "EV", "SC", "SP", "CC"]}

    sb = {}

    def alloc(stk, name, shape, dtype):
        t = stk.enter_context(nc.sbuf_tensor(name, shape, dtype))
        sb[name] = t
        return t

    for n in vec_names:
        alloc(est, n, [128, 16], F32)
    for n, sh, dt in [("umask", [128, 16], F32), ("su", [128, 16], F32),
                      ("sudp", [128, 64], F32), ("s_up", [128, 64], F32),
                      ("s_g", [128, 64], F32), ("s_dp", [128, 16], F32),
                      ("H128b", [128, 128], BF16), ("identb", [128, 128], BF16),
                      ("mask4", [128, 512], F32), ("onesb", [128, 1], BF16),
                      ("onesrow", [1, 128], F32),
                      ("xp0col", [128, 16], F32), ("xp0sq", [128, 16], F32),
                      ("pst", [1, 8], F32), ("p128", [128, 2], F32),
                      ("onesf", [128, 1], F32),
                      ("cbias", [128, 4], F32)]:
        alloc(est, n, sh, dt)
    if DEBUG:
        alloc(est, "dbuf", [128, TOK], BF16)
    LS = contextlib.ExitStack()
    for n, sh, dt in [("sfold", [128, 1024], F32),
                      ("S_run", [128, NCB * 64], F32), ("A_run", [128, 16], F32),
                      ("Apref", [128, 128], F32), ("Lsnap", [128, NCH * NCB * 64], BF16),
                      ("ALtmp", [128, 16], F32), ("aggst", [128, 16 * 65], F32)]:
        alloc(LS, n, sh, dt)

    ps = [est.enter_context(nc.psum_tensor("ps%d" % i, [128, 512], F32)) for i in range(7)]
    psb = est.enter_context(nc.psum_tensor("psb", [128, 1024], BF16))

    def blk_c(t, ci):
        return t[:, ci * TOK:(ci + 1) * TOK]

    def strip_load(g, dst, src_oc, ntile=16, s="WL"):
        g.dma_start(out=dst[:, 0:ntile * 128], in_=src_oc).then_inc(sem[s], 16)
        return 16

    def dump_cb(name, src, nblk):
        if not DEBUG:
            return
        with nc.Block() as blk:
            @blk.scalar
            def _(s):
                base_sp = C.get("SP")
                for ci in range(nblk):
                    if ci:
                        s.wait_ge(sem["SP"], base_sp + 16 * ci)
                    s.activation(sb["dbuf"][:, :], src[:, ci * TOK:(ci + 1) * TOK], AF.Copy)
                    s.drain()
                    s.sem_inc(sem["SC"], 1)
                C.inc("SC", nblk)

            @blk.sync
            def _(sync):
                base_sc = C.get("SC") - nblk
                for ci in range(nblk):
                    sync.wait_ge(sem["SC"], base_sc + ci + 1)
                    sync.dma_start(out=dbg[name][:, ci * TOK:(ci + 1) * TOK],
                                   in_=sb["dbuf"][:, :]).then_inc(sem["SP"], 16)
                sync.wait_ge(sem["SP"], C.inc("SP", 16 * nblk))

    def dump_small(name, src, width):
        if not DEBUG:
            return
        with nc.Block() as blk:
            @blk.vector
            def _(v):
                v.tensor_copy(sb["dbuf"][:, 0:width], src)
                v.drain()
                v.sem_inc(sem["EV"], 1)
                C.inc("EV", 1)

            @blk.sync
            def _(sync):
                sync.wait_ge(sem["EV"], C.get("EV"))
                sync.dma_start(out=dbg[name][:, 0:width], in_=sb["dbuf"][:, 0:width]).then_inc(sem["SP"], 16)
                sync.wait_ge(sem["SP"], C.inc("SP", 16))

    with nc.Block() as blk:
        @blk.sync
        def _(sync):
            n_loads = 0
            for n in vec_names:
                sync.dma_start(out=sb[n][:, :], in_=d_vec[n]).then_inc(sem["LD"], 16)
                n_loads += 1
            for n, d in [("umask", d_umask), ("su", d_su), ("sudp", d_sudp),
                         ("s_up", d_wscu), ("s_g", d_wscg), ("s_dp", d_wsdp),
                         ("mask4", d_mask4), ("onesrow", d_ones_row),
                         ("onesf", d_ones_col), ("xp0col", d_xp0c)]:
                sync.dma_start(out=sb[n][:, :], in_=d).then_inc(sem["LD"], 16)
                n_loads += 1
            C.inc("LD", 16 * n_loads)

        @blk.gpsimd
        def _(g):
            g.dma_start(out=sb["H128b"][:, :], in_=d_H128).then_inc(sem["WL"], 16)
            g.dma_start(out=sb["identb"][:, :], in_=d_ident).then_inc(sem["WL"], 16)
            g.dma_start(out=sb["onesb"][:, :], in_=d_ones_col).then_inc(sem["WL"], 16)
            C.inc("WL", 48)

        @blk.vector
        def _(v):
            v.wait_ge(sem["LD"], C.get("LD"))
            v.wait_ge(sem["WL"], C.get("WL"))
            v.tensor_scalar_mul(sb["s_up"][:, :], sb["s_up"][:, :], float(1.0 / np.sqrt(2048.0)))
            v.tensor_scalar_mul(sb["s_g"][:, :], sb["s_g"][:, :], float(1.0 / np.sqrt(2048.0)))
            v.tensor_scalar_mul(sb["s_dp"][:, :], sb["s_dp"][:, :], float(1.0 / np.sqrt(8192.0)))
            v.memset(sb["S_run"][:, :], 0.0)
            v.memset(sb["A_run"][:, :], 1.0)
            v.memset(sb["cbias"][:, 0:1], 1e-5)
            v.memset(sb["cbias"][:, 1:2], 64e-5)
            v.memset(sb["cbias"][:, 2:3], 1.0)
            v.drain()
            v.sem_inc(sem["EV"], 1)
            C.inc("EV", 1)


    def _early_out(close_a1b=False):
        with nc.Block() as blk:
            @blk.sync
            def _(sync):
                sync.dma_start(out=d_out[0:128, :], in_=sb["S_run"][:, :]).then_inc(sem["SP"], 16)
                sync.wait_ge(sem["SP"], C.inc("SP", 16))
        if close_a1b:
            A1b.close()
        LS.close()
        est.close()
        return nc

    A1b = contextlib.ExitStack()
    A1a = contextlib.ExitStack()
    xn = alloc(A1b, "xn", [128, NCB * TOK], BF16)
    dx = alloc(A1b, "dx", [128, NCB * TOK], BF16)
    alloc(A1b, "muB", [128, TOK], F32)
    alloc(A1b, "rB", [128, TOK], F32)
    xbuf = alloc(A1a, "xbuf", [128, NCB * TOK], F32)
    alloc(A1a, "statrow", [1, TOK], F32)
    alloc(A1a, "statrow2", [1, TOK], F32)
    alloc(A1a, "T1", [128, TOK], F32)
    alloc(A1a, "T2", [128, TOK], F32)
    xbb = xn

    with nc.Block() as blk:
        @blk.sync
        def _(sync):
            for ci in range(16):
                sync.dma_start(out=blk_c(xbuf, ci), in_=d_xT[ci * 128:(ci + 1) * 128, :]).then_inc(sem["LD"], 16)
            C.inc("LD", 16 * 16)

        @blk.scalar
        def _(s):
            s.wait_ge(sem["LD"], C.get("LD"))
            for ci in range(16):
                s.activation(blk_c(xbb, ci), blk_c(xbuf, ci), AF.Copy)
            s.activation(sb["xp0sq"][:, :], sb["xp0col"][:, :], AF.Square)
            s.drain()
            s.sem_inc(sem["SC"], 1)
            C.inc("SC", 1)

        @blk.tensor
        def _(t):
            t.wait_ge(sem["SC"], C.get("SC"))
            for hf in range(2):
                for ci in range(16):
                    t.matmul(ps[0 + hf][0:1, :], sb["onesb"][:, :],
                             blk_c(xbb, ci)[:, hf * 512:(hf + 1) * 512],
                             start=(ci == 0), stop=(ci == 15))
            t.matmul(ps[4][0:1, 0:16], sb["onesf"][:, :], sb["xp0col"][:, :], start=True, stop=True)
            t.matmul(ps[4][0:1, 16:32], sb["onesf"][:, :], sb["xp0sq"][:, :], start=True, stop=True)
            t.drain()
            t.sem_inc(sem["MM"], 1)
            C.inc("MM", 1)

        @blk.scalar
        def _(s):
            s.wait_ge(sem["MM"], C.get("MM"))
            for ci in range(16):
                s.activation(blk_c(xbb, ci), blk_c(xbuf, ci), AF.Square)
            s.drain()
            s.sem_inc(sem["SC"], 1)
            C.inc("SC", 1)

        @blk.tensor
        def _(t):
            t.wait_ge(sem["SC"], C.get("SC"))
            for hf in range(2):
                for ci in range(16):
                    t.matmul(ps[2 + hf][0:1, :], sb["onesb"][:, :],
                             blk_c(xbb, ci)[:, hf * 512:(hf + 1) * 512],
                             start=(ci == 0), stop=(ci == 15))
            t.drain()
            t.sem_inc(sem["MM"], 1)
            C.inc("MM", 1)

        @blk.vector
        def _(v):
            v.wait_ge(sem["MM"], C.get("MM"))
            for hf in range(2):
                sl = slice(hf * 512, (hf + 1) * 512)
                v.tensor_scalar_mul(sb["statrow"][0:1, sl], ps[0 + hf][0:1, :], float(1.0 / 2048.0))
                v.tensor_scalar_mul(sb["statrow2"][0:1, sl], ps[2 + hf][0:1, :], float(1.0 / 2048.0))
            v.reduce_sum(sb["pst"][0:1, 0:1], ps[4][0:1, 0:16], axis=AXX)
            v.reduce_sum(sb["pst"][0:1, 1:2], ps[4][0:1, 16:32], axis=AXX)
            v.drain()
            v.tensor_tensor(sb["T1"][0:1, 0:TOK], sb["statrow"][0:1, :], sb["statrow"][0:1, :], ALU.mult)
            v.tensor_scalar_mul(sb["pst"][0:1, 0:2], sb["pst"][0:1, 0:2], float(1.0 / 2048.0))
            v.drain()
            v.tensor_tensor(sb["statrow2"][0:1, :], sb["statrow2"][0:1, :], sb["T1"][0:1, 0:TOK], ALU.subtract)
            v.tensor_tensor(sb["pst"][0:1, 2:3], sb["pst"][0:1, 0:1], sb["pst"][0:1, 0:1], ALU.mult)
            v.drain()
            v.tensor_tensor(sb["pst"][0:1, 1:2], sb["pst"][0:1, 1:2], sb["pst"][0:1, 2:3], ALU.subtract)
            v.drain()
            v.sem_inc(sem["EV"], 1)
            C.inc("EV", 1)

        @blk.scalar
        def _(s):
            s.wait_ge(sem["EV"], C.get("EV"))
            s.activation(sb["statrow2"][0:1, :], sb["statrow2"][0:1, :], AF.Sqrt, bias=sb["cbias"][0:1, 0:1])
            s.activation(sb["pst"][0:1, 1:2], sb["pst"][0:1, 1:2], AF.Sqrt, bias=sb["cbias"][0:1, 0:1])
            s.drain()
            s.sem_inc(sem["SC"], 1)
            C.inc("SC", 1)

        @blk.vector
        def _(v):
            v.wait_ge(sem["SC"], C.get("SC"))
            v.reciprocal(sb["statrow2"][0:1, :], sb["statrow2"][0:1, :])
            v.reciprocal(sb["pst"][0:1, 1:2], sb["pst"][0:1, 1:2])
            v.drain()
            v.sem_inc(sem["EV"], 1)
            C.inc("EV", 1)

        @blk.tensor
        def _(t):
            t.wait_ge(sem["EV"], C.get("EV"))
            for hf in range(2):
                sl = slice(hf * 512, (hf + 1) * 512)
                t.matmul(ps[0 + hf][:, :], sb["onesrow"][0:1, :], sb["statrow"][0:1, sl], start=True, stop=True)
                t.matmul(ps[2 + hf][:, :], sb["onesrow"][0:1, :], sb["statrow2"][0:1, sl], start=True, stop=True)
            t.matmul(ps[4][:, 0:1], sb["onesrow"][0:1, :], sb["pst"][0:1, 0:1], start=True, stop=True)
            t.matmul(ps[4][:, 1:2], sb["onesrow"][0:1, :], sb["pst"][0:1, 1:2], start=True, stop=True)
            t.drain()
            t.sem_inc(sem["MM"], 1)
            C.inc("MM", 1)

        @blk.vector
        def _(v):
            v.wait_ge(sem["MM"], C.get("MM"))
            for hf in range(2):
                sl = slice(hf * 512, (hf + 1) * 512)
                v.tensor_copy(sb["muB"][:, sl], ps[0 + hf][:, :])
                v.tensor_copy(sb["rB"][:, sl], ps[2 + hf][:, :])
            v.tensor_copy(sb["p128"][:, 0:2], ps[4][:, 0:2])
            v.drain()
            v.tensor_scalar(sb["xp0col"][:, :], sb["xp0col"][:, :],
                            sb["p128"][:, 0:1], sb["p128"][:, 1:2], ALU.subtract, ALU.mult)
            for cp in range(8):
                a, b = 2 * cp, 2 * cp + 1
                v.tensor_tensor(sb["T1"][:, :], blk_c(xbuf, a), sb["muB"][:, :], ALU.subtract)
                v.tensor_tensor(sb["T2"][:, :], blk_c(xbuf, b), sb["muB"][:, :], ALU.subtract)
                v.drain()
                v.tensor_tensor(blk_c(xn, a), sb["T1"][:, :], sb["rB"][:, :], ALU.mult)
                v.tensor_tensor(blk_c(xn, b), sb["T2"][:, :], sb["rB"][:, :], ALU.mult)
                v.drain()
            for ci in range(16):
                v.tensor_tensor(blk_c(dx, ci)[:, 1:TOK], blk_c(xn, ci)[:, 0:TOK - 1],
                                blk_c(xn, ci)[:, 1:TOK], ALU.subtract)
                v.tensor_tensor(blk_c(dx, ci)[:, 0:1], sb["xp0col"][:, ci:ci + 1],
                                blk_c(xn, ci)[:, 0:1], ALU.subtract)
            v.drain()
            v.sem_inc(sem["EV"], 1)
            C.inc("EV", 1)
    A1a.close()
    dump_cb("xn", xn, 16)
    dump_cb("dx", dx, 16)

    A2 = contextlib.ExitStack()
    F0 = alloc(A2, "F0", [128, 16 * 512], F32)
    F1 = alloc(A2, "F1", [128, 16 * 512], F32)
    x1t = alloc(A2, "x1t", [128, NCB * TOK], BF16)
    alloc(A2, "H128su", [128, 16 * 128], BF16)

    with nc.Block() as blk:
        @blk.vector
        def _(v):
            for ci in range(16):
                v.tensor_scalar_mul(sb["H128su"][:, ci * 128:(ci + 1) * 128],
                                    sb["H128b"][:, :], sb["su"][:, ci:ci + 1])
            v.drain()
            v.sem_inc(sem["EV"], 1)
            C.inc("EV", 1)

    for hf in range(2):
        with nc.Block() as blk:
            @blk.tensor
            def _(t):
                evb = C.get("EV")
                for ci in range(16):
                    if ci >= 4:
                        t.wait_ge(sem["EV"], evb + ci - 3)
                    t.matmul(ps[ci % 4][:, :], sb["H128su"][:, ci * 128:(ci + 1) * 128],
                             blk_c(xn, ci)[:, hf * 512:(hf + 1) * 512], start=True, stop=True)
                    t.drain()
                    t.sem_inc(sem["MM"], 1)
                C.inc("MM", 16)

            @blk.vector
            def _(v):
                mmb = C.get("MM") - 16
                for ci in range(16):
                    v.wait_ge(sem["MM"], mmb + ci + 1)
                    v.tensor_copy(F0[:, ci * 512:(ci + 1) * 512], ps[ci % 4][:, :])
                    v.drain()
                    v.sem_inc(sem["EV"], 1)
                C.inc("EV", 16)
                v.drain()
                bufs = [F0, F1]
                for si, d in enumerate([1, 2, 4]):
                    src, dst = bufs[si % 2], bufs[(si + 1) % 2]
                    for i in range(16):
                        if i & d:
                            continue
                        j = i + d
                        a = src[:, i * 512:(i + 1) * 512]
                        b = src[:, j * 512:(j + 1) * 512]
                        v.tensor_tensor(dst[:, i * 512:(i + 1) * 512], a, b, ALU.add)
                        v.tensor_tensor(dst[:, j * 512:(j + 1) * 512], a, b, ALU.subtract)
                    v.drain()
                src = bufs[1]
                for i in range(16):
                    if i & 8:
                        continue
                    j = i + 8
                    a = src[:, i * 512:(i + 1) * 512]
                    b = src[:, j * 512:(j + 1) * 512]
                    v.tensor_tensor(x1t[:, i * TOK + hf * 512: i * TOK + hf * 512 + 512], a, b, ALU.add)
                    v.tensor_tensor(x1t[:, j * TOK + hf * 512: j * TOK + hf * 512 + 512], a, b, ALU.subtract)
                v.drain()
                v.sem_inc(sem["EV"], 1)
                C.inc("EV", 1)

    with nc.Block() as blk:
        @blk.sync
        def _(sync):
            sync.dma_start(out=scr["x1"], in_=x1t[:, :]).then_inc(sem["SP"], 16)
            sync.wait_ge(sem["SP"], C.inc("SP", 16))
    dump_cb("x1", x1t, 16)
    A2.close()

    if PHASES <= 1:
        return _early_out(close_a1b=True)

    A3 = contextlib.ExitStack()
    alloc(A3, "mmt", [128, TOK], BF16)
    alloc(A3, "tht", [128, TOK], BF16)
    xxx = alloc(A3, "xxx", [128, TOK], BF16)
    mx = alloc(A3, "mx", [128, NCB * TOK], BF16)
    pout = alloc(A3, "pout", [128, NCB * TOK], BF16)
    wstr = [alloc(A3, "wstr%d" % i, [128, 16 * 128], BF16) for i in range(2)]
    wstr2 = [alloc(A3, "wstr2_%d" % i, [128, 16 * 128], BF16) for i in range(2)]
    Ta = alloc(A3, "Ta", [128, 512], F32)
    Tb = alloc(A3, "Tb", [128, 512], F32)
    Tka = alloc(A3, "Tka", [128, 512], F32)
    Tkb = alloc(A3, "Tkb", [128, 512], F32)
    sb["muB2k"] = sb["muB"]
    sb["rB2k"] = sb["rB"]

    with nc.Block() as blk:
        @blk.gpsimd
        def _(g):
            g.dma_start(out=wstr[0][:, :], in_=d_maaw1).then_inc(sem["WL"], 16)
            C.inc("WL", 16)

        @blk.vector
        def _(v):
            mmb = C.get("MM")
            for ci in range(16):
                if ci >= 1:
                    v.wait_ge(sem["MM"], mmb + ci)
                v.scalar_tensor_tensor(xxx[:, :], blk_c(dx, ci), sb["maa_x"][:, ci:ci + 1],
                                       blk_c(xn, ci), ALU.mult, ALU.add)
                v.drain()
                v.sem_inc(sem["EV"], 1)
            C.inc("EV", 16)

        @blk.tensor
        def _(t):
            t.wait_ge(sem["WL"], C.get("WL"))
            evb = C.get("EV") - 16
            for ci in range(16):
                t.wait_ge(sem["EV"], evb + ci + 1)
                for hf in range(2):
                    t.matmul(ps[4 + hf][:, :], wstr[0][:, ci * 128:(ci + 1) * 128],
                             xxx[:, hf * 512:(hf + 1) * 512], start=(ci == 0), stop=(ci == 15))
                t.drain()
                t.sem_inc(sem["MM"], 1)
            C.inc("MM", 16)

        @blk.scalar
        def _(s):
            s.wait_ge(sem["MM"], C.get("MM"))
            for hf in range(2):
                s.activation(sb["mmt"][:, hf * 512:(hf + 1) * 512], ps[4 + hf][:, :], AF.Tanh)
            s.drain()
            s.sem_inc(sem["SC"], 1)
            C.inc("SC", 1)
    dump_cb("mm", sb["mmt"], 1)

    def mix_block(tgt_idx, vecname):
        with nc.Block() as blk:
            @blk.gpsimd
            def _(g):
                g.dma_start(out=wstr[0][:, :], in_=d_w2blk[:, tgt_idx * 2048:(tgt_idx + 1) * 2048]).then_inc(sem["WL"], 16)
                C.inc("WL", 16)

            @blk.tensor
            def _(t):
                t.wait_ge(sem["WL"], C.get("WL"))
                t.wait_ge(sem["SC"], C.get("SC"))
                evb = C.get("EV")
                for oc in range(16):
                    pp = ps[(oc % 2) * 2: (oc % 2) * 2 + 2]
                    if oc >= 2:
                        t.wait_ge(sem["EV"], evb + oc - 1)
                    for hf in range(2):
                        t.matmul(pp[hf][:, :], wstr[0][:, oc * 128:(oc + 1) * 128],
                                 sb["mmt"][:, hf * 512:(hf + 1) * 512], start=True, stop=True)
                    t.drain()
                    t.sem_inc(sem["MM"], 1)
                C.inc("MM", 16)

            @blk.vector
            def _(v):
                mmb = C.get("MM") - 16
                for oc in range(16):
                    pp = ps[(oc % 2) * 2: (oc % 2) * 2 + 2]
                    v.wait_ge(sem["MM"], mmb + oc + 1)
                    for hf in range(2):
                        v.scalar_tensor_tensor((Ta if hf == 0 else Tb)[:, :],
                                               pp[hf][:, :], sb[vecname][:, oc:oc + 1],
                                               blk_c(dx, oc)[:, hf * 512:(hf + 1) * 512],
                                               ALU.add, ALU.mult)
                    v.drain()
                    for hf in range(2):
                        v.tensor_tensor(blk_c(mx, oc)[:, hf * 512:(hf + 1) * 512],
                                        (Ta if hf == 0 else Tb)[:, :],
                                        blk_c(xn, oc)[:, hf * 512:(hf + 1) * 512], ALU.add)
                    v.drain()
                    v.sem_inc(sem["EV"], 1)
                C.inc("EV", 16)

    def proj_block(wdram, bias_col):
        with nc.Block() as blk:
            @blk.gpsimd
            def _(g):
                mmb = C.get("MM")
                for oc in range(16):
                    if oc >= 2:
                        g.wait_ge(sem["MM"], mmb + oc - 1)
                    strip_load(g, wstr[oc % 2], wdram[oc], s="WA" if oc % 2 == 0 else "WB")
                C.inc("WA", 16 * 8)
                C.inc("WB", 16 * 8)

            @blk.tensor
            def _(t):
                wlba = C.get("WA") - 16 * 8
                wlbb = C.get("WB") - 16 * 8
                scb = C.get("SC")
                for oc in range(16):
                    t.wait_ge(sem["WA" if oc % 2 == 0 else "WB"],
                              (wlba if oc % 2 == 0 else wlbb) + 16 * (oc // 2 + 1))
                    pp = ps[(oc % 2) * 2: (oc % 2) * 2 + 2]
                    if oc >= 2:
                        t.wait_ge(sem["SC"], scb + oc - 1)
                    for ci in range(16):
                        for hf in range(2):
                            t.matmul(pp[hf][:, :], wstr[oc % 2][:, ci * 128:(ci + 1) * 128],
                                     blk_c(mx, ci)[:, hf * 512:(hf + 1) * 512],
                                     start=(ci == 0), stop=(ci == 15))
                    t.drain()
                    t.sem_inc(sem["MM"], 1)
                C.inc("MM", 16)

            @blk.scalar
            def _(s):
                mmb = C.get("MM") - 16
                for oc in range(16):
                    pp = ps[(oc % 2) * 2: (oc % 2) * 2 + 2]
                    s.wait_ge(sem["MM"], mmb + oc + 1)
                    for hf in range(2):
                        s.activation(blk_c(pout, oc)[:, hf * 512:(hf + 1) * 512], pp[hf][:, :],
                                     AF.Identity, bias=sb[bias_col][:, oc:oc + 1])
                    s.drain()
                    s.sem_inc(sem["SC"], 1)
                C.inc("SC", 16)

    def fused_pass(tgt_idx, vecname, wdram, bias_col, k_mode=False):
        with nc.Block() as blk:
            evbM = C.get("EV")
            mmbM = C.get("MM")
            scbM = C.get("SC")
            ldbM = C.get("LD")
            wabM = C.get("WA")
            wbbM = C.get("WB")

            @blk.gpsimd
            def _(g):
                g.dma_start(out=wstr[0][:, :], in_=d_w2blk[:, tgt_idx * 2048:(tgt_idx + 1) * 2048]).then_inc(sem["WL"], 16)
                for oc in range(16):
                    if oc >= 2:
                        g.wait_ge(sem["MM"], mmbM + 16 + oc - 1)
                    strip_load(g, wstr2[oc % 2], wdram[oc], s="WA" if oc % 2 == 0 else "WB")
                C.inc("WL", 16)
                C.inc("WA", 16 * 8)
                C.inc("WB", 16 * 8)

            if k_mode:
                @blk.sync
                def _(sync):
                    for oc in range(16):
                        if oc >= 1:
                            sync.wait_ge(sem["SC"], scbM + oc)
                        sync.dma_start(out=xxx[:, :], in_=scr["wexp"][:, oc * TOK:(oc + 1) * TOK]).then_inc(sem["LD"], 16)
                    C.inc("LD", 16 * 16)

            @blk.tensor
            def _(t):
                t.wait_ge(sem["WL"], C.get("WL"))

                def m_mm(oc):
                    if oc >= 1:
                        t.wait_ge(sem["EV"], evbM + oc)
                    for hf in range(2):
                        ti = t.matmul(ps[4 + hf][:, :], wstr[0][:, oc * 128:(oc + 1) * 128],
                                      sb["mmt"][:, hf * 512:(hf + 1) * 512], start=True, stop=True)
                    ti.then_inc(sem["MM"], 1)

                m_mm(0)
                m_mm(1)
                t.wait_ge(sem["WA"], wabM + 16)
                for ci in range(16):
                    t.wait_ge(sem["EV"], evbM + ci + 1)
                    for hf in range(2):
                        ti = t.matmul(ps[0 + hf][:, :], wstr2[0][:, ci * 128:(ci + 1) * 128],
                                      blk_c(mx, ci)[:, hf * 512:(hf + 1) * 512],
                                      start=(ci == 0), stop=(ci == 15))
                    if ci + 2 < 16:
                        m_mm(ci + 2)
                ti.then_inc(sem["MM"], 1)
                for oc in range(1, 16):
                    t.wait_ge(sem["WA" if oc % 2 == 0 else "WB"],
                              (wabM if oc % 2 == 0 else wbbM) + 16 * (oc // 2 + 1))
                    pp = ps[(oc % 2) * 2: (oc % 2) * 2 + 2]
                    if oc >= 2:
                        t.wait_ge(sem["SC"], scbM + oc - 1)
                    for ci in range(16):
                        for hf in range(2):
                            ti = t.matmul(pp[hf][:, :], wstr2[oc % 2][:, ci * 128:(ci + 1) * 128],
                                          blk_c(mx, ci)[:, hf * 512:(hf + 1) * 512],
                                          start=(ci == 0), stop=(ci == 15))
                    ti.then_inc(sem["MM"], 1)
                C.inc("MM", 32)

            @blk.vector
            def _(v):
                for oc in range(16):
                    v.wait_ge(sem["MM"], mmbM + oc + 1)
                    for hf in range(2):
                        v.scalar_tensor_tensor((Ta if hf == 0 else Tb)[:, :],
                                               ps[4 + hf][:, :], sb[vecname][:, oc:oc + 1],
                                               blk_c(dx, oc)[:, hf * 512:(hf + 1) * 512],
                                               ALU.add, ALU.mult)
                    v.drain()
                    for hf in range(2):
                        v.tensor_tensor(blk_c(mx, oc)[:, hf * 512:(hf + 1) * 512],
                                        (Ta if hf == 0 else Tb)[:, :],
                                        blk_c(xn, oc)[:, hf * 512:(hf + 1) * 512], ALU.add)
                    v.drain()
                    v.sem_inc(sem["EV"], 1)
                C.inc("EV", 16)
                if k_mode:
                    for oc in range(16):
                        v.wait_ge(sem["SC"], scbM + oc + 1)
                        for hf in range(2):
                            v.tensor_tensor(blk_c(pout, oc)[:, hf * 512:(hf + 1) * 512],
                                            (Tka if hf == 0 else Tkb)[:, :],
                                            sb["rB2k"][:, hf * 512:(hf + 1) * 512], ALU.mult)
                        v.drain()
                        v.sem_inc(sem["EV"], 1)
                    C.inc("EV", 16)

            @blk.scalar
            def _(s):
                for oc in range(16):
                    pp = ps[(oc % 2) * 2: (oc % 2) * 2 + 2]
                    s.wait_ge(sem["MM"], mmbM + 16 + oc + 1)
                    if k_mode:
                        s.wait_ge(sem["LD"], ldbM + 16 * (oc + 1))
                        if oc >= 1:
                            s.wait_ge(sem["EV"], evbM + 16 + oc)
                        s.activation(sb["muB2k"][:, 0:TOK], xxx[:, :], AF.Exp, scale=-1.0)
                        for hf in range(2):
                            s.activation((Tka if hf == 0 else Tkb)[:, :], pp[hf][:, :],
                                         AF.Identity, bias=sb["kb"][:, oc:oc + 1])
                        s.drain()
                        s.activation(sb["rB2k"][:, 0:TOK], sb["muB2k"][:, 0:TOK], AF.Identity, scale=-1.0, bias=sb["cbias"][:, 2:3])
                        s.drain()
                        s.sem_inc(sem["SC"], 1)
                    else:
                        for hf in range(2):
                            s.activation(blk_c(pout, oc)[:, hf * 512:(hf + 1) * 512], pp[hf][:, :],
                                         AF.Identity, bias=sb[bias_col][:, oc:oc + 1])
                        s.drain()
                        s.sem_inc(sem["SC"], 1)
                C.inc("SC", 16)

    def spill_block(dst):
        with nc.Block() as blk:
            @blk.sync
            def _(sync):
                sync.dma_start(out=dst, in_=pout[:, :]).then_inc(sem["SP"], 16)
                sync.wait_ge(sem["SP"], C.inc("SP", 16))

    mix_block(3, "maa_w")
    with nc.Block() as blk:
        @blk.gpsimd
        def _(g):
            g.dma_start(out=wstr[0][:, 0:1024], in_=d_tdw1).then_inc(sem["WL"], 16)
            g.dma_start(out=wstr[1][0:64, 0:2048], in_=d_tdw2).then_inc(sem["WL"], 16)
            C.inc("WL", 32)

        @blk.tensor
        def _(t):
            t.wait_ge(sem["WL"], C.get("WL"))
            for ci in range(16):
                for hf in range(2):
                    t.matmul(ps[4 + hf][0:64, :], wstr[0][:, ci * 64:ci * 64 + 64],
                             blk_c(mx, ci)[:, hf * 512:(hf + 1) * 512],
                             start=(ci == 0), stop=(ci == 15))
            t.drain()
            t.sem_inc(sem["MM"], 1)
            C.inc("MM", 1)

        @blk.scalar
        def _(s):
            s.wait_ge(sem["MM"], C.get("MM"))
            for hf in range(2):
                s.activation(sb["tht"][0:64, hf * 512:(hf + 1) * 512], ps[4 + hf][0:64, :], AF.Tanh)
            s.drain()
            s.sem_inc(sem["SC"], 1)
            C.inc("SC", 1)

        @blk.tensor
        def _(t):
            t.wait_ge(sem["SC"], C.get("SC"))
            scb = C.get("SC")
            for oc in range(16):
                pp = ps[(oc % 2) * 2: (oc % 2) * 2 + 2]
                if oc >= 2:
                    t.wait_ge(sem["SC"], scb + oc - 1)
                for hf in range(2):
                    t.matmul(pp[hf][:, :], wstr[1][0:64, oc * 128:(oc + 1) * 128],
                             sb["tht"][0:64, hf * 512:(hf + 1) * 512], start=True, stop=True)
                t.drain()
                t.sem_inc(sem["MM"], 1)
            C.inc("MM", 16)

        @blk.scalar
        def _(s):
            mmb = C.get("MM") - 16
            evb = C.get("EV")
            for oc in range(16):
                pp = ps[(oc % 2) * 2: (oc % 2) * 2 + 2]
                s.wait_ge(sem["MM"], mmb + oc + 1)
                if oc >= 1:
                    s.wait_ge(sem["EV"], evb + oc)
                for hf in range(2):
                    s.activation((Ta if hf == 0 else Tb)[:, :], pp[hf][:, :],
                                 AF.Exp, bias=sb["tdv"][:, oc:oc + 1])
                s.drain()
                s.sem_inc(sem["SC"], 1)
            C.inc("SC", 16)

        @blk.vector
        def _(v):
            scb = C.get("SC") - 16
            for oc in range(16):
                v.wait_ge(sem["SC"], scb + oc + 1)
                for hf in range(2):
                    v.tensor_scalar_min(blk_c(pout, oc)[:, hf * 512:(hf + 1) * 512],
                                        (Ta if hf == 0 else Tb)[:, :], 5.0)
                v.drain()
                v.sem_inc(sem["EV"], 1)
            C.inc("EV", 16)
    spill_block(scr["wexp"])
    dump_cb("wexp", pout, 16)

    fused_pass(2, "maa_v", d_vw, "vb")
    spill_block(scr["v"])
    dump_cb("v", pout, 16)

    fused_pass(1, "maa_k", d_kw, "kb", k_mode=True)
    spill_block(scr["km"])
    dump_cb("km", pout, 16)

    fused_pass(0, "maa_r", d_qw, "qb")
    spill_block(scr["q"])
    dump_cb("q", pout, 16)
    A3.close()
    A1b.close()

    if PHASES <= 2:
        return _early_out()

    A4 = contextlib.ExitStack()
    qc = [alloc(A4, "qc%d" % i, [128, 2048], BF16) for i in range(2)]
    kmc = [alloc(A4, "kmc%d" % i, [128, 2048], BF16) for i in range(2)]
    vc = [alloc(A4, "vc%d" % i, [128, 2048], BF16) for i in range(2)]
    wxc = [alloc(A4, "wxc%d" % i, [128, 2048], BF16) for i in range(2)]
    csc = [alloc(A4, "csc%d" % i, [128, 2048], F32) for i in range(2)]
    gqc = [alloc(A4, "gqc%d" % i, [128, 2048], F32) for i in range(2)]
    gqic = alloc(A4, "gqic", [128, 2048], F32)
    qtc = [alloc(A4, "qtc%d" % i, [128, 2048], BF16) for i in range(2)]
    ktc = [alloc(A4, "ktc%d" % i, [128, 2048], BF16) for i in range(2)]
    k2c = [alloc(A4, "k2c%d" % i, [128, 2048], BF16) for i in range(2)]
    qts = [alloc(A4, "qts%d" % i, [64, 2048], BF16) for i in range(2)]
    kts = [alloc(A4, "kts%d" % i, [64, 2048], BF16) for i in range(2)]
    k2tm = [alloc(A4, "k2tm%d" % i, [128, 2048], BF16) for i in range(2)]
    vtm = [alloc(A4, "vtm%d" % i, [128, 2048], BF16) for i in range(2)]
    pmt = alloc(A4, "pmt", [128, 32 * 128], BF16)
    oic = alloc(A4, "oic", [128, 2048], BF16)

    evb0 = C.get("EV"); scb0 = C.get("SC"); mmb0 = C.get("MM")
    wab0 = C.get("LA"); wbb0 = C.get("LB"); spb0 = C.get("SP")
    NV, NS, NT = 9, 1, 7

    def vec_cnt(j, stage):
        return evb0 + j * NV + stage

    def mm_cnt(j, stage):
        return mmb0 + j * NT + stage

    with nc.Block() as blk:
        @blk.sync
        def _(sync):
            spb_s = C.get("SP")

            def loads(j):
                p = j % 2
                if j >= 2:
                    sync.wait_ge(sem["MM"], mm_cnt(j - 2, 7))
                for ci in range(16):
                    co = ci * TOK + j * 128
                    for nm, dst in [("q", qc[p]), ("km", kmc[p]), ("v", vc[p]), ("wexp", wxc[p])]:
                        sync.dma_start(out=dst[:, ci * 128:(ci + 1) * 128],
                                       in_=scr[nm][:, co:co + 128]).then_inc(sem["LA" if p == 0 else "LB"], 16)

            def spill(j):
                sync.wait_ge(sem["EV"], vec_cnt(j, 9))
                sync.dma_start(out=scr["oi"][:, j * 2048:(j + 1) * 2048], in_=oic[:, :]).then_inc(sem["SP"], 16)
                sync.dma_start(out=scr["qt"][:, j * 2048:(j + 1) * 2048], in_=qtc[j % 2][:, :]).then_inc(sem["SP"], 16)

            loads(0); loads(1); loads(2)
            for j in range(3, NCH):
                spill(j - 3)
                loads(j)
            spill(NCH - 3); spill(NCH - 2); spill(NCH - 1)
            sync.wait_ge(sem["SP"], spb_s + 32 * NCH)
            C.inc("LA", 16 * 64 * 4)
            C.inc("LB", 16 * 64 * 4)
            C.inc("SP", 32 * NCH)

        @blk.scalar
        def _(s):
            for j in range(NCH):
                p = j % 2
                s.wait_ge(sem["EV"], vec_cnt(j, 1))
                s.activation(gqc[p][:, :], csc[p][:, :], AF.Exp, scale=-1.0)
                s.activation(gqic[:, :], csc[p][:, :], AF.Exp)
                s.drain()
                s.sem_inc(sem["SC"], 1)
            C.inc("SC", NCH)

        @blk.gpsimd
        def _(g):
            for j in range(NCH):
                p = j % 2
                g.wait_ge(sem["EV"], vec_cnt(j, 2))
                g.dma_start(out=qts[p][:, :], in_=qtc[p][64:128, :]).then_inc(sem["SH"], 16)
                g.dma_start(out=kts[p][:, :], in_=ktc[p][64:128, :]).then_inc(sem["SH"], 16)
            C.inc("SH", 32 * NCH)

        @blk.vector
        def _(v):
            ldb0 = C.get("LD") - 32 * NCH
            for j in range(NCH):
                p = j % 2
                wa = wab0 + (j // 2 + 1) * 16 * 64 if p == 0 else wbb0 + (j // 2 + 1) * 16 * 64
                v.wait_ge(sem["LA" if p == 0 else "LB"], wa)
                for ci in range(16):
                    v.tensor_tensor_scan(csc[p][:, ci * 128:(ci + 1) * 128],
                                         wxc[p][:, ci * 128:(ci + 1) * 128],
                                         wxc[p][:, ci * 128:(ci + 1) * 128],
                                         0.0, ALU.add, ALU.bypass)
                v.drain()
                v.sem_inc(sem["EV"], 1)
                v.wait_ge(sem["SC"], scb0 + j + 1)
                if j >= 2:
                    v.wait_ge(sem["SP"], spb0 + 32 * (j - 1))
                v.tensor_tensor(qtc[p][:, :], qc[p][:, :], gqc[p][:, :], ALU.mult)
                v.tensor_tensor(ktc[p][:, :], kmc[p][:, :], gqic[:, :], ALU.mult)
                v.drain()
                for ci in range(16):
                    v.tensor_scalar_mul(k2c[p][:, ci * 128:(ci + 1) * 128],
                                        ktc[p][:, ci * 128:(ci + 1) * 128],
                                        gqc[p][:, ci * 128 + 127:ci * 128 + 128])
                v.drain()
                v.sem_inc(sem["EV"], 1)
                for g4 in range(4):
                    v.wait_ge(sem["MM"], mm_cnt(j, g4 + 1))
                    v.tensor_copy(vtm[p][:, g4 * 512:(g4 + 1) * 512], psb[:, 0:512])
                    v.tensor_copy(k2tm[p][:, g4 * 512:(g4 + 1) * 512], psb[:, 512:1024])
                    v.drain()
                    v.sem_inc(sem["EV"], 1)
                for wave in range(2):
                    v.wait_ge(sem["MM"], mm_cnt(j, 5 + wave))
                    for bank in range(4):
                        v.tensor_tensor(pmt[:, wave * 2048 + bank * 512: wave * 2048 + (bank + 1) * 512],
                                        ps[bank][:, :], sb["mask4"][:, :], ALU.mult)
                    v.drain()
                    v.sem_inc(sem["EV"], 1)
                v.wait_ge(sem["MM"], mm_cnt(j, 7))
                if j >= 1:
                    v.wait_ge(sem["SP"], spb0 + 32 * j)
                v.tensor_copy(sb["Lsnap"][:, j * 1024:(j + 1) * 1024], sb["S_run"][:, :])
                for ci in range(16):
                    v.tensor_copy(sb["Apref"][:, j * 16 + ci:j * 16 + ci + 1],
                                  sb["A_run"][:, ci:ci + 1])
                for ci in range(16):
                    v.tensor_copy(oic[:, ci * 128:(ci + 1) * 128],
                                  ps[ci % 4][:, (ci // 4) * 128:(ci // 4) * 128 + 128])
                v.drain()
                for ci in range(16):
                    v.scalar_tensor_tensor(sb["S_run"][:, ci * 64:(ci + 1) * 64],
                                           sb["S_run"][:, ci * 64:(ci + 1) * 64],
                                           gqc[p][:, ci * 128 + 127:ci * 128 + 128],
                                           ps[4 + ci % 2][:, (ci // 2) * 64:(ci // 2) * 64 + 64],
                                           ALU.mult, ALU.add)
                    v.tensor_scalar_mul(sb["A_run"][:, ci:ci + 1], sb["A_run"][:, ci:ci + 1],
                                        gqc[p][:, ci * 128 + 127:ci * 128 + 128])
                v.drain()
                v.sem_inc(sem["EV"], 1)
            C.inc("EV", NV * NCH)

        @blk.tensor
        def _(t):
            ldb0 = C.get("SH") - 32 * NCH
            spb = C.get("SP")
            for j in range(NCH):
                p = j % 2
                t.wait_ge(sem["EV"], vec_cnt(j, 2))
                for g4 in range(4):
                    if not (j == 0 and g4 == 0):
                        prev = vec_cnt(j, 2 + g4) if g4 > 0 else vec_cnt(j - 1, 6)
                        t.wait_ge(sem["EV"], prev)
                    for sl in range(4):
                        ci = g4 * 4 + sl
                        t.transpose(psb[:, sl * 128:(sl + 1) * 128], vc[p][:, ci * 128:(ci + 1) * 128], sb["identb"][:, :])
                        ti = t.transpose(psb[:, 512 + sl * 128: 512 + (sl + 1) * 128], k2c[p][:, ci * 128:(ci + 1) * 128], sb["identb"][:, :])
                    ti.then_inc(sem["MM"], 1)
                t.wait_ge(sem["SH"], ldb0 + 32 * (j + 1))
                if j >= 1:
                    t.wait_ge(sem["EV"], vec_cnt(j - 1, 9))
                for wave in range(2):
                    if wave == 1:
                        t.wait_ge(sem["EV"], vec_cnt(j, 7))
                    for hh in range(16):
                        h = wave * 16 + hh
                        ci, off = h // 2, (h % 2) * 64
                        kt_src = ktc[p][0:64, ci * 128:(ci + 1) * 128] if off == 0 else kts[p][:, ci * 128:(ci + 1) * 128]
                        qt_src = qtc[p][0:64, ci * 128:(ci + 1) * 128] if off == 0 else qts[p][:, ci * 128:(ci + 1) * 128]
                        ti = t.matmul(ps[hh // 4][:, (hh % 4) * 128:(hh % 4 + 1) * 128],
                                      kt_src, qt_src, start=True, stop=True)
                    ti.then_inc(sem["MM"], 1)
                t.wait_ge(sem["EV"], vec_cnt(j, 8))
                for h in range(32):
                    ci, off = h // 2, (h % 2) * 64
                    t.matmul(ps[ci % 4][off:off + 64, (ci // 4) * 128:(ci // 4) * 128 + 128],
                             vtm[p][:, ci * 128 + off:ci * 128 + off + 64],
                             pmt[:, h * 128:(h + 1) * 128],
                             start=True, stop=True, tile_position=(0, off))
                for h in range(32):
                    ci, off = h // 2, (h % 2) * 64
                    ti = t.matmul(ps[4 + ci % 2][off:off + 64, (ci // 2) * 64:(ci // 2) * 64 + 64],
                                  k2tm[p][:, ci * 128 + off:ci * 128 + off + 64],
                                  vtm[p][:, ci * 128 + off:ci * 128 + off + 64],
                                  start=True, stop=True, tile_position=(0, off))
                ti.then_inc(sem["MM"], 1)
            C.inc("MM", NT * NCH)

    A4.close()
    dump_small("sspan", sb["S_run"][:, :], NCB * 64)
    dump_small("aspan", sb["A_run"][:, :], 16)

    if PHASES <= 3:
        return _early_out()

    with nc.Block() as blk:
        @blk.vector
        def _(v):
            for ci in range(16):
                v.tensor_copy(sb["aggst"][:, ci * 65:ci * 65 + 64], sb["S_run"][:, ci * 64:(ci + 1) * 64])
                v.tensor_copy(sb["aggst"][:, ci * 65 + 64:ci * 65 + 65], sb["A_run"][:, ci:ci + 1])
            v.drain()
            v.sem_inc(sem["EV"], 1)
            C.inc("EV", 1)

        @blk.sync
        def _(sync):
            sync.wait_ge(sem["EV"], C.get("EV"))
            for ci in range(16):
                sync.dma_start(out=d_agin.ap()[ci * 128:(ci + 1) * 128, :],
                               in_=sb["aggst"][:, ci * 65:(ci + 1) * 65]).then_inc(sem["SP"], 16)
            C.inc("SP", 16 * 16)

        @blk.gpsimd
        def _(g):
            g.wait_ge(sem["SP"], C.get("SP"))
            g.collective_compute("AllGather", ALU.bypass,
                                 replica_groups=[list(range(8))],
                                 ins=[d_agin.ap()], outs=[d_agout.ap()]).then_inc(sem["CC"], 1)
            C.inc("CC", 1)

        @blk.vector
        def _(v):
            v.memset(sb["S_run"][:, :], 0.0)
            v.drain()
            v.sem_inc(sem["EV"], 1)
            C.inc("EV", 1)

    FB = contextlib.ExitStack()
    ag2 = alloc(FB, "ag2", [128, 16 * 65], F32)
    with nc.Block() as blk:
        agb = [sb["aggst"], ag2]
        evf = C.get("EV")
        ldf = C.get("LD")

        laf = C.get("LA")
        lbf = C.get("LB")

        @blk.sync
        def _(sync):
            sync.wait_ge(sem["CC"], C.get("CC"))
            for jj in range(7):
                if jj >= 2:
                    sync.wait_ge(sem["EV"], evf + jj - 1)
                for ci in range(16):
                    sync.dma_start(out=agb[jj % 2][:, ci * 65:(ci + 1) * 65],
                                   in_=d_agout.ap()[jj * 2048 + ci * 128: jj * 2048 + (ci + 1) * 128, :]).then_inc(sem["LA" if jj % 2 == 0 else "LB"], 16)
            C.inc("LA", 16 * 16 * 4)
            C.inc("LB", 16 * 16 * 3)

        @blk.vector
        def _(v):
            for jj in range(7):
                a = agb[jj % 2]
                v.wait_ge(sem["LA" if jj % 2 == 0 else "LB"],
                          (laf if jj % 2 == 0 else lbf) + 256 * (jj // 2 + 1))
                for ci in range(16):
                    v.tensor_scalar(sb["ALtmp"][:, ci:ci + 1],
                                    a[:, ci * 65 + 64:ci * 65 + 65],
                                    sb["umask"][:, jj:jj + 1],
                                    sb["umask"][:, 8 + jj:9 + jj], ALU.mult, ALU.add)
                    v.tensor_scalar_mul(sb["sfold"][:, ci * 64:(ci + 1) * 64],
                                        a[:, ci * 65:ci * 65 + 64],
                                        sb["umask"][:, jj:jj + 1])
                v.drain()
                for ci in range(16):
                    v.scalar_tensor_tensor(sb["S_run"][:, ci * 64:(ci + 1) * 64],
                                           sb["S_run"][:, ci * 64:(ci + 1) * 64],
                                           sb["ALtmp"][:, ci:ci + 1],
                                           sb["sfold"][:, ci * 64:(ci + 1) * 64],
                                           ALU.mult, ALU.add)
                v.drain()
                v.sem_inc(sem["EV"], 1)
            C.inc("EV", 7)
    FB.close()
    dump_small("sin", sb["S_run"][:, :], NCB * 64)

    with nc.Block() as blk:
        @blk.vector
        def _(v):
            for j in range(NCH):
                for ci in range(16):
                    v.scalar_tensor_tensor(sb["Lsnap"][:, j * 1024 + ci * 64:j * 1024 + (ci + 1) * 64],
                                           sb["S_run"][:, ci * 64:(ci + 1) * 64],
                                           sb["Apref"][:, j * 16 + ci:j * 16 + ci + 1],
                                           sb["Lsnap"][:, j * 1024 + ci * 64:j * 1024 + (ci + 1) * 64],
                                           ALU.mult, ALU.add)
            v.drain()
            v.sem_inc(sem["EV"], 1)
            C.inc("EV", 1)

    if PHASES <= 4:
        return _early_out()

    PB = contextlib.ExitStack()
    ynb = alloc(PB, "ynb", [128, NCB * TOK], BF16)
    qtr = [alloc(PB, "qtr%d" % i, [128, 2048], BF16) for i in range(2)]
    oir = [alloc(PB, "oir%d" % i, [128, 2048], BF16) for i in range(2)]
    qtro = [alloc(PB, "qtro%d" % i, [64, 2048], BF16) for i in range(2)]
    scso = [alloc(PB, "scso%d" % i, [64, 1024], BF16) for i in range(2)]
    ocb = alloc(PB, "ocb", [128, 2048], BF16)
    osq = alloc(PB, "osq", [128, 2048], BF16)
    Tcw = alloc(PB, "Tcw", [128, 2048], F32)
    muw = alloc(PB, "muw", [128, 2048], F32)
    rw = alloc(PB, "rw", [128, 2048], F32)
    mu1 = alloc(PB, "mu1", [1, 128], F32)
    rs1 = alloc(PB, "rs1", [1, 128], F32)

    evbP = C.get("EV"); mmbP = C.get("MM"); scbP = C.get("SC")
    labP = C.get("LA"); lbbP = C.get("LB")

    def vP(j, s):
        return evbP + 4 * j + s

    def mP(j, s):
        return mmbP + 3 * j + s

    def sP(j, s):
        return scbP + 3 * j + s

    with nc.Block() as blk:
        @blk.sync
        def _(sync):
            def loadsP(j):
                p = j % 2
                sname = "LA" if p == 0 else "LB"
                base = (labP if p == 0 else lbbP) + (j // 2) * 64
                if j >= 2:
                    sync.wait_ge(sem["MM"], mP(j - 2, 1))
                    sync.wait_ge(sem["EV"], vP(j - 2, 1))
                sync.dma_start(out=qtr[p][:, :], in_=scr["qt"][:, j * 2048:(j + 1) * 2048]).then_inc(sem[sname], 16)
                sync.dma_start(out=oir[p][:, :], in_=scr["oi"][:, j * 2048:(j + 1) * 2048]).then_inc(sem[sname], 16)
                sync.wait_ge(sem[sname], base + 32)
                sync.dma_start(out=qtro[p][:, :], in_=qtr[p][64:128, :]).then_inc(sem[sname], 16)
                sync.dma_start(out=scso[p][:, :], in_=sb["Lsnap"][64:128, j * 1024:(j + 1) * 1024]).then_inc(sem[sname], 16)
            loadsP(0)
            loadsP(1)
            for j in range(2, NCH):
                loadsP(j)
            C.inc("LA", 64 * 4)
            C.inc("LB", 64 * 4)

        @blk.tensor
        def _(t):
            for j in range(NCH):
                p = j % 2
                sname = "LA" if p == 0 else "LB"
                t.wait_ge(sem[sname], (labP if p == 0 else lbbP) + (j // 2) * 64 + 64)
                if j >= 1:
                    t.wait_ge(sem["EV"], vP(j - 1, 1))
                for h in range(32):
                    ci, off = h // 2, (h % 2) * 64
                    ls_src = (sb["Lsnap"][0:64, j * 1024 + ci * 64:j * 1024 + (ci + 1) * 64]
                              if off == 0 else scso[p][:, ci * 64:(ci + 1) * 64])
                    qt_src = (qtr[p][0:64, ci * 128:(ci + 1) * 128]
                              if off == 0 else qtro[p][:, ci * 128:(ci + 1) * 128])
                    ti = t.matmul(ps[ci % 4][off:off + 64, (ci // 4) * 128:(ci // 4) * 128 + 128],
                                  ls_src, qt_src, start=True, stop=True, tile_position=(0, off))
                ti.then_inc(sem["MM"], 1)
                t.wait_ge(sem["SC"], sP(j, 1))
                if j >= 1:
                    t.wait_ge(sem["EV"], vP(j - 1, 2))
                for ci in range(16):
                    t.matmul(ps[4][0:1, 0:128], sb["onesb"][:, :], ocb[:, ci * 128:(ci + 1) * 128],
                             start=(ci == 0), stop=(ci == 15))
                for ci in range(16):
                    ti = t.matmul(ps[5][0:1, 0:128], sb["onesb"][:, :], osq[:, ci * 128:(ci + 1) * 128],
                                  start=(ci == 0), stop=(ci == 15))
                ti.then_inc(sem["MM"], 1)
                t.wait_ge(sem["EV"], vP(j, 3))
                t.matmul(ps[6][:, 0:128], sb["onesrow"][0:1, :], mu1[0:1, :], start=True, stop=True)
                ti = t.matmul(ps[6][:, 128:256], sb["onesrow"][0:1, :], rs1[0:1, :], start=True, stop=True)
                ti.then_inc(sem["MM"], 1)
            C.inc("MM", 3 * NCH)

        @blk.vector
        def _(v):
            for j in range(NCH):
                p = j % 2
                sname = "LA" if p == 0 else "LB"
                v.wait_ge(sem["MM"], mP(j, 1))
                v.wait_ge(sem[sname], (labP if p == 0 else lbbP) + (j // 2) * 64 + 64)
                if j >= 1:
                    v.wait_ge(sem["SC"], sP(j - 1, 3))
                    v.wait_ge(sem["MM"], mP(j - 1, 2))
                for ci in range(16):
                    v.tensor_tensor(ocb[:, ci * 128:(ci + 1) * 128],
                                    oir[p][:, ci * 128:(ci + 1) * 128],
                                    ps[ci % 4][:, (ci // 4) * 128:(ci // 4) * 128 + 128], ALU.add)
                v.drain()
                v.sem_inc(sem["EV"], 1)
                v.wait_ge(sem["MM"], mP(j, 2))
                v.tensor_scalar_mul(mu1[0:1, :], ps[4][0:1, 0:128], float(1.0 / 2048.0))
                v.tensor_scalar_mul(rs1[0:1, :], ps[5][0:1, 0:128], float(1.0 / 2048.0))
                v.drain()
                v.tensor_tensor(Tcw[0:1, 0:128], mu1[0:1, :], mu1[0:1, :], ALU.mult)
                v.drain()
                v.tensor_tensor(rs1[0:1, :], rs1[0:1, :], Tcw[0:1, 0:128], ALU.subtract)
                v.drain()
                v.sem_inc(sem["EV"], 1)
                v.wait_ge(sem["SC"], sP(j, 2))
                v.reciprocal(rs1[0:1, :], rs1[0:1, :])
                v.drain()
                v.sem_inc(sem["EV"], 1)
                v.wait_ge(sem["MM"], mP(j, 3))
                for ci in range(16):
                    v.tensor_copy(muw[:, ci * 128:(ci + 1) * 128], ps[6][:, 0:128])
                    v.tensor_copy(rw[:, ci * 128:(ci + 1) * 128], ps[6][:, 128:256])
                v.drain()
                v.tensor_tensor(Tcw[:, :], ocb[:, :], muw[:, :], ALU.subtract)
                v.drain()
                v.tensor_tensor(Tcw[:, :], Tcw[:, :], rw[:, :], ALU.mult)
                v.drain()
                v.sem_inc(sem["EV"], 1)
            C.inc("EV", 4 * NCH)

        @blk.scalar
        def _(s):
            for j in range(NCH):
                s.wait_ge(sem["EV"], vP(j, 1))
                s.activation(osq[:, :], ocb[:, :], AF.Square)
                s.drain()
                s.sem_inc(sem["SC"], 1)
                s.wait_ge(sem["EV"], vP(j, 2))
                s.activation(rs1[0:1, :], rs1[0:1, :], AF.Sqrt, bias=sb["cbias"][0:1, 1:2])
                s.drain()
                s.sem_inc(sem["SC"], 1)
                s.wait_ge(sem["EV"], vP(j, 4))
                for ci in range(16):
                    s.activation(ynb[:, ci * TOK + j * 128:ci * TOK + j * 128 + 128],
                                 Tcw[:, ci * 128:(ci + 1) * 128],
                                 AF.Identity, scale=sb["lng"][:, ci:ci + 1], bias=sb["lnb"][:, ci:ci + 1])
                s.drain()
                s.sem_inc(sem["SC"], 1)
            C.inc("SC", 3 * NCH)
    dump_cb("yn", ynb, 16)

    OW = contextlib.ExitStack()
    wstrO = [alloc(OW, "wstrO_%d" % i, [128, 2048], BF16) for i in range(2)]
    sbuf_s = alloc(OW, "sbuf_s", [128, NCB * TOK], BF16)
    with nc.Block() as blk:
        @blk.gpsimd
        def _(g):
            mmb = C.get("MM")
            for oc in range(16):
                if oc >= 2:
                    g.wait_ge(sem["MM"], mmb + oc - 1)
                strip_load(g, wstrO[oc % 2], d_ow[oc], s="WA" if oc % 2 == 0 else "WB")
            C.inc("WA", 16 * 8)
            C.inc("WB", 16 * 8)

        @blk.tensor
        def _(t):
            wlba = C.get("WA") - 16 * 8
            wlbb = C.get("WB") - 16 * 8
            evb = C.get("EV")
            for oc in range(16):
                t.wait_ge(sem["WA" if oc % 2 == 0 else "WB"],
                          (wlba if oc % 2 == 0 else wlbb) + 16 * (oc // 2 + 1))
                pp = ps[(oc % 2) * 2: (oc % 2) * 2 + 2]
                if oc >= 2:
                    t.wait_ge(sem["EV"], evb + oc - 1)
                for ci in range(16):
                    for hf in range(2):
                        t.matmul(pp[hf][:, :], wstrO[oc % 2][:, ci * 128:(ci + 1) * 128],
                                 blk_c(ynb, ci)[:, hf * 512:(hf + 1) * 512],
                                 start=(ci == 0), stop=(ci == 15))
                t.drain()
                t.sem_inc(sem["MM"], 1)
            C.inc("MM", 16)

        @blk.vector
        def _(v):
            mmb = C.get("MM") - 16
            for oc in range(16):
                v.wait_ge(sem["MM"], mmb + oc + 1)
                pp = ps[(oc % 2) * 2: (oc % 2) * 2 + 2]
                for hf in range(2):
                    v.tensor_copy(blk_c(sbuf_s, oc)[:, hf * 512:(hf + 1) * 512], pp[hf][:, :])
                v.drain()
                v.sem_inc(sem["EV"], 1)
            C.inc("EV", 16)

        @blk.sync
        def _(sync):
            sync.wait_ge(sem["EV"], C.get("EV"))
            sync.dma_start(out=scr["s"], in_=sbuf_s[:, :]).then_inc(sem["SP"], 16)
            sync.wait_ge(sem["SP"], C.inc("SP", 16))
    dump_cb("s", sbuf_s, 16)
    OW.close()
    PB.close()

    if PHASES <= 5:
        return _early_out()

    LS.close()

    PC = contextlib.ExitStack()
    zb = alloc(PC, "zb", [128, 64 * TOK], BF16)
    UG = contextlib.ExitStack()
    x1b = alloc(UG, "x1b", [128, NCB * TOK], BF16)
    alloc(UG, "Hdp", [128, 128], BF16)
    ztp = [alloc(UG, "ztp%d" % i, [128, TOK], BF16) for i in range(8)]
    wstr3 = [alloc(UG, "wstr3_%d" % i, [128, 2048], BF16) for i in range(4)]
    hb = alloc(UG, "hb", [128, TOK], BF16)
    sgb = alloc(UG, "sgb", [128, TOK], F32)

    with nc.Block() as blk:
        @blk.sync
        def _(sync):
            sync.dma_start(out=x1b[:, :], in_=scr["x1"]).then_inc(sem["LD"], 16)
            C.inc("LD", 16)

        @blk.gpsimd
        def _(g):
            mmb = C.get("MM")
            for ic in range(64):
                if ic >= 2:
                    g.wait_ge(sem["MM"], mmb + 2 * (ic - 1))
                sname = "WA" if ic % 2 == 0 else "WB"
                strip_load(g, wstr3[(ic % 2) * 2], d_wup[ic], s=sname)
                strip_load(g, wstr3[(ic % 2) * 2 + 1], d_wg[ic], s=sname)
            C.inc("WA", 32 * 32)
            C.inc("WB", 32 * 32)

        @blk.tensor
        def _(t):
            t.wait_ge(sem["LD"], C.get("LD"))
            wlba = C.get("WA") - 32 * 32
            wlbb = C.get("WB") - 32 * 32
            evb = C.get("EV")
            scb = C.get("SC")
            for ic in range(64):
                t.wait_ge(sem["WA" if ic % 2 == 0 else "WB"],
                          (wlba if ic % 2 == 0 else wlbb) + 32 * (ic // 2 + 1))
                wu, wg = wstr3[(ic % 2) * 2], wstr3[(ic % 2) * 2 + 1]
                if ic >= 1:
                    t.wait_ge(sem["EV"], evb + 2 * ic - 1)
                    t.wait_ge(sem["SC"], scb + ic)
                for ci in range(16):
                    for hf in range(2):
                        t.matmul(ps[hf][:, :], wu[:, ci * 128:(ci + 1) * 128],
                                 blk_c(x1b, ci)[:, hf * 512:(hf + 1) * 512],
                                 start=(ci == 0), stop=(ci == 15))
                        t.matmul(ps[2 + hf][:, :], wg[:, ci * 128:(ci + 1) * 128],
                                 blk_c(x1b, ci)[:, hf * 512:(hf + 1) * 512],
                                 start=(ci == 0), stop=(ci == 15))
                t.drain()
                t.sem_inc(sem["MM"], 1)
                t.wait_ge(sem["EV"], evb + 2 * ic + 1)
                for hf in range(2):
                    t.matmul(ps[4 + hf][:, :], sb["Hdp"][:, :], hb[:, hf * 512:(hf + 1) * 512],
                             start=True, stop=True)
                t.drain()
                t.sem_inc(sem["MM"], 1)
            C.inc("MM", 128)

        @blk.scalar
        def _(s):
            mmb = C.get("MM") - 128
            evb = C.get("EV")
            for ic in range(64):
                s.wait_ge(sem["MM"], mmb + 2 * ic + 1)
                if ic >= 1:
                    s.wait_ge(sem["EV"], evb + 2 * ic - 1)
                for hf in range(2):
                    s.activation(sgb[:, hf * 512:(hf + 1) * 512], ps[2 + hf][:, :],
                                 AF.Silu, scale=sb["s_g"][:, ic:ic + 1])
                s.drain()
                s.sem_inc(sem["SC"], 1)
            C.inc("SC", 64)

        @blk.vector
        def _(v):
            mmb = C.get("MM") - 128
            scb = C.get("SC") - 64

            def bf_batch(pairs, d):
                for gs in range(0, len(pairs), 8):
                    grp = pairs[gs:gs + 8]
                    for kk, i in enumerate(grp):
                        v.tensor_tensor(ztp[kk][:, :], zb[:, i * TOK:(i + 1) * TOK],
                                        zb[:, (i + d) * TOK:(i + d + 1) * TOK], ALU.add)
                    v.drain()
                    for i in grp:
                        jb = i + d
                        v.tensor_tensor(zb[:, jb * TOK:(jb + 1) * TOK],
                                        zb[:, i * TOK:(i + 1) * TOK],
                                        zb[:, jb * TOK:(jb + 1) * TOK], ALU.subtract)
                    v.drain()
                    for kk, i in enumerate(grp):
                        v.tensor_copy(zb[:, i * TOK:(i + 1) * TOK], ztp[kk][:, :])
                    v.drain()

            for ic in range(64):
                v.wait_ge(sem["SC"], scb + ic + 1)
                v.tensor_scalar_mul(sb["Hdp"][:, :], sb["H128b"][:, :], sb["sudp"][:, ic:ic + 1])
                for hf in range(2):
                    v.scalar_tensor_tensor(hb[:, hf * 512:(hf + 1) * 512], ps[hf][:, :],
                                           sb["s_up"][:, ic:ic + 1],
                                           sgb[:, hf * 512:(hf + 1) * 512], ALU.mult, ALU.mult)
                v.drain()
                v.sem_inc(sem["EV"], 1)
                v.wait_ge(sem["MM"], mmb + 2 * ic + 2)
                for hf in range(2):
                    v.tensor_copy(zb[:, ic * TOK + hf * 512: ic * TOK + (hf + 1) * 512], ps[4 + hf][:, :])
                v.drain()
                v.sem_inc(sem["EV"], 1)
                for d in [1, 2, 4, 8, 16, 32]:
                    if (ic + 1) % (2 * d) == 0:
                        base = ic + 1 - 2 * d
                        bf_batch([base + i for i in range(d)], d)
            C.inc("EV", 128)

    if DEBUG:
        with nc.Block() as blk:
            @blk.scalar
            def _(s):
                spb = C.get("SP")
                for ib in range(64):
                    if ib:
                        s.wait_ge(sem["SP"], spb + 16 * ib)
                    s.activation(sb["dbuf"][:, :], zb[:, ib * TOK:(ib + 1) * TOK], AF.Copy)
                    s.drain()
                    s.sem_inc(sem["SC"], 1)
                C.inc("SC", 64)

            @blk.sync
            def _(sync):
                scb = C.get("SC") - 64
                for ib in range(64):
                    sync.wait_ge(sem["SC"], scb + ib + 1)
                    sync.dma_start(out=dbg["z"][:, ib * TOK:(ib + 1) * TOK],
                                   in_=sb["dbuf"][:, :]).then_inc(sem["SP"], 16)
                sync.wait_ge(sem["SP"], C.inc("SP", 16 * 64))
    UG.close()

    DN = contextlib.ExitStack()
    gcomb = alloc(DN, "gcomb", [128, TOK], F32)
    srel = alloc(DN, "srel", [128, TOK], BF16)
    xrel = alloc(DN, "xrel", [128, TOK], F32)
    outb = alloc(DN, "outb", [128, TOK], F32)
    wstr4 = [alloc(DN, "wstr4_%d" % i, [128, 64 * 128], BF16) for i in range(2)]

    with nc.Block() as blk:
        @blk.gpsimd
        def _(g):
            mmb = C.get("MM")
            for fc in range(16):
                if fc >= 2:
                    g.wait_ge(sem["MM"], mmb + fc - 1)
                strip_load(g, wstr4[fc % 2], d_wd[fc], ntile=64, s="WA" if fc % 2 == 0 else "WB")
            C.inc("WA", 16 * 8)
            C.inc("WB", 16 * 8)

        @blk.sync
        def _(sync):
            evb = C.get("EV")
            for fc in range(16):
                if fc >= 1:
                    sync.wait_ge(sem["EV"], evb + 2 * fc)
                sync.dma_start(out=srel[:, :], in_=scr["s"][:, fc * TOK:(fc + 1) * TOK]).then_inc(sem["LD"], 16)
                sync.dma_start(out=xrel[:, :], in_=d_xT[fc * 128:(fc + 1) * 128, :]).then_inc(sem["LD"], 16)
            C.inc("LD", 32 * 16)

        @blk.tensor
        def _(t):
            wlba = C.get("WA") - 16 * 8
            wlbb = C.get("WB") - 16 * 8
            scb = C.get("SC")
            for fc in range(16):
                t.wait_ge(sem["WA" if fc % 2 == 0 else "WB"],
                          (wlba if fc % 2 == 0 else wlbb) + 16 * (fc // 2 + 1))
                pp = ps[(fc % 2) * 2: (fc % 2) * 2 + 2]
                if fc >= 2:
                    t.wait_ge(sem["SC"], scb + fc - 1)
                for cj in range(64):
                    for hf in range(2):
                        t.matmul(pp[hf][:, :], wstr4[fc % 2][:, cj * 128:(cj + 1) * 128],
                                 zb[:, cj * TOK + hf * 512: cj * TOK + (hf + 1) * 512],
                                 start=(cj == 0), stop=(cj == 63))
                t.drain()
                t.sem_inc(sem["MM"], 1)
            C.inc("MM", 16)

        @blk.scalar
        def _(s):
            mmb = C.get("MM") - 16
            evb = C.get("EV")
            for fc in range(16):
                s.wait_ge(sem["MM"], mmb + fc + 1)
                if fc >= 1:
                    s.wait_ge(sem["EV"], evb + 2 * fc - 1)
                pp = ps[(fc % 2) * 2: (fc % 2) * 2 + 2]
                for hf in range(2):
                    s.activation(gcomb[:, hf * 512:(hf + 1) * 512], pp[hf][:, :],
                                 AF.Silu, scale=sb["s_dp"][:, fc:fc + 1])
                s.drain()
                s.sem_inc(sem["SC"], 1)
                s.wait_ge(sem["EV"], evb + 2 * fc + 2)
                s.dma_start(out=d_out[fc * 128:(fc + 1) * 128, :], in_=outb[:, :]).then_inc(sem["SP"], 16)
            C.inc("SC", 16)
            s.wait_ge(sem["SP"], C.inc("SP", 16 * 16))

        @blk.vector
        def _(v):
            scb = C.get("SC") - 16
            ldb = C.get("LD") - 32 * 16
            spb = C.get("SP") - 16 * 16
            for fc in range(16):
                v.wait_ge(sem["SC"], scb + fc + 1)
                v.wait_ge(sem["LD"], ldb + 32 * (fc + 1))
                if fc >= 1:
                    v.wait_ge(sem["SP"], spb + 16 * fc)
                v.tensor_tensor(outb[:, :], gcomb[:, :], srel[:, :], ALU.mult)
                v.drain()
                v.sem_inc(sem["EV"], 1)
                v.tensor_tensor(outb[:, :], outb[:, :], xrel[:, :], ALU.add)
                v.drain()
                v.sem_inc(sem["EV"], 1)
            C.inc("EV", 32)

    DN.close()
    PC.close()
    est.close()
    return nc


_NC_CACHE = {}


def _stage(inputs):
    x = np.asarray(inputs["L0_attn_x"], dtype=np.float32)

    def colvec(v):
        return np.ascontiguousarray(np.asarray(v, np.float32).reshape(16, 128).T)

    H128 = _hadamard(128).astype(np.float32)
    maskT = np.triu(np.ones((128, 128), np.float32))
    mask4 = np.ascontiguousarray(np.concatenate([maskT] * 4, axis=1))

    com = {
        "c_maa_x": colvec(inputs["L0_attn_maa_x"]),
        "c_maa_r": colvec(inputs["L0_attn_maa_r"]),
        "c_maa_k": colvec(inputs["L0_attn_maa_k"]),
        "c_maa_v": colvec(inputs["L0_attn_maa_v"]),
        "c_maa_w": colvec(inputs["L0_attn_maa_w"]),
        "c_tdv": colvec(inputs["L0_attn_time_decay"]),
        "c_qb": colvec(inputs["L0_attn_q_b"]),
        "c_kb": colvec(inputs["L0_attn_k_b"]),
        "c_vb": colvec(inputs["L0_attn_v_b"]),
        "c_lng": colvec(inputs["L0_attn_ln_g"]),
        "c_lnb": colvec(inputs["L0_attn_ln_b"]),
        "c_su": colvec(inputs["L0_ffn_SU_ug"]),
        "c_sudp": np.ascontiguousarray(np.asarray(inputs["L0_ffn_SU_dp"], np.float32).reshape(64, 128).T),
        "c_wscu": np.ascontiguousarray(np.asarray(inputs["L0_ffn_Wscale_ug"], np.float32)[:8192].reshape(64, 128).T),
        "c_wscg": np.ascontiguousarray(np.asarray(inputs["L0_ffn_Wscale_ug"], np.float32)[8192:].reshape(64, 128).T),
        "c_wsdp": colvec(inputs["L0_ffn_Wscale_dp"]),
        "c_H128": H128,
        "c_ident": np.eye(128, dtype=np.float32),
        "c_mask4": mask4,
        "c_onescol": np.ones((128, 1), np.float32),
        "c_onesrow": np.ones((1, 128), np.float32),
        "w_maaw1": np.ascontiguousarray(np.asarray(inputs["L0_attn_maa_w1"], np.float32).reshape(16, 128, 128).transpose(1, 0, 2).reshape(128, 2048)),
        "w_tdw1": np.ascontiguousarray(np.asarray(inputs["L0_attn_td_w1"], np.float32).reshape(16, 128, 64).transpose(1, 0, 2).reshape(128, 1024)),
        "w_tdw2": np.ascontiguousarray(np.asarray(inputs["L0_attn_td_w2"], np.float32)),
    }
    w2 = np.asarray(inputs["L0_attn_maa_w2"], np.float32)
    W2blk = np.zeros((128, 4 * 2048), np.float32)
    for i in range(4):
        W2blk[32 * i:32 * i + 32, 2048 * i:2048 * (i + 1)] = w2[i]
    com["w_w2blk"] = np.ascontiguousarray(W2blk)

    def tile4(w):
        return np.ascontiguousarray(
            np.asarray(w, np.float32).reshape(16, 128, 16, 128).transpose(2, 1, 0, 3).reshape(16, 128, 2048))

    com["w_qw"] = tile4(inputs["L0_attn_q_w"])
    com["w_kw"] = tile4(inputs["L0_attn_k_w"])
    com["w_vw"] = tile4(inputs["L0_attn_v_w"])
    com["w_ow"] = tile4(inputs["L0_attn_o_w"])
    wupT = np.asarray(inputs["L0_ffn_W_up"], np.float32).T
    wgT = np.asarray(inputs["L0_ffn_W_gate"], np.float32).T
    com["w_wup"] = np.ascontiguousarray(wupT.reshape(16, 128, 64, 128).transpose(2, 1, 0, 3).reshape(64, 128, 2048))
    com["w_wg"] = np.ascontiguousarray(wgT.reshape(16, 128, 64, 128).transpose(2, 1, 0, 3).reshape(64, 128, 2048))
    wdT = np.asarray(inputs["L0_ffn_W_down"], np.float32).T
    com["w_wd"] = np.ascontiguousarray(wdT.reshape(64, 128, 16, 128).transpose(2, 1, 0, 3).reshape(16, 128, 8192))

    in_maps = []
    for core in range(8):
        b, qt = core // 4, core % 4
        t0 = qt * TOK
        m = dict(com)
        m["xT"] = np.ascontiguousarray(x[b, t0:t0 + TOK, :].T)
        xp = np.zeros((2048,), np.float32) if t0 == 0 else np.asarray(x[b, t0 - 1, :], np.float32)
        m["xp0"] = np.ascontiguousarray(xp.reshape(1, 2048))
        m["xp0c"] = colvec(xp)
        um = np.zeros((128, 16), np.float32)
        for j in range(7):
            pred = (j // 4 == b) and (j % 4 < qt)
            um[:, j] = 1.0 if pred else 0.0
            um[:, 8 + j] = 0.0 if pred else 1.0
        m["i_umask"] = um
        in_maps.append(m)
    return in_maps


def kernel(**inputs):
    if "nc" not in _NC_CACHE:
        _NC_CACHE["nc"] = build_nc()
    nc = _NC_CACHE["nc"]
    in_maps = _stage(inputs)
    res = run_bass_kernel_spmd(nc, in_maps, core_ids=list(range(8)))
    out = np.zeros((2, 4096, 2048), np.float32)
    for core in range(8):
        b, qt = core // 4, core % 4
        out[b, qt * TOK:(qt + 1) * TOK, :] = res.results[core]["o_out"].T
    if DEBUG:
        kernel._dbg = res.results
    return out


